# version 06
# baseline (speedup 1.0000x reference)
import numpy as np
import ml_dtypes
import concourse.bass as bass
import concourse.mybir as mybir
from concourse.ap import AP
from concourse.bass_utils import run_bass_kernel_spmd

F32 = mybir.dt.float32
BF16 = mybir.dt.bfloat16
AF = mybir.ActivationFunctionType
ALU = mybir.AluOpType
AX = mybir.AxisListType

D_MODEL = 2048
NCH = 16
D_FF = 8192
NFH = 64
N_HEAD_A = 16
DH = 128
D_INNER = 2048
D_XB = 512
NH_M = 32
HD = 64
DS = 64
DCONV = 4
D_PROJ = 2 * D_INNER + 2 * D_XB + NH_M
CONV_CH = 2 * D_XB + D_INNER
LN_EPS = 1e-5
RMS_EPS_A = 1.1920929e-07
EPS_M = 1e-5
MASK_NEG = -30000.0

ENGINES = ("sync", "tensor", "scalar", "vector", "gpsimd")

_CE = [h for h in range(NH_M) if (h // 4) % 2 == 0]
_CO = [h for h in range(NH_M) if (h // 4) % 2 == 1]


def c_head_slot(h):
    if (h // 4) % 2 == 0:
        return _CE.index(h), 0
    return _CO.index(h), 64


def c_perm_rows():
    perm = np.zeros(D_INNER, dtype=np.int64)
    for h in range(NH_M):
        chunk, base = c_head_slot(h)
        for n in range(DS):
            perm[chunk * 128 + base + n] = h * DS + n
    return perm


class Prog:

    def __init__(self, nc):
        self.nc = nc
        self.q = {e: [] for e in ENGINES}
        self.sems = {}
        self.cnt = {}

    def sem(self, name):
        self.sems[name] = self.nc.alloc_semaphore(name)
        self.cnt[name] = 0
        return name

    def c(self, name):
        return self.cnt[name]

    def snap(self):
        return dict(self.cnt)

    def emit(self, eng, fn, wait=(), inc=None, amt=1):
        tgt = None
        if inc is not None:
            self.cnt[inc] += amt
            tgt = (self.sems[inc], amt)
        wl = tuple((s, v) for (s, v) in wait if v > 0)
        self.q[eng].append((fn, wl, tgt))

    def dma(self, eng, out, in_, wait=(), inc=None, **kw):
        self.emit(eng, lambda e: e.dma_start(out=out, in_=in_, **kw),
                  wait=wait, inc=inc, amt=16)

    def mm(self, out, lhsT, rhs, start, stop, wait=(), inc=None):
        self.emit("tensor",
                  lambda e: e.matmul(out, lhsT, rhs, start=start, stop=stop),
                  wait=wait, inc=inc)

    def trans(self, out, in_, ident, wait=(), inc=None):
        self.emit("tensor", lambda e: e.transpose(out, in_, ident),
                  wait=wait, inc=inc)

    def act(self, out, in_, func, bias=0.0, scale=1.0, accum_out=None,
            wait=(), inc="act"):
        self.emit("scalar",
                  lambda e: e.activation(out, in_, func, bias=bias, scale=scale,
                                         accum_out=accum_out),
                  wait=wait, inc=inc)

    def v(self, meth, *args, wait=(), inc="dve", **kw):
        self.emit("vector", lambda e: getattr(e, meth)(*args, **kw),
                  wait=wait, inc=inc)

    def cchandler(self, kind, ins, outs, rg, wait=()):
        self.emit("gpsimd", lambda e: e.collective_compute(
            kind, ALU.bypass, replica_groups=rg, ins=ins, outs=outs),
            wait=wait, inc="cc")

    def replay(self):
        nc = self.nc
        with nc.Block() as blk:
            for ename in ENGINES:
                ops = self.q[ename]
                if not ops:
                    continue

                def make_body(ops):
                    def body(eng):
                        for fn, waits, inctgt in ops:
                            for s, vl in waits:
                                eng.wait_ge(self.sems[s], vl)
                            inst = fn(eng)
                            if inctgt is not None:
                                inst.then_inc(inctgt[0], inctgt[1])
                    return body

                getattr(blk, ename)(make_body(ops))


def wmax(*waitlists):
    d = {}
    for wl in waitlists:
        for s, v in wl:
            d[s] = max(d.get(s, 0), v)
    return list(d.items())


class MK:
    def __init__(self, NC=8, T=8192, dbg=False):
        self.NC = NC
        self.T = T
        self.Tc = T // NC
        self.TT = min(512, self.Tc)
        self.QT = min(512, T)
        self.Ha = N_HEAD_A // NC
        self.NJ = self.Tc // 128
        self.dbg = dbg
        self.nc = bass.Bass("TRN2", target_bir_lowering=False)
        self.p = Prog(self.nc)
        for s in ("pe", "act", "dve", "gp", "cc",
                  "qc", "q0x0", "q0x1", "q0w", "q0b0", "q0b1", "q0s0", "q0s1",
                  "gw", "q1x0", "q1x1", "q1s0", "q1s1", "q2m", "q2o",
                  "gA", "gw30", "gw31", "q3r0", "q3r1", "q3s0", "q3s1",
                  "qf0", "qf1", "flx", "fw0", "fw1", "fd0", "fd1",
                  "fra0", "fra1", "frb0", "frb1", "fs0", "fs1",
                  "ml0", "ml1", "mz0", "mz1", "mh", "mhi", "msum",
                  "msr0", "msr1", "mza0", "mza1", "mo0", "mo1"):
            self.p.sem(s)
        self._declare_dram()
        self._declare_sbuf()
        self._declare_psum()
        self.greg = self.nc.gpsimd.alloc_register("goff")

    def _dram(self, name, shape, dtype, kind="Internal"):
        return self.nc.dram_tensor(name, list(shape), dtype, kind=kind)

    def _declare_dram(self):
        Tc, T = self.Tc, self.T
        I, O = "ExternalInput", "ExternalOutput"
        dbgk = O if self.dbg else "Internal"
        self.x_sh = self._dram("x_sh", [Tc, D_MODEL], F32, I)
        self.wqkvT = self._dram("wqkvT", [D_MODEL, self.Ha * 3 * DH], F32, I)
        self.woT = self._dram("woT", [D_MODEL, D_MODEL], F32, I)
        self.sincosT = self._dram("sincosT", [256, T], F32, I)
        nmr = self.QT // 128
        self.maskadd = self._dram("maskadd", [128, nmr * self.QT], F32, I)
        self.ident_bf = self._dram("ident_bf", [128, 128], BF16, I)
        self.ident_f32 = self._dram("ident_f32", [128, 128], F32, I)
        self.pt_bf = self._dram("pt_bf", [128, 128], BF16, I)
        self.trineg = self._dram("trineg", [128, 128], F32, I)
        self.i32sel = self._dram("i32sel", [32, 32], F32, I)
        self.onesrow = self._dram("onesrow", [1, 128], F32, I)
        self.onescol_bf = self._dram("onescol_bf", [128, 1], BF16, I)
        self.ones128 = self._dram("ones128", [128, 128], F32, I)
        self.gateT = self._dram("gateT", [D_MODEL, D_FF], F32, I)
        self.upT = self._dram("upT", [D_MODEL, D_FF], F32, I)
        self.downT = self._dram("downT", [D_FF, D_MODEL], F32, I)
        self.inprojT = self._dram("inprojT", [D_MODEL, D_PROJ], F32, I)
        self.outprojT = self._dram("outprojT", [D_INNER, D_MODEL], F32, I)
        self.bgateT = self._dram("bgateT", [D_MODEL, D_FF], F32, I)
        self.bupT = self._dram("bupT", [D_MODEL, D_FF], F32, I)
        self.bdownT = self._dram("bdownT", [D_FF, D_MODEL], F32, I)
        self.convw = self._dram("convw", [CONV_CH, DCONV], F32, I)
        self.convb = self._dram("convb", [CONV_CH, 1], F32, I)
        self.dtbias = self._dram("dtbias", [NH_M, 1], F32, I)
        self.alog = self._dram("alog", [NH_M, 1], F32, I)
        self.dmrow = self._dram("dmrow", [1, NH_M], F32, I)
        self.ssmw = self._dram("ssmw", [NCH, 128], F32, I)
        self.inlnw = self._dram("inlnw", [NCH, 128], F32, I)
        self.postlnw = self._dram("postlnw", [NCH, 128], F32, I)
        self.pmrow = self._dram("pmrow", [1, self.NC * 32], F32, I)
        self.haloz = self._dram("haloz", [1, 1], F32, I)
        self.halooff = self._dram("halooff", [1, 24], mybir.dt.int32, I)
        self.y_sh = self._dram("y_sh", [Tc, D_MODEL], F32, O)
        self.r0d = self._dram("r0d", [D_MODEL, Tc], F32)
        self.r1d = self._dram("r1d", [D_MODEL, Tc], F32, dbgk)
        self.r2d = self._dram("r2d", [D_MODEL, Tc], F32, dbgk)
        self.z1d = self._dram("z1d", [D_MODEL, Tc], F32, dbgk)
        self.zsd = self._dram("zsd", [D_INNER, Tc], BF16)
        self.inner2d = self._dram("inner2d", [D_MODEL, Tc], F32, dbgk)
        self.r4d = self._dram("r4d", [D_MODEL, Tc], F32, dbgk)
        self.ccl_in = self._dram("ccl_in", [D_MODEL, Tc], BF16)
        self.ccl_out = self._dram("ccl_out", [self.NC * D_MODEL, Tc], BF16)
        self.cco_in = self._dram("cco_in", [self.Ha * DH, T], BF16)
        self.cco_out = self._dram("cco_out", [self.NC * self.Ha * DH, T], BF16)
        self.aoff = self._dram("aoff", [1, NCH], mybir.dt.int32, I)
        self.cch_in = self._dram("cch_in", [CONV_CH, 3], BF16)
        self.cch_out = self._dram("cch_out", [self.NC * CONV_CH, 3], BF16)
        self.ccs_in = self._dram("ccs_in", [DS + 1, NH_M * HD], F32)
        self.ccs_out = self._dram("ccs_out", [self.NC * (DS + 1), NH_M * HD], F32)

    def _declare_sbuf(self):
        nc, Tc, T, TT = self.nc, self.Tc, self.T, self.TT
        sb = lambda n, s, d: nc.alloc_sbuf_tensor(n, list(s), d)
        self.c_identbf = sb("c_identbf", [128, 128], BF16)
        self.c_identf = sb("c_identf", [128, 128], F32)
        self.c_ptbf = sb("c_ptbf", [128, 128], BF16)
        self.c_trineg = sb("c_trineg", [128, 128], F32)
        self.c_i32sel = sb("c_i32sel", [32, 32], F32)
        self.c_onesrow = sb("c_onesrow", [1, 128], F32)
        self.c_onescol = sb("c_onescol", [128, 1], BF16)
        self.c_ones128 = sb("c_ones128", [128, 128], F32)
        self.c_ones32 = sb("c_ones32", [32, 128], F32)
        self.c_convw = sb("c_convw", [128, 24 * DCONV], F32)
        self.c_convb = sb("c_convb", [128, 24], F32)
        self.c_dtb = sb("c_dtb", [NH_M, 1], F32)
        self.c_nege = sb("c_nege", [NH_M, 1], F32)
        self.c_dm64 = sb("c_dm64", [128, NH_M], F32)
        self.c_hz128 = sb("c_hz128", [128, 1], F32)
        self.c_ssmw = sb("c_ssmw", [128, NCH], F32)
        self.c_inlnw = sb("c_inlnw", [128, NCH], F32)
        self.c_postlnw = sb("c_postlnw", [128, NCH], F32)
        self.c_prow = sb("c_prow", [1, self.NC * 32], F32)
        self.c_pmrow = sb("c_pmrow", [1, self.NC * 32], F32)
        self.c_haloz = sb("c_haloz", [1, 1], F32)
        self.c_lneps = sb("c_lneps", [128, 1], F32)
        self.c_rmseps = sb("c_rmseps", [128, 1], F32)
        self.c_one = sb("c_one", [128, 1], F32)
        self.c_qsc = sb("c_qsc", [128, 1], F32)
        self.c_halooff = sb("c_halooff", [1, 24], mybir.dt.int32)
        self.c_aoff = sb("c_aoff", [1, NCH], mybir.dt.int32)

        self.arena_size = 200 * 1024
        self.arena = sb("arena", [128, self.arena_size], mybir.dt.uint8)
        self.arena0 = nc.lookup_mloc(self.arena).addr
        assert self.arena0 + self.arena_size <= 229344 - 64, self.arena0

        def A(name, shape, dtype, off):
            per = int(np.prod(shape[1:])) * mybir.dt.size(dtype)
            assert off + per <= self.arena_size, (name, off, per)
            return nc.alloc_sbuf_tensor_at(name, list(shape), dtype,
                                           offset=self.arena0 + off)

        HI = 174 * 1024
        self.p0_x = [A(f"p0_x{i}", [128, D_MODEL], F32, i * 8192) for i in (0, 1)]
        self.p0_ln = A("p0_ln", [128, D_MODEL], BF16, 16384)
        self.p0_sq = A("p0_sq", [128, D_MODEL], BF16, 20480)
        self.p0_fm = A("p0_fm", [128, NCH * Tc], BF16, 24576)
        self.p0_st = A("p0_st", [128, 16], F32, 24576 + NCH * Tc * 2)
        self.b_x = [A(f"b_x{i}", [128, D_MODEL], F32, HI + i * 8192)
                    for i in (0, 1)]
        self.b_st = [A(f"b_st{i}", [128, 512], F32, HI + 16384 + i * 2048)
                     for i in (0, 1)]
        hTb = self.Ha * T * 2
        self.q_reg = A("q_reg", [128, self.Ha * T], BF16, 0)
        self.k_reg = A("k_reg", [128, self.Ha * T], BF16, hTb)
        self.v_reg = A("v_reg", [128, self.Ha * T], BF16, 2 * hTb)
        o1 = 3 * hTb
        ow = self.Ha * 3 * DH
        self.wqkv_sb = A("wqkv_sb", [128, NCH * ow], BF16, o1)
        o2 = o1 + NCH * ow * 2
        self.xg = [A(f"xg{i}", [128, NCH * TT], BF16, o2 + i * NCH * TT * 2)
                   for i in (0, 1)]
        o3 = o2 + 2 * NCH * TT * 2
        self.sc_sin = [A(f"sc_sin{i}", [128, TT], F32, o3 + i * TT * 4)
                       for i in (0, 1)]
        self.sc_cos = [A(f"sc_cos{i}", [128, TT], F32, o3 + (2 + i) * TT * 4)
                       for i in (0, 1)]
        o4 = o3 + 4 * TT * 4
        self.p1_q2 = [A(f"p1_q2{i}", [128, TT], BF16, o4 + i * TT * 2)
                      for i in (0, 1)]
        self.p1_qraw = [A(f"p1_qraw{i}", [128, TT], BF16, o4 + (2 + i) * TT * 2)
                        for i in (0, 1)]
        o5 = o4 + 4 * TT * 2
        self.p1_ssrow = A("p1_ssrow", [1, TT], F32, o5)
        self.p1_t1 = A("p1_t1", [128, TT], F32, o5 + TT * 4)
        self.p1_t2 = A("p1_t2", [128, TT], F32, o5 + 2 * TT * 4)
        self.p1_rstd = A("p1_rstd", [128, TT], F32, o5 + 3 * TT * 4)
        self.p1_tmp = A("p1_tmp", [128, TT], F32, o5 + 4 * TT * 4)
        p1_end = o5 + 5 * TT * 4
        assert p1_end <= HI, p1_end
        self.o_reg = A("o_reg", [128, self.Ha * T], BF16, o1)
        oo = o1 + hTb
        self.p2_exp = [A(f"p2_exp{i}", [128, 512], BF16, oo + i * 1024)
                       for i in range(6)]
        self.p2_acc = [A(f"p2_acc{i}", [128, 512], F32, oo + 6144 + i * 2048)
                       for i in (0, 1)]
        self.p2_tmp = A("p2_tmp", [128, self.QT], F32, oo + 10240)
        self.p2_recip = [A(f"p2_recip{i}", [128, self.QT], F32,
                           oo + 12288 + i * 2048) for i in (0, 1)]
        self.p2_mask = A("p2_mask", [128, (self.QT // 128) * self.QT], F32,
                         oo + 16384)
        p2_end = oo + 16384 + (self.QT // 128) * self.QT * 4
        assert p2_end <= HI, p2_end
        self.p3_a = A("p3_a", [128, NCH * Tc], BF16, 0)
        self.p3_wo = [A(f"p3_wo{i}", [128, NCH * 128], BF16, 32768 + i * 4096)
                      for i in (0, 1)]
        self.p3_r = [A(f"p3_r{i}", [128, TT], F32, 40960 + i * TT * 4)
                     for i in (0, 1)]
        self.p3_st = [A(f"p3_st{i}", [128, TT], F32, 40960 + (2 + i) * TT * 4)
                      for i in (0, 1)]
        self.f_xb = A("f_xb", [128, NCH * Tc], BF16, 0)
        self.f_hid = A("f_hid", [128, NFH * Tc], BF16, 32768)
        _Tc3 = Tc + 3
        _r1 = max(24 * _Tc3 * 2, NCH * Tc * 2 + 2 * NH_M * HD * 4)
        _mend = (32768 + (_r1 + 63) // 64 * 64 + 24 * Tc * 2 + Tc * 4 +
                 2 * self.NJ * 256 + max(2 * self.NJ * 1024, Tc * 8) +
                 NH_M * HD * 4)
        fo = max(32768 + NFH * Tc * 2, _mend)
        self.f_x2 = [A(f"f_x2{i}", [128, TT], BF16, fo + i * TT * 2)
                     for i in (0, 1)]
        lo = fo + 2 * TT * 2
        self.f_rows = A("f_rows", [1, 8 * TT], F32, lo)
        lo2 = lo + 8 * TT * 4
        self.f_mub = [A(f"f_mub{i}", [128, TT], F32, lo2 + i * TT * 4)
                      for i in (0, 1)]
        self.f_rb = [A(f"f_rb{i}", [128, TT], F32, lo2 + (2 + i) * TT * 4)
                     for i in (0, 1)]
        self.f_scr = [A(f"f_scr{i}", [128, TT], F32, lo2 + (4 + i) * TT * 4)
                      for i in (0, 1)]
        ln_end = lo2 + 6 * TT * 4
        assert ln_end <= self.arena_size
        self.f_w = [A(f"f_w{i}", [128, NCH * 128], BF16, fo + i * 4096)
                    for i in (0, 1, 2, 3)]
        self.f_sg = [A(f"f_sg{i}", [128, Tc], BF16, fo + 16384 + i * Tc * 2)
                     for i in (0, 1)]
        gu_end = fo + 16384 + 2 * Tc * 2
        assert gu_end <= self.arena_size, gu_end
        self.f_wd = [A(f"f_wd{i}", [128, NFH * 128], BF16, i * 16384)
                     for i in (0, 1)]
        self.f_res = [A(f"f_res{i}", [128, TT], F32, fo + i * TT * 4)
                      for i in (0, 1)]
        self.f_res2 = [A(f"f_res2{i}", [128, TT], F32, fo + (2 + i) * TT * 4)
                       for i in (0, 1)]
        self.f_st = [A(f"f_st{i}", [128, TT], F32, fo + (4 + i) * TT * 4)
                     for i in (0, 1)]
        self.f_ost = [A(f"f_ost{i}", [128, 512], F32, fo + (6 + i) * TT * 4)
                      for i in (0, 1)]
        Tc3 = Tc + 3
        MB = 32768
        srb = NH_M * HD * 4
        r1 = max(24 * Tc3 * 2, NCH * Tc * 2 + 2 * srb)
        self.m_xbc = A("m_xbc", [128, 24 * Tc3], BF16, MB)
        self.m_y = A("m_y", [128, NCH * Tc], BF16, MB)
        self.m_sr = [A(f"m_sr{i}", [64, NH_M * HD], F32,
                       MB + NCH * Tc * 2 + i * srb) for i in (0, 1)]
        mo = MB + (r1 + 63) // 64 * 64
        self.m_xc = A("m_xc", [128, 24 * Tc], BF16, mo)
        mo2 = mo + 24 * Tc * 2
        self.m_ld = A("m_ld", [64, Tc], F32, mo2)
        mo3 = mo2 + Tc * 4
        self.m_ldt = A("m_ldt", [128, self.NJ * 64], F32, mo3)
        self.m_nldt = A("m_nldt", [128, self.NJ * 64], F32, mo3 + self.NJ * 256)
        mo4 = mo3 + 2 * self.NJ * 256
        self.m_dtraw = A("m_dtraw", [32, Tc], F32, mo4)
        self.m_zs = [A(f"m_zs{i}", [128, Tc], BF16, mo4 + Tc * 4 + i * Tc * 2)
                     for i in (0, 1)]
        self.m_xt = A("m_xt", [128, self.NJ * 512], BF16, mo4)
        self.m_bt = A("m_bt", [128, self.NJ * 512], BF16, mo4 + self.NJ * 1024)
        mo5 = mo4 + max(2 * self.NJ * 1024, Tc * 8)
        self.m_hst = A("m_hst", [64, NH_M * HD], F32, mo5)
        m_end = mo5 + NH_M * HD * 4
        fo_b = max(32768 + NFH * Tc * 2, m_end)
        assert m_end <= fo_b, (m_end, fo_b)
        self.m_hb = A("m_hb", [64, NH_M * HD], BF16, fo_b)
        t0 = fo_b + 8192
        self.m_esb = [A(f"m_esb{i}", [128, 128], F32, t0 + i * 512)
                      for i in (0, 1)]
        self.m_arg = [A(f"m_arg{i}", [128, 128], F32, t0 + 1024 + i * 512)
                      for i in (0, 1)]
        self.m_dtile = [A(f"m_dtile{i}", [128, 128], F32, t0 + 2048 + i * 512)
                        for i in (0, 1)]
        self.m_bw = [A(f"m_bw{i}", [128, 64], BF16, t0 + 3072 + i * 128)
                     for i in (0, 1)]
        self.m_cs = [A(f"m_cs{i}", [64, 128], BF16, t0 + 3328 + i * 256)
                     for i in (0, 1)]
        self.m_wt = [A(f"m_wt{i}", [128, 128], BF16, t0 + 3840 + i * 256)
                     for i in (0, 1)]
        t1 = t0 + 4352
        self.m_lacc = A("m_lacc", [32, 1], F32, t1)
        self.m_pcore = A("m_pcore", [32, 1], F32, t1 + 64)
        self.m_pb = A("m_pb", [64, self.NC * 32], F32, t1 + 128)
        self.m_wrow = A("m_wrow", [1, self.NC * 32], F32,
                        t1 + 128 + self.NC * 128)
        self.m_grow = A("m_grow", [1, self.NC * 32], F32,
                        t1 + 128 + 2 * self.NC * 128)
        assert t1 + 128 + 3 * self.NC * 128 <= fo_b + 16384
        self.m_cscr = [A(f"m_cscr{i}", [128, Tc], F32, fo_b + i * 2 * TT * 4)
                       for i in (0, 1)]
        self.m_x2b = [A(f"m_x2b{i}", [128, TT], BF16,
                        fo_b + (6 + i) * TT * 4) for i in (0, 1)]

    def _declare_psum(self):
        nc = self.nc
        self.ps = [nc.alloc_psum_tensor(f"ps{i}", [128, 1024], F32)
                   for i in range(3)]
        self.ps3b = nc.alloc_psum_tensor("ps3b", [128, 512], F32)
        self.pbf = nc.alloc_psum_tensor("pbf", [128, 1024], BF16)

    def const_load(self):
        p = self.p
        pairs = [(self.c_identbf, self.ident_bf), (self.c_identf, self.ident_f32),
                 (self.c_ptbf, self.pt_bf), (self.c_trineg, self.trineg),
                 (self.c_i32sel, self.i32sel), (self.c_onesrow, self.onesrow),
                 (self.c_onescol, self.onescol_bf), (self.c_dtb, self.dtbias),
                 (self.c_ones128, self.ones128),
                 (self.c_pmrow, self.pmrow),
                 (self.c_haloz, self.haloz), (self.c_halooff, self.halooff),
                 (self.c_aoff, self.aoff)]
        for sbuf, dr in pairs:
            h = dr.shape
            p.dma("sync", sbuf[:], dr[0:h[0], 0:h[1]], inc="qc")
        cw = AP(self.convw, 0, [[DCONV, 128], [128 * DCONV, 24], [1, DCONV]])
        p.dma("sync", self.c_convw.ap().rearrange("p (c k) -> p c k", k=DCONV),
              cw, inc="qc")
        cb = AP(self.convb, 0, [[1, 128], [128, 24]])
        p.dma("sync", self.c_convb[:], cb, inc="qc",
              allow_slow_non_contiguous=True)
        for sbuf, dr in ((self.c_ssmw, self.ssmw), (self.c_inlnw, self.inlnw),
                         (self.c_postlnw, self.postlnw)):
            p.dma("sync", sbuf[:], AP(dr, 0, [[1, 128], [128, NCH]]),
                  inc="qc", allow_slow_non_contiguous=True)
        p.dma("sync", self.c_nege[:], self.alog[0:NH_M, 0:1], inc="qc")
        p.dma("sync", self.c_prow[0:1, 0:NH_M], self.dmrow[0:1, 0:NH_M],
              inc="qc")
        w0 = p.c("qc")
        p.act(self.c_nege[:], self.c_nege[:], AF.Exp, wait=[("qc", w0)])
        p.v("tensor_scalar_mul", self.c_nege[:], self.c_nege[:], -1.0,
            wait=[("act", p.c("act"))])
        p.v("memset", self.c_ones32[:], 1.0)
        p.v("memset", self.c_lneps[:], LN_EPS)
        p.v("memset", self.c_rmseps[:], RMS_EPS_A)
        p.v("memset", self.c_one[:], 1.0)
        p.v("memset", self.c_qsc[:], -0.5 * float(np.log(128.0)))
        p.mm(self.ps3b[0:128, 0:NH_M], self.c_onesrow[0:1, 0:128],
             self.c_prow[0:1, 0:NH_M], True, True,
             wait=[("qc", w0)], inc="pe")
        p.v("tensor_copy", self.c_dm64[:], self.ps3b[0:128, 0:NH_M],
            wait=[("pe", p.c("pe"))])
        p.mm(self.ps3b[0:128, 64:65], self.c_onesrow[0:1, 0:128],
             self.c_haloz[0:1, 0:1], True, True,
             wait=[("qc", w0)], inc="pe")
        p.v("tensor_copy", self.c_hz128[:], self.ps3b[0:128, 64:65],
            wait=[("pe", p.c("pe"))])
        self.const_w = [("qc", p.c("qc")), ("dve", p.c("dve"))]

    def phase0(self):
        p, Tc = self.p, self.Tc
        nt = Tc // 128
        hist = []
        for i in range(nt):
            xb = self.p0_x[i % 2]
            qx = f"q0x{i % 2}"
            wt = list(self.const_w)
            if i >= 2:
                h2 = hist[i - 2]
                wt += [("pe", h2["pe"]), ("act", h2["act"]), ("dve", h2["dve"])]
            p.dma("sync", xb[:], self.x_sh[i * 128:(i + 1) * 128, :],
                  wait=wt, inc=qx)
            ld = p.c(qx)
            st = self.p0_st
            p.v("reduce_sum", st[:, 0:1], xb[:], AX.X,
                wait=[(qx, ld), ("dve", p.c("dve"))])
            p.act(self.p0_sq[:], xb[:], AF.Square, accum_out=st[:, 1:2],
                  wait=[(qx, ld), ("dve", p.c("dve")), ("act", p.c("act"))])
            a_sq = p.c("act")
            p.v("tensor_scalar_mul", st[:, 2:3], st[:, 0:1], 1.0 / D_MODEL,
                wait=[("dve", p.c("dve"))])
            p.v("tensor_mul", st[:, 3:4], st[:, 2:3], st[:, 2:3],
                wait=[("dve", p.c("dve"))])
            p.v("scalar_tensor_tensor", st[:, 4:5], st[:, 1:2], 1.0 / D_MODEL,
                st[:, 3:4], ALU.mult, ALU.subtract,
                wait=[("act", a_sq), ("dve", p.c("dve"))])
            p.act(st[:, 5:6], st[:, 4:5], AF.Sqrt, bias=self.c_lneps[:, 0:1],
                  wait=[("dve", p.c("dve"))])
            p.v("reciprocal", st[:, 6:7], st[:, 5:6], wait=[("act", p.c("act"))])
            p.v("scalar_tensor_tensor", st[:, 7:8], st[:, 2:3], -1.0,
                st[:, 6:7], ALU.mult, ALU.mult, wait=[("dve", p.c("dve"))])
            st_done = p.c("dve")
            p.v("tensor_scalar", self.p0_ln[:], xb[:], st[:, 6:7],
                st[:, 7:8], ALU.mult, ALU.add, wait=[("dve", st_done)])
            lnd = p.c("dve")
            for g in range(2):
                for c8 in range(8):
                    c = g * 8 + c8
                    wtr = [("dve", lnd)]
                    if i > 0 or g > 0:
                        wtr.append(("dve", self._p0ev))
                    p.trans(self.pbf[:, c8 * 128:(c8 + 1) * 128],
                            self.p0_ln[:, c * 128:(c + 1) * 128],
                            self.c_identbf[:], wait=wtr if c8 == 0 else [],
                            inc="pe")
                tp = p.c("pe")
                for c8 in range(8):
                    c = g * 8 + c8
                    p.v("tensor_copy",
                        self.p0_fm[:, c * Tc + i * 128:c * Tc + (i + 1) * 128],
                        self.pbf[:, c8 * 128:(c8 + 1) * 128],
                        wait=[("pe", tp)] if c8 == 0 else [])
                self._p0ev = p.c("dve")
            hist.append(p.snap())
        evd = p.c("dve")
        for c in range(NCH):
            p.dma("sync", self.ccl_in[c * 128:(c + 1) * 128, :],
                  self.p0_fm[:, c * Tc:(c + 1) * Tc],
                  wait=[("dve", evd)] if c == 0 else [], inc="q0w")
        wrote = p.c("q0w")
        p.cchandler("AllGather", [self.ccl_in.ap().opt()],
                    [self.ccl_out.ap().opt()], [list(range(self.NC))],
                    wait=[("q0w", wrote)])
        self.cc_ln = p.c("cc")
        bhist = []
        for i in range(nt):
            xb = self.b_x[i % 2]
            qx = f"q0b{i % 2}"
            wt = [("pe", hist[-1]["pe"]), ("act", hist[-1]["act"])]
            if i >= 2:
                wt.append(("pe", bhist[i - 2]["pe"]))
            p.dma("sync", xb[:], self.x_sh[i * 128:(i + 1) * 128, :],
                  wait=wt, inc=qx)
            ld = p.c(qx)
            for g in range(NCH // 4):
                for c4 in range(4):
                    c = g * 4 + c4
                    wtr = [(qx, ld)]
                    if i > 0 or g > 0:
                        wtr.append(("dve", self._bev))
                    p.trans(self.ps[2][:, 512 + c4 * 128:512 + (c4 + 1) * 128],
                            xb[:, c * 128:(c + 1) * 128], self.c_identf[:],
                            wait=wtr if c4 == 0 else [], inc="pe")
                tp = p.c("pe")
                sbuf = self.b_st[g % 2]
                qs = f"q0s{g % 2}"
                wev = [("pe", tp), (qs, self._bst2.get(g % 2, 0))]
                for c4 in range(4):
                    p.v("tensor_copy", sbuf[:, c4 * 128:(c4 + 1) * 128],
                        self.ps[2][:, 512 + c4 * 128:512 + (c4 + 1) * 128],
                        wait=wev if c4 == 0 else [])
                ev = p.c("dve")
                self._bev = ev
                for c4 in range(4):
                    c = g * 4 + c4
                    p.dma("sync", self.r0d[c * 128:(c + 1) * 128,
                                           i * 128:(i + 1) * 128],
                          sbuf[:, c4 * 128:(c4 + 1) * 128],
                          wait=[("dve", ev)] if c4 == 0 else [], inc=qs)
                self._bst2[g % 2] = p.c(qs)
            bhist.append(p.snap())
        self.r0_done = [("q0s0", p.c("q0s0")), ("q0s1", p.c("q0s1"))]

    def phase1(self):
        p, T, Tc, TT, Ha = self.p, self.T, self.Tc, self.TT, self.Ha
        ow = Ha * 3 * DH
        wdst = self.wqkv_sb.ap().rearrange("p (c o) -> p c o", o=ow)
        for grp in range(4):
            p.dma("gpsimd", wdst[:, grp * 4:(grp + 1) * 4, :],
                  AP(self.wqkvT, grp * 4 * 128 * ow,
                     [[ow, 128], [128 * ow, 4], [1, ow]]), inc="gw")
        wq_ld = p.c("gw")
        units = [(s, tt) for s in range(self.NC) for tt in range(Tc // TT)]
        qk_ps = [self.ps[0][:, 0:TT], self.ps[0][:, 512:512 + TT]]
        rot_ps = self.ps[2][:, 0:TT]
        ss_ps = self.ps3b[0:1, 0:TT]
        bc_ps = self.ps3b[:, 0:TT]
        uhist = []
        shist = []
        for u, (s, tt) in enumerate(units):
            xgb = self.xg[u % 2]
            qx = f"q1x{u % 2}"
            qs = f"q1s{u % 2}"
            g0 = s * Tc + tt * TT
            wt = [("cc", self.cc_ln)]
            if u >= 2:
                wt.append(("pe", uhist[u - 2]["pe"]))
            for c in range(NCH):
                p.dma("sync", xgb[:, c * TT:(c + 1) * TT],
                      self.ccl_out[s * D_MODEL + c * 128:
                                   s * D_MODEL + (c + 1) * 128,
                                   tt * TT:(tt + 1) * TT],
                      wait=wt if c == 0 else [], inc=qx)
            wsc = [("dve", uhist[u - 2]["dve"])] if u >= 2 else []
            p.dma("sync", self.sc_sin[u % 2][:], self.sincosT[0:128, g0:g0 + TT],
                  wait=wsc, inc=qs)
            p.dma("sync", self.sc_cos[u % 2][:], self.sincosT[128:256, g0:g0 + TT],
                  inc=qs)
            ld, lds = p.c(qx), p.c(qs)
            scs, scc = self.sc_sin[u % 2], self.sc_cos[u % 2]
            for slot in range(2 * Ha):
                qk, hh = divmod(slot, Ha)
                psb = qk_ps[slot % 2]
                wmm = [(qx, ld), ("gw", wq_ld)]
                gsl = u * 2 * Ha + slot
                if gsl >= 2:
                    h2 = shist[gsl - 2]
                    wmm += [("dve", h2["dve"]), ("act", h2["act"])]
                for c in range(NCH):
                    p.mm(psb, self.wqkv_sb[:, c * ow + slot * DH:
                                           c * ow + (slot + 1) * DH],
                         xgb[:, c * TT:(c + 1) * TT], c == 0, c == NCH - 1,
                         wait=wmm if c == 0 else [], inc="pe")
                mmd = p.c("pe")
                sq, qraw = self.p1_q2[slot % 2], self.p1_qraw[slot % 2]
                p.act(sq[:], psb, AF.Square, wait=[("pe", mmd)])
                p.v("tensor_copy", qraw[:], psb, wait=[("act", p.c("act"))])
                cpd = p.c("dve")
                p.mm(ss_ps, self.c_onescol[:, 0:1], sq[:], True, True,
                     wait=[("act", p.c("act")), ("dve", cpd),
                           ("act", self._p1_prev_rstd)], inc="pe")
                p.mm(rot_ps, self.c_ptbf[:], qraw[:], True, True,
                     wait=[("dve", cpd)], inc="pe")
                mm2 = p.c("pe")
                p.v("tensor_copy", self.p1_ssrow[:], ss_ps, wait=[("pe", mm2)])
                p.mm(bc_ps, self.c_onesrow[:], self.p1_ssrow[:], True, True,
                     wait=[("dve", p.c("dve"))], inc="pe")
                bcd = p.c("pe")
                p.act(self.p1_tmp[:], bc_ps, AF.Ln, bias=self.c_rmseps[:, 0:1],
                      scale=1.0 / DH, wait=[("pe", bcd), ("dve", p.c("dve"))])
                p.act(self.p1_rstd[:], self.p1_tmp[:], AF.Exp, scale=-0.5,
                      bias=(self.c_qsc[:, 0:1] if qk == 0 else 0.0),
                      wait=[("act", p.c("act"))])
                rstd_done = p.c("act")
                self._p1_prev_rstd = rstd_done
                p.v("tensor_mul", self.p1_t1[:], qraw[:], scc[:],
                    wait=[("dve", cpd), (qs, lds)])
                p.v("tensor_mul", self.p1_t2[:], rot_ps, scs[:],
                    wait=[("pe", mm2), ("dve", p.c("dve"))])
                p.v("tensor_add", self.p1_t1[:], self.p1_t1[:], self.p1_t2[:],
                    wait=[("dve", p.c("dve"))])
                dst = self.q_reg if qk == 0 else self.k_reg
                p.v("tensor_mul", dst[:, hh * T + g0:hh * T + g0 + TT],
                    self.p1_t1[:], self.p1_rstd[:],
                    wait=[("dve", p.c("dve")), ("act", rstd_done)])
                shist.append(p.snap())
            for vv in range(TT // 128):
                wmm = [(qx, ld), ("gw", wq_ld)]
                if self._v_hist:
                    wmm.append(("dve", self._v_hist[-1]))
                nsl = (Ha * DH + 511) // 512
                for c in range(NCH):
                    for ns in range(nsl):
                        n0, n1 = ns * 512, min((ns + 1) * 512, Ha * DH)
                        p.mm(self.ps[1][:, n0:n1],
                             xgb[:, c * TT + vv * 128:c * TT + vv * 128 + 128],
                             self.wqkv_sb[:, c * ow + 2 * Ha * DH + n0:
                                          c * ow + 2 * Ha * DH + n1],
                             c == 0, c == NCH - 1,
                             wait=wmm if (c == 0 and ns == 0) else [], inc="pe")
                mmd = p.c("pe")
                gtc = (g0 + vv * 128) // 128
                for hh in range(Ha):
                    p.v("tensor_copy",
                        self.v_reg[:, hh * T + gtc * 128:hh * T + (gtc + 1) * 128],
                        self.ps[1][:, hh * DH:(hh + 1) * DH],
                        wait=[("pe", mmd)] if hh == 0 else [])
                self._v_hist.append(p.c("dve"))
            uhist.append(p.snap())
        self.p1_done = p.snap()

    def phase2(self):
        p, T, QT, Ha, Tc = self.p, self.T, self.QT, self.Ha, self.Tc
        nmr = QT // 128
        D = 4
        NS = 4
        NE = 6
        p.dma("sync", self.p2_mask[:], self.maskadd[0:128, 0:nmr * QT],
              wait=[("pe", self.p1_done["pe"]), ("dve", self.p1_done["dve"])],
              inc="q2m")
        mload = p.c("q2m")
        s_ps = [self.ps[0][:, 0:512], self.ps[0][:, 512:1024],
                self.ps[1][:, 0:512], self.ps[1][:, 512:1024]]
        av_ps = [self.ps[2][:, 0:QT], self.ps[2][:, 512:512 + QT]]
        bc = self.ps3b[:, 0:QT]
        rdy = [("dve", self.p1_done["dve"]), ("q2m", mload)] + self.r0_done
        nq = T // QT
        blocks = []
        for hh in range(Ha):
            for qt in range(nq):
                qi = hh * nq + qt
                nkc = (qt + 1) * QT // 128
                for kc in range(nkc):
                    blocks.append((qi, hh, qt, kc, kc == 0, kc == nkc - 1))
        NB = len(blocks)
        exp_act = {}
        av_pe = {}
        acc_dve = {}
        bcast_pe = {}
        ln_act = {}
        expr_act = {}
        omul_dve = {}

        def emit_back(gb):
            qi, hh, qt, kc, first, last = blocks[gb]
            eb = self.p2_exp[gb % NE]
            wav = [("act", exp_act[gb])]
            if first:
                wav.append(("dve", omul_dve.get(qi - 2, 0)))
            p.mm(av_ps[qi % 2],
                 self.v_reg[:, hh * T + kc * 128:hh * T + (kc + 1) * 128],
                 eb[:], first, last, wait=wav, inc="pe")
            av_pe[gb] = p.c("pe")
            acc = self.p2_acc[qi % 2]
            if first:
                p.v("tensor_copy", acc[:], eb[:],
                    wait=[("act", exp_act[gb]),
                          ("pe", bcast_pe.get(qi - 2, 0)),
                          ("dve", p.c("dve"))])
            else:
                p.v("tensor_add", acc[:], acc[:], eb[:],
                    wait=[("act", exp_act[gb]), ("dve", p.c("dve"))])
            acc_dve[gb] = p.c("dve")
            if last:
                p.mm(bc, self.c_ones128[:], acc[:], True, True,
                     wait=[("dve", acc_dve[gb]),
                           ("act", ln_act.get(qi - 1, 0))], inc="pe")
                bcast_pe[qi] = p.c("pe")
                p.act(self.p2_tmp[:], bc, AF.Ln, wait=[("pe", bcast_pe[qi])])
                ln_act[qi] = p.c("act")
                p.act(self.p2_recip[qi % 2][:], self.p2_tmp[:], AF.Exp,
                      scale=-1.0, wait=[("dve", omul_dve.get(qi - 2, 0))])
                expr_act[qi] = p.c("act")
                p.v("tensor_mul",
                    self.o_reg[:, hh * T + qt * QT:hh * T + (qt + 1) * QT],
                    av_ps[qi % 2], self.p2_recip[qi % 2][:],
                    wait=[("act", expr_act[qi]), ("pe", av_pe[gb]),
                          ("dve", p.c("dve"))])
                omul_dve[qi] = p.c("dve")

        for gb in range(NB):
            qi, hh, qt, kc, first, last = blocks[gb]
            sp = s_ps[gb % NS]
            wmm = list(rdy) if gb < NS else [("act", exp_act[gb - NS])]
            p.mm(sp,
                 self.k_reg[:, hh * T + kc * 128:hh * T + (kc + 1) * 128],
                 self.q_reg[:, hh * T + qt * QT:hh * T + (qt + 1) * QT],
                 True, True, wait=wmm, inc="pe")
            qk_pe = p.c("pe")
            r = kc - qt * nmr
            wexp = [("pe", qk_pe)]
            if r >= 0:
                p.v("tensor_add", sp, sp, self.p2_mask[:, r * QT:(r + 1) * QT],
                    wait=[("pe", qk_pe), ("dve", p.c("dve"))])
                wexp = [("pe", qk_pe), ("dve", p.c("dve"))]
            if gb >= NE:
                wexp += [("pe", av_pe[gb - NE]), ("dve", acc_dve[gb - NE])]
            p.act(self.p2_exp[gb % NE][:], sp, AF.Exp, wait=wexp)
            exp_act[gb] = p.c("act")
            if gb >= D:
                emit_back(gb - D)
        for gb in range(max(NB - D, 0), NB):
            emit_back(gb)
        odone = p.c("dve")
        for hh in range(Ha):
            p.dma("sync", self.cco_in[hh * DH:(hh + 1) * DH, :],
                  self.o_reg[:, hh * T:(hh + 1) * T],
                  wait=[("dve", odone)] if hh == 0 else [], inc="q2o")
        wrote = p.c("q2o")
        p.cchandler("AllGather", [self.cco_in.ap().opt()],
                    [self.cco_out.ap().opt()], [list(range(self.NC))],
                    wait=[("q2o", wrote)])
        self.cc_o = p.c("cc")

    def phase3(self):
        p, Tc, TT = self.p, self.Tc, self.TT
        ntt = Tc // TT
        for c in range(NCH):
            p.emit("gpsimd",
                   lambda e, c=c: e.reg_load(self.greg,
                                             self.c_aoff[0:1, c:c + 1]),
                   wait=[("cc", self.cc_o), ("pe", self.p1_done["pe"]),
                         ("qc", self.const_w[0][1])] if c == 0 else [])
            p.dma("gpsimd", self.p3_a[:, c * Tc:(c + 1) * Tc],
                  AP(self.cco_out, self.greg, [[self.T, 128], [1, Tc]]),
                  inc="gA")
        a_ld = p.c("gA")
        cnt = 0
        hist = {}
        whist = {}
        for oc in range(NCH):
            wb = self.p3_wo[oc % 2]
            gw3 = f"gw3{oc % 2}"
            wwt = [("cc", self.cc_o)]
            if oc >= 2:
                wwt.append(("pe", whist[oc - 2]))
            for grp in range(4):
                p.dma("gpsimd", wb[:, grp * 512:(grp + 1) * 512],
                      AP(self.woT, oc * 128 + grp * 4 * 128 * D_MODEL,
                         [[D_MODEL, 128], [128 * D_MODEL, 4], [1, 128]]),
                      wait=wwt if grp == 0 else [], inc=gw3)
            wld = p.c(gw3)
            for tt in range(ntt):
                psb = self.ps[0][:, 0:TT] if cnt % 2 == 0 else self.ps[1][:, 0:TT]
                rb = self.p3_r[tt % 2]
                q3r = f"q3r{tt % 2}"
                q3s = f"q3s{tt % 2}"
                wr = [("dve", hist.get(cnt - ntt, {"dve": 0})["dve"])] +                     self.r0_done
                p.dma("sync", rb[:], self.r0d[oc * 128:(oc + 1) * 128,
                                              tt * TT:(tt + 1) * TT],
                      wait=wr, inc=q3r)
                rld = p.c(q3r)
                wmm = [("gA", a_ld), (gw3, wld),
                       ("dve", hist.get(cnt - 2, {"dve": 0})["dve"])]
                for c in range(NCH):
                    p.mm(psb, wb[:, c * 128:(c + 1) * 128],
                         self.p3_a[:, c * Tc + tt * TT:c * Tc + (tt + 1) * TT],
                         c == 0, c == NCH - 1, wait=wmm if c == 0 else [],
                         inc="pe")
                mmd = p.c("pe")
                st = self.p3_st[tt % 2]
                wev = [("pe", mmd), (q3r, rld),
                       (q3s, hist.get(cnt - ntt, {q3s: 0}).get(q3s, 0))]
                p.v("tensor_add", st[:], psb, rb[:], wait=wev)
                p.dma("sync", self.r1d[oc * 128:(oc + 1) * 128,
                                       tt * TT:(tt + 1) * TT], st[:],
                      wait=[("dve", p.c("dve"))], inc=q3s)
                hist[cnt] = p.snap()
                cnt += 1
            whist[oc] = p.c("pe")
        self.r1_done = [("q3s0", p.c("q3s0")), ("q3s1", p.c("q3s1"))]


    def mamba(self, ready, upto=None):
        p, Tc, TT, NJ, NC = self.p, self.Tc, self.TT, self.NJ, self.NC
        nth = Tc // TT
        Tc3 = Tc + 3
        s1p = [self.ps[0][0:1, 0:TT], self.ps[0][0:1, 512:512 + TT]]
        s2p = [self.ps[1][0:1, 0:TT], self.ps[1][0:1, 512:512 + TT]]
        s3p = [self.ps[0][0:1, 0:TT], self.ps[0][0:1, 512:512 + TT]]
        R = self.f_rows
        lnact = {}
        x2pe = {}
        for c in range(NCH):
            xt = self.m_cscr[c % 2]
            xb = self.m_zs[c % 2]
            x2a = self.m_y[:, (c % 2) * Tc:(c % 2 + 1) * Tc]
            ql = f"ml{c % 2}"
            wl = list(ready)
            if c >= 2:
                wl += [("act", lnact.get(c % 2, 0)), ("pe", x2pe.get(c % 2, 0)),
                       ("dve", self._m_cpd.get(c % 2, 0))]
            p.dma("sync", xt[:], self.r2d[c * 128:(c + 1) * 128, :],
                  wait=wl, inc=ql)
            ld = p.c(ql)
            p.v("tensor_copy", xb[:], xt[:],
                wait=[(ql, ld), ("dve", p.c("dve"))] + (ready if c == 0 else []))
            self._m_cpd[c % 2] = p.c("dve")
            p.act(x2a, xb[:], AF.Square,
                  wait=[("dve", p.c("dve")), ("act", lnact.get(c % 2, 0))] +
                  (ready if c == 0 else []))
            aq = p.c("act")
            lnact[c % 2] = aq
            for th in range(nth):
                p.mm(s1p[th], self.c_onescol[:, 0:1],
                     xb[:, th * TT:(th + 1) * TT], c == 0, c == NCH - 1,
                     wait=[("dve", self._m_cpd[c % 2]), ("act", aq)]
                     if th == 0 else [], inc="pe")
                p.mm(s2p[th], self.c_onescol[:, 0:1],
                     x2a[:, th * TT:(th + 1) * TT],
                     c == 0, c == NCH - 1, inc="pe")
            x2pe[c % 2] = p.c("pe")
        smm = p.c("pe")
        if upto == "m0a":
            return []
        for th in range(nth):
            mb = self.ps[2][:, 0:TT]
            vb = self.ps[2][:, 512:512 + TT]
            p.v("tensor_copy", R[0:1, 0:TT], s1p[th],
                wait=[("pe", max(smm, self._f_bcd)), ("dve", p.c("dve"))])
            p.v("tensor_copy", R[0:1, TT:2 * TT], s2p[th],
                wait=[("dve", p.c("dve"))])
            p.v("tensor_scalar_mul", R[0:1, 2 * TT:3 * TT], R[0:1, 0:TT],
                1.0 / D_MODEL, wait=[("dve", p.c("dve"))])
            p.v("tensor_mul", R[0:1, 3 * TT:4 * TT], R[0:1, 2 * TT:3 * TT],
                R[0:1, 2 * TT:3 * TT], wait=[("dve", p.c("dve"))])
            p.v("scalar_tensor_tensor", R[0:1, 4 * TT:5 * TT],
                R[0:1, TT:2 * TT], 1.0 / D_MODEL, R[0:1, 3 * TT:4 * TT],
                ALU.mult, ALU.subtract, wait=[("dve", p.c("dve"))])
            rowd = p.c("dve")
            p.mm(mb, self.c_onesrow[:], R[0:1, 2 * TT:3 * TT], True, True,
                 wait=[("dve", rowd), ("act", self._f_bcact)], inc="pe")
            p.mm(vb, self.c_onesrow[:], R[0:1, 4 * TT:5 * TT], True, True,
                 inc="pe")
            bcd = p.c("pe")
            self._f_bcd = bcd
            p.act(self.f_scr[0][:], vb, AF.Ln, bias=self.c_lneps[:, 0:1],
                  wait=[("pe", bcd), ("dve", self._f_aplydve)])
            p.act(self.f_rb[th][:], self.f_scr[0][:], AF.Exp, scale=-0.5,
                  wait=[("act", p.c("act"))])
            p.act(self.f_mub[th][:], mb, AF.Copy, wait=[("act", p.c("act"))])
            self._f_bcact = p.c("act")
        nrm = p.c("act")
        rows_dve = p.c("dve")
        if upto == "m0b":
            return []
        zh = {}
        for c in range(NCH):
            xt = self.m_cscr[c % 2]
            ql = f"ml{c % 2}"
            wl = [("act", nrm), ("dve", p.c("dve")), ("pe", smm)]
            if c >= 2:
                wl.append(("dve", zh[c - 2]["dve"]))
                wl.append(("pe", zh[c - 2]["pe"]))
            p.dma("sync", xt[:], self.r2d[c * 128:(c + 1) * 128, :],
                  wait=wl, inc=ql)
            ld = p.c(ql)
            for th in range(nth):
                st = self.f_st[th % 2]
                qz = f"mz{th % 2}"
                p.v("tensor_sub", self.f_scr[th % 2][:],
                    xt[:, th * TT:(th + 1) * TT], self.f_mub[th][:],
                    wait=[(ql, ld), ("act", nrm), ("dve", p.c("dve"))])
                p.v("tensor_mul", st[:], self.f_scr[th % 2][:],
                    self.f_rb[th][:],
                    wait=[("dve", p.c("dve")), (qz, p.c(qz))])
                zst = p.c("dve")
                p.dma("sync", self.z1d[c * 128:(c + 1) * 128,
                                       th * TT:(th + 1) * TT], st[:],
                      wait=[("dve", zst)], inc=qz)
                p.v("tensor_copy",
                    self.f_xb[:, c * Tc + th * TT:c * Tc + (th + 1) * TT],
                    st[:], wait=[("dve", zst)])
            cpd = p.c("dve")
            x2a = self.m_y[:, (c % 2) * Tc:(c % 2 + 1) * Tc]
            p.act(x2a, self.f_xb[:, c * Tc:(c + 1) * Tc], AF.Square,
                  wait=[("dve", cpd), ("act", lnact.get(c % 2, 0))])
            aq = p.c("act")
            lnact[c % 2] = aq
            for th in range(nth):
                p.mm(s3p[th], self.c_onescol[:, 0:1],
                     x2a[:, th * TT:(th + 1) * TT], c == 0, c == NCH - 1,
                     wait=[("act", aq), ("dve", rows_dve)] if th == 0 else [],
                     inc="pe")
            zh[c] = p.snap()
        smm2 = p.c("pe")
        if upto == "m0c":
            return []
        for th in range(nth):
            vb = self.ps[2][:, 512:512 + TT]
            p.v("tensor_copy", R[0:1, TT:2 * TT], s3p[th],
                wait=[("pe", max(smm2, self._f_bcd)), ("dve", p.c("dve"))])
            p.mm(vb, self.c_onesrow[:], R[0:1, TT:2 * TT], True, True,
                 wait=[("dve", p.c("dve")), ("act", self._f_bcact)], inc="pe")
            self._f_bcd = p.c("pe")
            p.act(self.f_scr[0][:], vb, AF.Ln, bias=self.c_lneps[:, 0:1],
                  scale=1.0 / D_MODEL,
                  wait=[("pe", p.c("pe")), ("dve", p.c("dve"))])
            p.act(self.f_rb[th][:], self.f_scr[0][:], AF.Exp, scale=-0.5,
                  wait=[("act", p.c("act"))])
            self._f_bcact = p.c("act")
        nrm2 = p.c("act")
        if upto == "m0d":
            return []
        for c in range(NCH):
            for th in range(nth):
                p.v("tensor_mul", self.f_scr[th % 2][:],
                    self.f_xb[:, c * Tc + th * TT:c * Tc + (th + 1) * TT],
                    self.f_rb[th][:],
                    wait=[("act", nrm2), ("dve", p.c("dve"))])
                p.v("tensor_scalar_mul",
                    self.f_xb[:, c * Tc + th * TT:c * Tc + (th + 1) * TT],
                    self.f_scr[th % 2][:], self.c_inlnw[:, c:c + 1],
                    wait=[("dve", p.c("dve"))])
        rms3_done = p.c("dve")
        if upto == "m0":
            return []
        if self.dbg:
            d6 = self._dram("dbg_rms3", [NCH * 128, Tc], F32, "ExternalOutput")
            p.dma("gpsimd", d6.ap().rearrange("(c p) t -> p c t", p=128),
                  self.f_xb.ap().rearrange("p (c t) -> p c t", t=Tc),
                  wait=[("dve", rms3_done)], inc="mhi")
            self._rms3dump = p.c("mhi")
        ghist = {}
        for oc in range(41):
            ncol = 128 if oc < 40 else 32
            fwq = f"fw{oc % 2}"
            wb = self.f_w[oc % 2]
            wwt = [("dve", rms3_done), ("mz0", p.c("mz0")),
                   ("mz1", p.c("mz1"))]
            if self.dbg and oc == 0:
                wwt.append(("mhi", self._rms3dump))
            if oc >= 2:
                wwt.append(("pe", ghist[oc - 2]))
            for grp in range(4):
                p.dma("gpsimd", wb[:, grp * 4 * ncol:(grp + 1) * 4 * ncol],
                      AP(self.inprojT, oc * 128 + grp * 4 * 128 * D_PROJ,
                         [[D_PROJ, 128], [128 * D_PROJ, 4], [1, ncol]]),
                      wait=wwt if grp == 0 else [], inc=fwq)
            wld = p.c(fwq)
            psA = self.ps[oc % 2]
            wA = [("dve", rms3_done), (fwq, wld),
                  ("act", self._m_psact.get(oc % 2, 0)),
                  ("dve", self._m_psdve.get(oc % 2, 0))]
            M = ncol
            for th in range(nth):
                for c in range(NCH):
                    p.mm(psA[0:M, th * TT:th * TT + TT],
                         wb[:, c * ncol:(c + 1) * ncol],
                         self.f_xb[:, c * Tc + th * TT:c * Tc + (th + 1) * TT],
                         c == 0, c == NCH - 1,
                         wait=wA if (th == 0 and c == 0) else [], inc="pe")
            mmd = p.c("pe")
            ghist[oc] = mmd
            if oc < 16:
                sg = self.f_sg[oc % 2]
                qza = f"mza{oc % 2}"
                p.act(sg[:, 0:Tc], psA[:, 0:Tc], AF.Silu,
                      wait=[("pe", mmd),
                            (qza, self._m_zast.get(oc % 2, 0))])
                self._m_psact[oc % 2] = p.c("act")
                p.dma("sync", self.zsd[oc * 128:(oc + 1) * 128, :],
                      sg[:, 0:Tc], wait=[("act", p.c("act"))], inc=qza)
                self._m_zast[oc % 2] = p.c(qza)
            elif oc < 40:
                ch = oc - 16
                p.v("tensor_copy",
                    self.m_xbc[:, ch * Tc3 + 3:ch * Tc3 + 3 + Tc],
                    psA[:, 0:Tc], wait=[("pe", mmd)])
                self._m_psdve[oc % 2] = p.c("dve")
            else:
                p.v("tensor_copy", self.m_dtraw[:, 0:Tc], psA[0:32, 0:Tc],
                    wait=[("pe", mmd)])
                self._m_psdve[oc % 2] = p.c("dve")
        inproj_done = [("dve", p.c("dve")), ("act", p.c("act")),
                       ("pe", p.c("pe"))]
        if upto == "m1":
            return []
        for cc in range(24):
            p.dma("sync", self.cch_in[cc * 128:(cc + 1) * 128, 0:3],
                  self.m_xbc[:, cc * Tc3 + Tc:cc * Tc3 + Tc + 3],
                  wait=inproj_done if cc == 0 else [], inc="mh")
        p.cchandler("AllGather", [self.cch_in.ap().opt()],
                    [self.cch_out.ap().opt()], [list(range(NC))],
                    wait=[("mh", p.c("mh"))])
        cch = p.c("cc")
        xbc3 = self.m_xbc.ap().rearrange("p (c t) -> p c t", t=Tc3)
        for grp in range(4):
            p.emit("gpsimd",
                   lambda e, grp=grp: e.reg_load(
                       self.greg, self.c_halooff[0:1, 6 * grp:6 * grp + 1]),
                   wait=[("cc", cch), ("qc", self.const_w[0][1])]
                   if grp == 0 else [])
            p.dma("gpsimd", xbc3[:, 6 * grp:6 * (grp + 1), 0:3],
                  AP(self.cch_out, self.greg,
                     [[3, 128], [128 * 3, 6], [1, 3]]), inc="mhi")
        hld = p.c("mhi")
        p.v("tensor_scalar_mul", xbc3[:, 0:24, 0:3], xbc3[:, 0:24, 0:3],
            self.c_hz128[:, 0:1], wait=[("mhi", hld), ("dve", p.c("dve"))])
        halo_done = p.c("dve")
        convh = {}
        for cc in range(24):
            scr = self.m_cscr[cc % 2]
            w0 = [("dve", halo_done)]
            if cc >= 2:
                w0.append(("act", convh[cc - 2]))
            p.v("tensor_scalar", scr[:],
                self.m_xbc[:, cc * Tc3 + 0:cc * Tc3 + Tc],
                self.c_convw[:, cc * DCONV:cc * DCONV + 1],
                self.c_convb[:, cc:cc + 1], ALU.mult, ALU.add,
                wait=w0 + [("dve", p.c("dve"))])
            for k in range(1, DCONV):
                p.v("scalar_tensor_tensor", scr[:],
                    self.m_xbc[:, cc * Tc3 + k:cc * Tc3 + k + Tc],
                    self.c_convw[:, cc * DCONV + k:cc * DCONV + k + 1],
                    scr[:], ALU.mult, ALU.add, wait=[("dve", p.c("dve"))])
            p.act(self.m_xc[:, cc * Tc:(cc + 1) * Tc], scr[:], AF.Silu,
                  wait=[("dve", p.c("dve"))])
            convh[cc] = p.c("act")
        conv_done = p.c("act")
        if upto == "m3":
            return []
        dtb = self.c_dtb[0:NH_M, 0:1]
        sA = self.m_cscr[0][0:32, 0:Tc]
        sB = self.m_cscr[1][0:32, 0:Tc]
        p.act(sA, self.m_dtraw[:, 0:Tc], AF.Abs, bias=dtb,
              wait=[("act", conv_done), ("dve", p.c("dve"))])
        p.act(sB, sA, AF.Exp, scale=-1.0, wait=[("act", p.c("act"))])
        p.act(sA, sB, AF.Ln, bias=self.c_one[0:32, 0:1],
              wait=[("act", p.c("act"))])
        p.act(sB, self.m_dtraw[:, 0:Tc], AF.Relu, bias=dtb,
              wait=[("act", p.c("act"))])
        p.v("tensor_add", self.m_ld[32:64, 0:Tc], sA, sB,
            wait=[("dve", p.c("dve")), ("act", p.c("act"))])
        dtp_done = p.c("dve")
        if self.dbg:
            d5 = self._dram("dbg_dtraw", [32, Tc], F32, "ExternalOutput")
            p.dma("gpsimd", d5[:, :], self.m_dtraw[:, 0:Tc],
                  wait=[("dve", dtp_done)], inc="mhi")
            self._dtrawdump = p.c("mhi")
        p.v("tensor_scalar_mul", sA, self.m_ld[32:64, 0:Tc],
            self.c_nege[0:NH_M, 0:1], wait=[("dve", dtp_done)])
        p.v("memset", self.m_lacc[:], 0.0, wait=[("dve", p.c("dve"))])
        for j in range(NJ):
            p.v("tensor_tensor_scan", self.m_ld[0:32, j * 128:(j + 1) * 128],
                self.c_ones32[:, 0:128],
                self.m_cscr[0][0:32, j * 128:(j + 1) * 128],
                0.0, ALU.mult, ALU.add, wait=[("dve", p.c("dve"))])
            p.v("tensor_add", self.m_lacc[:], self.m_lacc[:],
                self.m_ld[0:32, j * 128 + 127:j * 128 + 128],
                wait=[("dve", p.c("dve"))])
        p.act(self.m_pcore[:], self.m_lacc[:], AF.Exp,
              wait=[("dve", p.c("dve"))])
        pcore_done = p.c("act")
        L_done = p.c("dve")
        tev = 0
        for j in range(NJ):
            wtr = [("dve", L_done), ("act", conv_done)]
            if self.dbg and j == 0:
                wtr.append(("mhi", self._dtrawdump))
            if j > 0:
                wtr.append(("dve", tev))
            p.trans(self.ps[2][:, 0:64], self.m_ld[:, j * 128:(j + 1) * 128],
                    self.c_identf[0:64, 0:64], wait=wtr, inc="pe")
            for g2 in range(4):
                p.trans(self.pbf[:, g2 * 128:(g2 + 1) * 128],
                        self.m_xc[:, g2 * Tc + j * 128:g2 * Tc + (j + 1) * 128],
                        self.c_identbf[:], inc="pe")
                p.trans(self.pbf[:, (4 + g2) * 128:(5 + g2) * 128],
                        self.m_xc[:, (4 + g2) * Tc + j * 128:
                                  (4 + g2) * Tc + (j + 1) * 128],
                        self.c_identbf[:], inc="pe")
            tp = p.c("pe")
            p.v("tensor_copy", self.m_ldt[:, j * 64:(j + 1) * 64],
                self.ps[2][:, 0:64], wait=[("pe", tp)])
            p.v("tensor_scalar_mul", self.m_nldt[:, j * 64:(j + 1) * 64],
                self.m_ldt[:, j * 64:(j + 1) * 64], -1.0,
                wait=[("dve", p.c("dve"))])
            for g2 in range(4):
                p.v("tensor_copy",
                    self.m_xt[:, j * 512 + g2 * 128:j * 512 + (g2 + 1) * 128],
                    self.pbf[:, g2 * 128:(g2 + 1) * 128],
                    wait=[("pe", tp)] if g2 == 0 else [])
                p.v("tensor_copy",
                    self.m_bt[:, j * 512 + g2 * 128:j * 512 + (g2 + 1) * 128],
                    self.pbf[:, (4 + g2) * 128:(5 + g2) * 128])
            tev = p.c("dve")
        p.v("memset", self.m_hst[:], 0.0, wait=[("dve", tev)])
        prep_done = p.c("dve")
        if upto == "m4":
            return []
        bL = [self.ps[2][:, 0:128], self.ps[2][:, 512:640]]
        Gp = [self.ps[0][:, 0:128], self.ps[0][:, 512:640]]
        Yp = [self.ps[1][0:64, 0:128], self.ps[1][0:64, 512:640]]
        Sp = self.ps3b[0:64, 0:64]
        ih = {}
        lvl = 9
        if upto is not None and upto.startswith("m5") and len(upto) > 2:
            lvl = int(upto[2])
        u = 0
        for j in range(NJ):
            for h in range(NH_M):
                g = h // 4
                pb = bL[u % 2]
                wbl = [("dve", prep_done), ("qc", self.const_w[0][1]),
                       ("mz0", p.c("mz0")), ("mz1", p.c("mz1"))]
                if u >= 2:
                    wbl += [("act", ih[u - 2]["act"]), ("dve", ih[u - 2]["dve"])]
                p.mm(pb, AP(self.c_i32sel, h, [[32, 32], [0, 128]]),
                     self.m_ld[0:32, j * 128:(j + 1) * 128], True, True,
                     wait=wbl, inc="pe")
                blm = p.c("pe")
                esb = self.m_esb[u % 2]
                p.act(esb[:], pb, AF.Exp, wait=[("pe", blm)])
                ea = p.c("act")
                if lvl >= 2:
                    arg = self.m_arg[u % 2]
                    p.v("scalar_tensor_tensor", arg[:], pb,
                        self.m_nldt[:, j * 64 + h:j * 64 + h + 1],
                        self.c_trineg[:], ALU.add, ALU.add,
                        wait=[("pe", blm), ("act", ea), ("dve", p.c("dve"))])
                    p.act(self.m_dtile[u % 2][:], arg[:], AF.Exp,
                          wait=[("dve", p.c("dve")), ("act", ea)])
                dta = p.c("act")
                if lvl >= 3:
                    bw = self.m_bw[u % 2]
                    bts = self.m_bt[:, j * 512 + (g // 2) * 128 + (g % 2) * 64:
                                    j * 512 + (g // 2) * 128 + (g % 2) * 64 + 64]
                    p.v("tensor_scalar", bw[:], bts,
                        self.m_ldt[:, j * 64 + 32 + h:j * 64 + 32 + h + 1],
                        self.m_dtile[u % 2][:, 127:128], ALU.mult, ALU.mult,
                        wait=[("act", dta), ("dve", p.c("dve"))])
                    bwd = p.c("dve")
                    xts = self.m_xt[:, j * 512 + (g // 2) * 128 + (g % 2) * 64:
                                    j * 512 + (g // 2) * 128 + (g % 2) * 64 + 64]
                    p.mm(Sp, bw[:], xts, True, True,
                         wait=[("dve", bwd), ("dve", self._m_hrun)], inc="pe")
                sm = p.c("pe")
                if lvl >= 4:
                    p.v("scalar_tensor_tensor",
                        self.m_hst[:, h * HD:(h + 1) * HD],
                        self.m_hst[:, h * HD:(h + 1) * HD],
                        esb[0:64, 127:128], Sp, ALU.mult, ALU.add,
                        wait=[("pe", sm), ("act", ea), ("dve", p.c("dve"))])
                    self._m_hrun = p.c("dve")
                ih[u] = p.snap()
                u += 1
        pass1_done = p.c("dve")
        if upto is not None and upto.startswith("m5"):
            return []
        p.dma("sync", self.ccs_in[0:64, :], self.m_hst[:],
              wait=[("dve", pass1_done)], inc="msum")
        p.dma("sync", AP(self.ccs_in, 64 * NH_M * HD, [[1, NH_M], [1, 1]]),
              self.m_pcore[:], wait=[("act", pcore_done)], inc="msum",
              allow_slow_non_contiguous=True)
        p.v("memset", self.f_scr[0][0:1, 0:TT], 0.0,
            wait=[("dve", p.c("dve")), ("act", nrm2)])
        zmd = p.c("dve")
        nz = NH_M * HD - NH_M
        for zi in range((nz + TT - 1) // TT):
            z0, z1 = zi * TT, min((zi + 1) * TT, nz)
            p.dma("sync", AP(self.ccs_in, 64 * NH_M * HD + NH_M + z0,
                             [[1, 1], [1, z1 - z0]]),
                  self.f_scr[0][0:1, 0:z1 - z0],
                  wait=[("dve", zmd)], inc="msum")
        p.cchandler("AllGather", [self.ccs_in.ap().opt()],
                    [self.ccs_out.ap().opt()], [list(range(NC))],
                    wait=[("msum", p.c("msum"))])
        ccs = p.c("cc")
        p.dma("sync", self.m_grow.ap().rearrange("o (r h) -> o r h", h=32),
              AP(self.ccs_out, 64 * NH_M * HD,
                 [[1, 1], [(DS + 1) * NH_M * HD, NC], [1, 32]]),
              wait=[("cc", ccs)], inc="msum")
        gld = p.c("msum")
        p.v("tensor_mul", R[0:1, 0:NC * 32], self.m_grow[0:1, :],
            self.c_pmrow[0:1, :], wait=[("msum", gld), ("dve", p.c("dve"))])
        p.v("tensor_scalar", R[0:1, NC * 32:2 * NC * 32], self.c_pmrow[0:1, :],
            -1.0, 1.0, ALU.mult, ALU.add, wait=[("dve", p.c("dve"))])
        p.v("tensor_add", R[0:1, 0:NC * 32], R[0:1, 0:NC * 32],
            R[0:1, NC * 32:2 * NC * 32], wait=[("dve", p.c("dve"))])
        p.v("memset", self.m_grow[0:1, (NC - 1) * 32:NC * 32], 1.0,
            wait=[("dve", p.c("dve"))])
        for r in range(NC - 2, -1, -1):
            p.v("tensor_mul", self.m_grow[0:1, r * 32:(r + 1) * 32],
                self.m_grow[0:1, (r + 1) * 32:(r + 2) * 32],
                R[0:1, (r + 1) * 32:(r + 2) * 32], wait=[("dve", p.c("dve"))])
        p.v("tensor_mul", self.m_wrow[0:1, :], self.m_grow[0:1, :],
            self.c_pmrow[0:1, :], wait=[("dve", p.c("dve"))])
        wrd = p.c("dve")
        p.mm(self.ps[2][0:64, 0:NC * 32], self.c_onesrow[0:1, 0:64],
             self.m_wrow[0:1, :], True, True, wait=[("dve", wrd)], inc="pe")
        p.v("tensor_copy", self.m_pb[:, 0:NC * 32],
            self.ps[2][0:64, 0:NC * 32], wait=[("pe", p.c("pe"))])
        p.v("memset", self.m_hst[:], 0.0,
            wait=[("dve", p.c("dve")), ("msum", p.c("msum"))])
        pbd = p.c("dve")
        for r in range(NC):
            sr = self.m_sr[r % 2]
            qs = f"msr{r % 2}"
            wl = [("cc", ccs), ("dve", pbd)]
            if r >= 2:
                wl.append(("dve", self._m_srd.get(r % 2, 0)))
            p.dma("sync", sr[:],
                  self.ccs_out[r * (DS + 1):r * (DS + 1) + 64, :],
                  wait=wl, inc=qs)
            srl = p.c(qs)
            for h in range(NH_M):
                p.v("scalar_tensor_tensor", self.m_hst[:, h * HD:(h + 1) * HD],
                    sr[:, h * HD:(h + 1) * HD],
                    self.m_pb[:, r * 32 + h:r * 32 + h + 1],
                    self.m_hst[:, h * HD:(h + 1) * HD], ALU.mult, ALU.add,
                    wait=[(qs, srl), ("dve", p.c("dve"))] if h == 0 else
                    [("dve", p.c("dve"))])
            self._m_srd[r % 2] = p.c("dve")
        hstart_done = p.c("dve")
        if upto == "m6":
            return []
        u = 0
        ih2 = {}
        for j in range(NJ):
            p.v("tensor_copy", self.m_hb[:], self.m_hst[:],
                wait=[("dve", p.c("dve"))])
            hbd = p.c("dve")
            for h in range(NH_M):
                g = h // 4
                cc, cb = c_head_slot(h)
                pb = bL[u % 2]
                wbl = [("dve", hstart_done), ("dve", hbd)]
                if u >= 2:
                    wbl += [("act", ih2[u - 2]["act"]),
                            ("dve", ih2[u - 2]["dve"])]
                p.mm(pb, AP(self.c_i32sel, h, [[32, 32], [0, 128]]),
                     self.m_ld[0:32, j * 128:(j + 1) * 128], True, True,
                     wait=wbl, inc="pe")
                blm = p.c("pe")
                esb = self.m_esb[u % 2]
                p.act(esb[:], pb, AF.Exp, wait=[("pe", blm)])
                ea = p.c("act")
                arg = self.m_arg[u % 2]
                p.v("scalar_tensor_tensor", arg[:], pb,
                    self.m_nldt[:, j * 64 + h:j * 64 + h + 1],
                    self.c_trineg[:], ALU.add, ALU.add,
                    wait=[("pe", blm), ("act", ea), ("dve", p.c("dve"))])
                p.act(self.m_dtile[u % 2][:], arg[:], AF.Exp,
                      wait=[("dve", p.c("dve")), ("act", ea)])
                dta = p.c("act")
                Gb = Gp[u % 2]
                bsec = self.m_xc[(g % 2) * 64:(g % 2) * 64 + 64,
                                 (4 + g // 2) * Tc + j * 128:
                                 (4 + g // 2) * Tc + (j + 1) * 128]
                csec = self.m_xc[cb:cb + 64,
                                 (8 + cc) * Tc + j * 128:
                                 (8 + cc) * Tc + (j + 1) * 128]
                p.mm(Gb, bsec, csec, True, True,
                     wait=[("act", conv_done),
                           ("dve", self._m_wtd.get(u % 2, 0))], inc="pe")
                gm = p.c("pe")
                wt = self.m_wt[u % 2]
                p.v("scalar_tensor_tensor", wt[:], Gb,
                    self.m_ldt[:, j * 64 + 32 + h:j * 64 + 32 + h + 1],
                    self.m_dtile[u % 2][:], ALU.mult, ALU.mult,
                    wait=[("pe", gm), ("act", dta), ("dve", p.c("dve"))])
                wtd = p.c("dve")
                self._m_wtd[u % 2] = wtd
                cs = self.m_cs[u % 2]
                p.v("tensor_mul", cs[:], csec, esb[cb:cb + 64, :],
                    wait=[("act", ea), ("dve", p.c("dve"))])
                csd = p.c("dve")
                Yb = Yp[u % 2]
                xts = self.m_xt[:, j * 512 + (g // 2) * 128 + (g % 2) * 64:
                                j * 512 + (g // 2) * 128 + (g % 2) * 64 + 64]
                p.mm(Yb, xts, wt[:], True, False,
                     wait=[("dve", wtd), ("dve", self._m_yev.get(u % 2, 0))],
                     inc="pe")
                p.mm(Yb, self.m_hb[:, h * HD:(h + 1) * HD], cs[:],
                     False, True, wait=[("dve", csd)], inc="pe")
                ym = p.c("pe")
                xsec = self.m_xc[(g % 2) * 64:(g % 2) * 64 + 64,
                                 (g // 2) * Tc + j * 128:
                                 (g // 2) * Tc + (j + 1) * 128]
                p.v("scalar_tensor_tensor",
                    self.m_y[(h % 2) * 64:(h % 2) * 64 + 64,
                             (h // 2) * Tc + j * 128:
                             (h // 2) * Tc + (j + 1) * 128],
                    xsec, self.c_dm64[(g % 2) * 64:(g % 2) * 64 + 64, h:h + 1],
                    Yb, ALU.mult, ALU.add,
                    wait=[("pe", ym), ("dve", p.c("dve"))])
                self._m_yev[u % 2] = p.c("dve")
                bw = self.m_bw[u % 2]
                p.v("tensor_scalar", bw[:], bts,
                    self.m_ldt[:, j * 64 + 32 + h:j * 64 + 32 + h + 1],
                    self.m_dtile[u % 2][:, 127:128], ALU.mult, ALU.mult,
                    wait=[("act", dta), ("dve", p.c("dve"))])
                bwd = p.c("dve")
                p.mm(Sp, bw[:], xts, True, True,
                     wait=[("dve", bwd), ("dve", self._m_hrun)], inc="pe")
                sm = p.c("pe")
                p.v("scalar_tensor_tensor", self.m_hst[:, h * HD:(h + 1) * HD],
                    self.m_hst[:, h * HD:(h + 1) * HD],
                    esb[0:64, 127:128], Sp, ALU.mult, ALU.add,
                    wait=[("pe", sm), ("dve", p.c("dve"))])
                self._m_hrun = p.c("dve")
                ih2[u] = p.snap()
                u += 1
        y_done = p.c("dve")
        if upto == "m7":
            return []
        if self.dbg:
            d1 = self._dram("dbg_xc", [24 * 128, Tc], F32, "ExternalOutput")
            p.dma("gpsimd", d1.ap().rearrange("(c p) t -> p c t", p=128),
                  self.m_xc.ap().rearrange("p (c t) -> p c t", t=Tc),
                  wait=[("act", conv_done)], inc="mhi")
            d2 = self._dram("dbg_ld", [64, Tc], F32, "ExternalOutput")
            p.dma("gpsimd", d2[:, :], self.m_ld[:], wait=[("dve", y_done)],
                  inc="mhi")
            d3 = self._dram("dbg_y", [NCH * 128, Tc], F32, "ExternalOutput")
            p.dma("gpsimd", d3.ap().rearrange("(c p) t -> p c t", p=128),
                  self.m_y.ap().rearrange("p (c t) -> p c t", t=Tc),
                  wait=[("dve", y_done)], inc="mhi")
            d4 = self._dram("dbg_hst", [64, NH_M * HD], F32, "ExternalOutput")
            p.dma("gpsimd", d4[:, :], self.m_hst[:],
                  wait=[("dve", hstart_done)], inc="mhi")

        gh = {}
        for c in range(NCH):
            zs = self.m_zs[c % 2]
            ql = f"ml{c % 2}"
            wl = [("dve", y_done), ("mza0", p.c("mza0")),
                  ("mza1", p.c("mza1")), ("mhi", p.c("mhi"))]
            if c >= 2:
                wl.append(("dve", gh[c - 2]["dve"]))
            p.dma("sync", zs[:], self.zsd[c * 128:(c + 1) * 128, :],
                  wait=wl, inc=ql)
            ld = p.c(ql)
            p.v("tensor_mul", self.m_y[:, c * Tc:(c + 1) * Tc],
                self.m_y[:, c * Tc:(c + 1) * Tc], zs[:],
                wait=[(ql, ld), ("dve", p.c("dve"))])
            yzd = p.c("dve")
            x2b = self.m_x2b[c % 2]
            for th in range(nth):
                p.act(x2b[:], self.m_y[:, c * Tc + th * TT:
                                       c * Tc + (th + 1) * TT], AF.Square,
                      wait=[("dve", yzd), ("act", p.c("act")),
                            ("pe", self._m_g2pe.get(c % 2, 0))])
                p.mm(s1p[th], self.c_onescol[:, 0:1], x2b[:],
                     c == 0, c == NCH - 1,
                     wait=[("act", p.c("act"))], inc="pe")
            self._m_g2pe[c % 2] = p.c("pe")
            gh[c] = p.snap()
        smm3 = p.c("pe")
        for th in range(nth):
            vb = self.ps[2][:, 512:512 + TT]
            p.v("tensor_copy", R[0:1, TT:2 * TT], s1p[th],
                wait=[("pe", max(smm3, self._f_bcd)), ("dve", p.c("dve"))])
            p.mm(vb, self.c_onesrow[:], R[0:1, TT:2 * TT], True, True,
                 wait=[("dve", p.c("dve")), ("act", self._f_bcact)], inc="pe")
            self._f_bcd = p.c("pe")
            p.act(self.f_scr[0][:], vb, AF.Ln, bias=self.c_lneps[:, 0:1],
                  scale=1.0 / D_MODEL,
                  wait=[("pe", p.c("pe")), ("dve", p.c("dve"))])
            p.act(self.f_rb[th][:], self.f_scr[0][:], AF.Exp, scale=-0.5,
                  wait=[("act", p.c("act"))])
            self._f_bcact = p.c("act")
        nrm3 = p.c("act")
        for c in range(NCH):
            for th in range(nth):
                p.v("tensor_mul", self.f_scr[th % 2][:],
                    self.m_y[:, c * Tc + th * TT:c * Tc + (th + 1) * TT],
                    self.f_rb[th][:], wait=[("act", nrm3), ("dve", p.c("dve"))])
                p.v("tensor_scalar_mul",
                    self.m_y[:, c * Tc + th * TT:c * Tc + (th + 1) * TT],
                    self.f_scr[th % 2][:], self.c_ssmw[:, c:c + 1],
                    wait=[("dve", p.c("dve"))])
        ynorm_done = p.c("dve")
        cnt = 0
        hist = {}
        wh = {}
        for oc in range(NCH):
            fwq = f"fw{oc % 2}"
            wb = self.f_wd[oc % 2]
            wwt = [("dve", ynorm_done)]
            if oc >= 2:
                wwt.append(("pe", wh[oc - 2]))
            for grp in range(4):
                p.dma("gpsimd", wb[:, grp * 512:(grp + 1) * 512],
                      AP(self.outprojT, oc * 128 + grp * 4 * 128 * D_MODEL,
                         [[D_MODEL, 128], [128 * D_MODEL, 4], [1, 128]]),
                      wait=wwt if grp == 0 else [], inc=fwq)
            wld = p.c(fwq)
            for th in range(nth):
                psb = self.ps[0][:, 0:TT] if cnt % 2 == 0 else self.ps[1][:, 0:TT]
                wmm = [("dve", ynorm_done), (fwq, wld),
                       ("dve", hist.get(cnt - 2, {"dve": 0})["dve"])]
                for c in range(NCH):
                    p.mm(psb, wb[:, c * 128:(c + 1) * 128],
                         self.m_y[:, c * Tc + th * TT:c * Tc + (th + 1) * TT],
                         c == 0, c == NCH - 1, wait=wmm if c == 0 else [],
                         inc="pe")
                mmd = p.c("pe")
                rz = self.f_res[th % 2]
                qra = f"fra{th % 2}"
                p.dma("sync", rz[:], self.z1d[oc * 128:(oc + 1) * 128,
                                              th * TT:(th + 1) * TT],
                      wait=[("dve", hist.get(cnt - nth, {"dve": 0})["dve"]),
                            ("dve", ynorm_done),
                            ("mz0", p.c("mz0")), ("mz1", p.c("mz1"))],
                      inc=qra)
                rld = p.c(qra)
                st = self.f_st[th % 2]
                qs = f"mo{th % 2}"
                p.v("tensor_add", st[:], psb, rz[:],
                    wait=[("pe", mmd), (qra, rld),
                          (qs, hist.get(cnt - nth, {qs: 0}).get(qs, 0))])
                p.dma("sync", self.inner2d[oc * 128:(oc + 1) * 128,
                                           th * TT:(th + 1) * TT], st[:],
                      wait=[("dve", p.c("dve"))], inc=qs)
                hist[cnt] = p.snap()
                cnt += 1
            wh[oc] = p.c("pe")
        return [("mo0", p.c("mo0")), ("mo1", p.c("mo1"))]

    def ffn_block(self, in_dram, wg, wu, wd, res_drams, out_dram, ready,
                  norm="ln", normw=None, out_tm=False):
        p, Tc, TT = self.p, self.Tc, self.TT
        nth = Tc // TT
        for c in range(NCH):
            p.dma("gpsimd", self.f_xb[:, c * Tc:(c + 1) * Tc],
                  in_dram[c * 128:(c + 1) * 128, :],
                  wait=ready if c == 0 else [], inc="flx")
        lx = p.c("flx")
        s1p = [self.ps[0][0:1, 0:TT], self.ps[0][0:1, 512:512 + TT]]
        s2p = [self.ps[1][0:1, 0:TT], self.ps[1][0:1, 512:512 + TT]]
        for c in range(NCH):
            for th in range(nth):
                up = (c * nth + th) % 2
                x2 = self.f_x2[up]
                wsq = [("flx", lx)] + (ready if (c == 0 and th == 0) else [])
                wsq.append(("pe", self._f_x2pe.get(up, 0)))
                p.act(x2[:], self.f_xb[:, c * Tc + th * TT:
                                       c * Tc + (th + 1) * TT], AF.Square,
                      wait=wsq)
                aq = p.c("act")
                if norm == "ln":
                    p.mm(s1p[th], self.c_onescol[:, 0:1],
                         self.f_xb[:, c * Tc + th * TT:c * Tc + (th + 1) * TT],
                         c == 0, c == NCH - 1,
                         wait=[("flx", lx), ("act", aq)], inc="pe")
                p.mm(s2p[th], self.c_onescol[:, 0:1],
                     x2[:], c == 0, c == NCH - 1,
                     wait=[("act", aq)], inc="pe")
                self._f_x2pe[up] = p.c("pe")
        smm = p.c("pe")
        R = self.f_rows
        appw = []
        for th in range(nth):
            mb = self.ps[2][:, 0:TT]
            vb = self.ps[2][:, 512:512 + TT]
            if norm == "ln":
                p.v("tensor_copy", R[0:1, 0:TT], s1p[th],
                    wait=[("pe", max(smm, self._f_bcd)),
                          ("dve", self._f_rowdve)])
                p.v("tensor_copy", R[0:1, TT:2 * TT], s2p[th],
                    wait=[("dve", p.c("dve"))])
                p.v("tensor_scalar_mul", R[0:1, 2 * TT:3 * TT], R[0:1, 0:TT],
                    1.0 / D_MODEL, wait=[("dve", p.c("dve"))])
                p.v("tensor_mul", R[0:1, 3 * TT:4 * TT],
                    R[0:1, 2 * TT:3 * TT], R[0:1, 2 * TT:3 * TT],
                    wait=[("dve", p.c("dve"))])
                p.v("scalar_tensor_tensor", R[0:1, 4 * TT:5 * TT],
                    R[0:1, TT:2 * TT], 1.0 / D_MODEL, R[0:1, 3 * TT:4 * TT],
                    ALU.mult, ALU.subtract, wait=[("dve", p.c("dve"))])
                rowd = p.c("dve")
                self._f_rowdve = rowd
                wbc = [("dve", rowd), ("act", self._f_bcact)]
                p.mm(mb, self.c_onesrow[:], R[0:1, 2 * TT:3 * TT], True, True,
                     wait=wbc, inc="pe")
                p.mm(vb, self.c_onesrow[:], R[0:1, 4 * TT:5 * TT], True, True,
                     inc="pe")
                bcd = p.c("pe")
                self._f_bcd = bcd
                p.act(self.f_scr[0][:], vb, AF.Ln, bias=self.c_lneps[:, 0:1],
                      wait=[("pe", bcd), ("dve", self._f_aplydve)])
                p.act(self.f_rb[th][:], self.f_scr[0][:], AF.Exp, scale=-0.5,
                      wait=[("act", p.c("act"))])
                p.act(self.f_mub[th][:], mb, AF.Copy,
                      wait=[("act", p.c("act"))])
                self._f_bcact = p.c("act")
            else:
                p.v("tensor_copy", R[0:1, TT:2 * TT], s2p[th],
                    wait=[("pe", max(smm, self._f_bcd)),
                          ("dve", self._f_rowdve)])
                rowd = p.c("dve")
                self._f_rowdve = rowd
                wbc = [("dve", rowd), ("act", self._f_bcact)]
                p.mm(vb, self.c_onesrow[:], R[0:1, TT:2 * TT], True, True,
                     wait=wbc, inc="pe")
                bcd = p.c("pe")
                self._f_bcd = bcd
                p.act(self.f_scr[0][:], vb, AF.Ln, bias=self.c_lneps[:, 0:1],
                      scale=1.0 / D_MODEL,
                      wait=[("pe", bcd), ("dve", self._f_aplydve)])
                p.act(self.f_rb[th][:], self.f_scr[0][:], AF.Exp, scale=-0.5,
                      wait=[("act", p.c("act"))])
                self._f_bcact = p.c("act")
            nrm = p.c("act")
            for c in range(NCH):
                scr = self.f_scr[c % 2]
                wap = [("act", nrm), ("dve", p.c("dve"))]
                if norm == "ln":
                    p.v("tensor_sub",
                        scr[:], self.f_xb[:, c * Tc + th * TT:
                                          c * Tc + (th + 1) * TT],
                        self.f_mub[th][:], wait=wap)
                    p.v("tensor_mul",
                        self.f_xb[:, c * Tc + th * TT:c * Tc + (th + 1) * TT],
                        scr[:], self.f_rb[th][:], wait=[("dve", p.c("dve"))])
                else:
                    p.v("tensor_mul", scr[:],
                        self.f_xb[:, c * Tc + th * TT:c * Tc + (th + 1) * TT],
                        self.f_rb[th][:], wait=wap)
                    p.v("tensor_scalar_mul",
                        self.f_xb[:, c * Tc + th * TT:c * Tc + (th + 1) * TT],
                        scr[:], normw[:, c:c + 1], wait=[("dve", p.c("dve"))])
            self._f_aplydve = p.c("dve")
        xb_ready = p.c("dve")
        ghist = {}
        for oc in range(NFH):
            fwq = f"fw{oc % 2}"
            wgb = self.f_w[oc % 2]
            wub = self.f_w[2 + oc % 2]
            wwt = [("dve", xb_ready)]
            if oc >= 2:
                wwt.append(("pe", ghist[oc - 2]))
            for grp in range(4):
                p.dma("gpsimd", wgb[:, grp * 512:(grp + 1) * 512],
                      AP(wg, oc * 128 + grp * 4 * 128 * D_FF,
                         [[D_FF, 128], [128 * D_FF, 4], [1, 128]]),
                      wait=wwt if grp == 0 else [], inc=fwq)
            for grp in range(4):
                p.dma("gpsimd", wub[:, grp * 512:(grp + 1) * 512],
                      AP(wu, oc * 128 + grp * 4 * 128 * D_FF,
                         [[D_FF, 128], [128 * D_FF, 4], [1, 128]]),
                      inc=fwq)
            wld = p.c(fwq)
            psA = self.ps[(2 * oc) % 3]
            psB = self.ps[(2 * oc + 1) % 3]
            wA = [("dve", xb_ready), (fwq, wld),
                  ("act", self._f_psact.get((2 * oc) % 3, 0)),
                  ("dve", self._f_psdve.get((2 * oc) % 3, 0))]
            for th in range(nth):
                for c in range(NCH):
                    p.mm(psA[:, th * TT:th * TT + TT],
                         wgb[:, c * 128:(c + 1) * 128],
                         self.f_xb[:, c * Tc + th * TT:c * Tc + (th + 1) * TT],
                         c == 0, c == NCH - 1,
                         wait=wA if (th == 0 and c == 0) else [], inc="pe")
            gmm = p.c("pe")
            wB = [(fwq, wld), ("act", self._f_psact.get((2 * oc + 1) % 3, 0)),
                  ("dve", self._f_psdve.get((2 * oc + 1) % 3, 0))]
            for th in range(nth):
                for c in range(NCH):
                    p.mm(psB[:, th * TT:th * TT + TT],
                         wub[:, c * 128:(c + 1) * 128],
                         self.f_xb[:, c * Tc + th * TT:c * Tc + (th + 1) * TT],
                         c == 0, c == NCH - 1,
                         wait=wB if (th == 0 and c == 0) else [], inc="pe")
            umm = p.c("pe")
            sg = self.f_sg[oc % 2]
            p.act(sg[:, 0:Tc], psA[:, 0:Tc], AF.Silu,
                  wait=[("pe", gmm), ("dve", self._f_sgdve.get(oc % 2, 0))])
            self._f_psact[(2 * oc) % 3] = p.c("act")
            p.v("tensor_mul", self.f_hid[:, oc * Tc:(oc + 1) * Tc],
                psB[:, 0:Tc], sg[:, 0:Tc],
                wait=[("pe", umm), ("act", p.c("act"))])
            self._f_psdve[(2 * oc + 1) % 3] = p.c("dve")
            self._f_psdve[(2 * oc) % 3] = p.c("dve")
            self._f_sgdve[oc % 2] = p.c("dve")
            ghist[oc] = p.c("pe")
        hid_done = p.c("dve")
        gu_pe = p.c("pe")
        cnt = 0
        hist = {}
        dwh = {}
        for oc2 in range(NCH):
            fdq = f"fd{oc2 % 2}"
            wdb = self.f_wd[oc2 % 2]
            wwt = [("pe", gu_pe)]
            if oc2 >= 2:
                wwt.append(("pe", dwh[oc2 - 2]))
            for grp in range(16):
                p.dma("gpsimd", wdb[:, grp * 512:(grp + 1) * 512],
                      AP(wd, oc2 * 128 + grp * 4 * 128 * D_MODEL,
                         [[D_MODEL, 128], [128 * D_MODEL, 4], [1, 128]]),
                      wait=wwt if grp == 0 else [], inc=fdq)
            wld = p.c(fdq)
            for th in range(nth):
                psb = self.ps[0][:, 0:TT] if cnt % 2 == 0 else self.ps[1][:, 0:TT]
                wmm = [("dve", hid_done), (fdq, wld),
                       ("dve", hist.get(cnt - 2, {"dve": 0})["dve"])]
                for hc in range(NFH):
                    p.mm(psb, wdb[:, hc * 128:(hc + 1) * 128],
                         self.f_hid[:, hc * Tc + th * TT:
                                    hc * Tc + (th + 1) * TT],
                         hc == 0, hc == NFH - 1,
                         wait=wmm if hc == 0 else [], inc="pe")
                mmd = p.c("pe")
                ra = self.f_res[th % 2]
                qra = f"fra{th % 2}"
                wr = [("dve", hist.get(cnt - nth, {"dve": 0})["dve"]),
                      ("pe", gu_pe), ("dve", hid_done)]
                p.dma("sync", ra[:], res_drams[0][oc2 * 128:(oc2 + 1) * 128,
                                                  th * TT:(th + 1) * TT],
                      wait=wr, inc=qra)
                rld = [(qra, p.c(qra))]
                if len(res_drams) > 1:
                    rb2 = self.f_res2[th % 2]
                    qrb = f"frb{th % 2}"
                    p.dma("sync", rb2[:],
                          res_drams[1][oc2 * 128:(oc2 + 1) * 128,
                                       th * TT:(th + 1) * TT],
                          wait=wr, inc=qrb)
                    rld.append((qrb, p.c(qrb)))
                st = self.f_st[th % 2]
                qs = f"fs{th % 2}"
                wev = [("pe", mmd),
                       (qs, hist.get(cnt - nth, {qs: 0}).get(qs, 0))] + rld
                if len(res_drams) > 1:
                    p.v("tensor_add", st[:], psb, self.f_res2[th % 2][:],
                        wait=wev)
                    p.v("tensor_add", st[:], st[:], ra[:],
                        wait=[("dve", p.c("dve"))])
                else:
                    p.v("tensor_add", st[:], psb, ra[:], wait=wev)
                evd = p.c("dve")
                if not out_tm:
                    p.dma("sync", out_dram[oc2 * 128:(oc2 + 1) * 128,
                                           th * TT:(th + 1) * TT], st[:],
                          wait=[("dve", evd)], inc=qs)
                else:
                    for c4 in range(TT // 128):
                        wtr = [("dve", evd)]
                        if cnt > 0 or c4 > 0:
                            wtr.append(("dve", self._f_tmev))
                        p.trans(self.ps[2][:, c4 * 128:(c4 + 1) * 128],
                                st[:, c4 * 128:(c4 + 1) * 128],
                                self.c_identf[:],
                                wait=wtr if c4 == 0 else [], inc="pe")
                    tp = p.c("pe")
                    ob = self.f_ost[th % 2]
                    for c4 in range(TT // 128):
                        p.v("tensor_copy", ob[:, c4 * 128:(c4 + 1) * 128],
                            self.ps[2][:, c4 * 128:(c4 + 1) * 128],
                            wait=[("pe", tp),
                                  (qs, hist.get(cnt - nth, {qs: 0}).get(qs, 0))]
                            if c4 == 0 else [])
                    self._f_tmev = p.c("dve")
                    for c4 in range(TT // 128):
                        p.dma("sync",
                              out_dram[th * TT + c4 * 128:
                                       th * TT + (c4 + 1) * 128,
                                       oc2 * 128:(oc2 + 1) * 128],
                              ob[:, c4 * 128:(c4 + 1) * 128],
                              wait=[("dve", self._f_tmev)] if c4 == 0 else [],
                              inc=qs)
                hist[cnt] = p.snap()
                cnt += 1
            dwh[oc2] = p.c("pe")
        return [("fs0", p.c("fs0")), ("fs1", p.c("fs1"))]

    def build(self, upto=None):
        self._p0ev = 0
        self._bev = 0
        self._bst2 = {}
        self._v_hist = []
        self._p1_prev_rstd = 0
        self._f_rowdve = 0
        self._f_bcact = 0
        self._f_aplydve = 0
        self._f_x2pe = {}
        self._f_psact = {}
        self._f_psdve = {}
        self._f_sgdve = {}
        self._f_tmev = 0
        self._f_bcd = 0
        self._m_cpd = {}
        self._m_psact = {}
        self._m_psdve = {}
        self._m_zast = {}
        self._m_hrun = 0
        self._m_srd = {}
        self._m_wtd = {}
        self._m_yev = {}
        self._m_g2pe = {}
        self.const_load()
        self.phase0()
        if upto == "p0":
            self.dump_end()
            return self._fin()
        self.phase1()
        if upto == "p1":
            self.dump_qkv()
            self.dump_end()
            return self._fin()
        self.phase2()
        if upto == "p2":
            self.dump_o()
            self.dump_end()
            return self._fin()
        self.phase3()
        if upto == "p3":
            self.finish(self.r1d)
            return self._fin()
        ready = list(self.r1_done) + [("dve", self.p.c("dve")),
                                      ("pe", self.p.c("pe"))]
        self.r2_done = self.ffn_block(self.r1d, self.gateT, self.upT,
                                      self.downT, [self.r1d], self.r2d,
                                      ready, norm="ln")
        if upto == "ffn1":
            self.finish(self.r2d)
            return self._fin()
        ready = list(self.r2_done) + [("dve", self.p.c("dve")),
                                      ("pe", self.p.c("pe")),
                                      ("act", self.p.c("act"))]
        self.i2_done = self.mamba(ready, upto=upto)
        if upto is not None and upto.startswith("m") and upto != "mamba":
            self.dump_end()
            return self._fin()
        if upto == "mamba":
            self.finish(self.inner2d)
            return self._fin()
        ready = list(self.i2_done) + [("dve", self.p.c("dve")),
                                      ("pe", self.p.c("pe")),
                                      ("act", self.p.c("act"))]
        self.r4_done = self.ffn_block(self.inner2d, self.bgateT, self.bupT,
                                      self.bdownT, [self.inner2d, self.r2d],
                                      self.r4d, ready, norm="rms",
                                      normw=self.c_postlnw)
        if upto == "ffnb":
            self.finish(self.r4d)
            return self._fin()
        ready = list(self.r4_done) + [("dve", self.p.c("dve")),
                                      ("pe", self.p.c("pe")),
                                      ("act", self.p.c("act"))]
        self.y_done = self.ffn_block(self.r4d, self.gateT, self.upT,
                                     self.downT, [self.r4d], self.y_sh,
                                     ready, norm="ln", out_tm=True)
        self.p.emit("sync", lambda e: e.nop(),
                    wait=[("fs0", self.p.c("fs0")), ("fs1", self.p.c("fs1"))])
        return self._fin()

    def _fin(self):
        self.p.replay()
        self.nc.finalize()
        return self.nc

    def dump_qkv(self):
        p, T = self.p, self.T
        d = self._dram("dbg_qkv", [3 * 128, self.Ha * T], F32,
                       "ExternalOutput")
        w = [("dve", self.p1_done["dve"]), ("pe", self.p1_done["pe"])]
        p.dma("gpsimd", d[0:128, :], self.q_reg[:], wait=w, inc="gA")
        p.dma("gpsimd", d[128:256, :], self.k_reg[:], inc="gA")
        p.dma("gpsimd", d[256:384, :], self.v_reg[:], inc="gA")

    def dump_o(self):
        p, T = self.p, self.T
        d = self._dram("dbg_o", [self.NC * self.Ha * DH, T], F32,
                       "ExternalOutput")
        p.dma("gpsimd", d[:, :], self.cco_out.ap(),
              wait=[("cc", self.cc_o)], inc="gA")

    def dump_end(self):
        p = self.p
        p.emit("sync", lambda e: e.nop(),
               wait=[(s, self.p.c(s)) for s in self.p.cnt if self.p.c(s) > 0])

    def finish(self, src_dram):
        p, Tc = self.p, self.Tc
        nt = Tc // 128
        hist = {}
        fst = {}
        fev = 0
        for i in range(nt):
            xb = self.b_x[i % 2]
            qf = f"qf{i % 2}"
            wt = [(s, self.p.c(s)) for s in
                  ("q3s0", "q3s1", "fs0", "fs1", "mo0", "mo1", "mz0", "mz1")]
            wt += [("dve", self.p.c("dve")), ("pe", self.p.c("pe"))] \
                if i == 0 else []
            if i >= 2:
                wt.append(("pe", hist[i - 2]["pe"]))
            xbr = xb.ap().rearrange("p (c j) -> p c j", j=128)
            for grp in range(4):
                p.dma("sync", xbr[:, grp * 4:(grp + 1) * 4, :],
                      AP(src_dram, i * 128 + grp * 4 * 128 * Tc,
                         [[Tc, 128], [128 * Tc, 4], [1, 128]]),
                      wait=wt if grp == 0 else [], inc=qf)
            ld = p.c(qf)
            for g in range(NCH // 4):
                for c4 in range(4):
                    c = g * 4 + c4
                    wtr = [(qf, ld)]
                    if i > 0 or g > 0:
                        wtr.append(("dve", fev))
                    p.trans(self.ps[1][:, c4 * 128:(c4 + 1) * 128],
                            xb[:, c * 128:(c + 1) * 128], self.c_identf[:],
                            wait=wtr if c4 == 0 else [], inc="pe")
                tp = p.c("pe")
                sbuf = self.b_st[g % 2]
                qs = f"q0s{g % 2}"
                wev = [("pe", tp), (qs, fst.get(g % 2, self._bst2.get(g % 2, 0)))]
                for c4 in range(4):
                    p.v("tensor_copy", sbuf[:, c4 * 128:(c4 + 1) * 128],
                        self.ps[1][:, c4 * 128:(c4 + 1) * 128],
                        wait=wev if c4 == 0 else [])
                ev = p.c("dve")
                fev = ev
                for c4 in range(4):
                    c = g * 4 + c4
                    p.dma("sync", self.y_sh[i * 128:(i + 1) * 128,
                                            c * 128:(c + 1) * 128],
                          sbuf[:, c4 * 128:(c4 + 1) * 128],
                          wait=[("dve", ev)] if c4 == 0 else [], inc=qs)
                fst[g % 2] = p.c(qs)
            hist[i] = p.snap()
        p.emit("sync", lambda e: e.nop(),
               wait=[("q0s0", p.c("q0s0")), ("q0s1", p.c("q0s1"))])


def _rope_tables(T, D):
    inv_freq = 1.0 / 10000.0 ** (np.arange(0, D, 2, dtype=np.float32) / D)
    t = np.arange(T, dtype=np.float32)
    freqs = t[:, None] * inv_freq[None, :]
    sin = np.repeat(np.sin(freqs), 2, axis=-1).astype(np.float32)
    cos = np.repeat(np.cos(freqs), 2, axis=-1).astype(np.float32)
    return sin, cos


def _host_constants(NC, T):
    QT = min(512, T)
    sin, cos = _rope_tables(T, DH)
    sincosT = np.ascontiguousarray(
        np.concatenate([sin.T, cos.T], axis=0).astype(np.float32))
    nmr = QT // 128
    r = np.arange(128)[:, None]
    j = np.arange(QT)[None, :]
    mask = np.concatenate(
        [np.where(128 * rr + r > j, np.float32(MASK_NEG), np.float32(0.0))
         for rr in range(nmr)], axis=1).astype(np.float32)
    ident = np.eye(128)
    ptm = np.zeros((128, 128), dtype=np.float32)
    for i in range(64):
        ptm[2 * i, 2 * i + 1] = -1.0
        ptm[2 * i + 1, 2 * i] = 1.0
    s = np.arange(128)[:, None]
    trineg = np.where(s > j[:, 0:128], np.float32(MASK_NEG),
                      np.float32(0.0)).astype(np.float32)
    return {
        "sincosT": sincosT,
        "maskadd": np.ascontiguousarray(mask),
        "ident_bf": ident.astype(ml_dtypes.bfloat16),
        "ident_f32": ident.astype(np.float32),
        "pt_bf": np.ascontiguousarray(ptm.T).astype(ml_dtypes.bfloat16),
        "trineg": trineg,
        "i32sel": np.eye(32, dtype=np.float32),
        "onesrow": np.ones((1, 128), dtype=np.float32),
        "onescol_bf": np.ones((128, 1), dtype=ml_dtypes.bfloat16),
        "ones128": np.ones((128, 128), dtype=np.float32),
    }


def make_in_maps(inputs, NC, T):
    Tc = T // NC
    Ha = N_HEAD_A // NC
    g = {k: np.ascontiguousarray(np.asarray(v)) for k, v in inputs.items()}
    x = g["L0_attn_a_x"].reshape(T, D_MODEL)
    consts = _host_constants(NC, T)
    wqkv = g["L0_attn_a_W_qkv"]
    perm = c_perm_rows()
    inproj = g["L0_attn_b_in_proj_w"]
    inproj_p = inproj.copy()
    inproj_p[3072:5120] = inproj[3072:5120][perm]
    convw = g["L0_attn_b_conv_w"].copy()
    convb = g["L0_attn_b_conv_b"].copy()
    convw[1024:3072] = convw[1024:3072][perm]
    convb[1024:3072] = convb[1024:3072][perm]
    base = {
        "woT": np.ascontiguousarray(g["L0_attn_a_W_o"].T),
        "gateT": np.ascontiguousarray(g["L0_ffn_w_gate"].T),
        "upT": np.ascontiguousarray(g["L0_ffn_w_up"].T),
        "downT": np.ascontiguousarray(g["L0_ffn_w_down"].T),
        "bgateT": np.ascontiguousarray(g["L0_attn_b_gate_w"].T),
        "bupT": np.ascontiguousarray(g["L0_attn_b_up_w"].T),
        "bdownT": np.ascontiguousarray(g["L0_attn_b_down_w"].T),
        "inprojT": np.ascontiguousarray(inproj_p.T),
        "outprojT": np.ascontiguousarray(g["L0_attn_b_out_proj_w"].T),
        "convw": np.ascontiguousarray(convw),
        "convb": np.ascontiguousarray(convb.reshape(CONV_CH, 1)),
        "dtbias": np.ascontiguousarray(
            g["L0_attn_b_dt_bias"].reshape(NH_M, 1)),
        "alog": np.ascontiguousarray(g["L0_attn_b_A_log"].reshape(NH_M, 1)),
        "dmrow": np.ascontiguousarray(g["L0_attn_b_D"].reshape(1, NH_M)),
        "ssmw": np.ascontiguousarray(
            g["L0_attn_b_ssm_norm_w"].reshape(NCH, 128)),
        "inlnw": np.ascontiguousarray(
            g["L0_attn_b_input_ln_w"].reshape(NCH, 128)),
        "postlnw": np.ascontiguousarray(
            g["L0_attn_b_post_ln_w"].reshape(NCH, 128)),
    }
    base.update(consts)
    in_maps = []
    for c in range(NC):
        m = dict(base)
        m["x_sh"] = np.ascontiguousarray(x[c * Tc:(c + 1) * Tc])
        rows = []
        for s in range(3):
            for hh in range(Ha):
                h = c * Ha + hh
                rows.append(wqkv[s * D_MODEL + h * DH:s * D_MODEL + (h + 1) * DH])
        m["wqkvT"] = np.ascontiguousarray(np.concatenate(rows, 0).T)
        pm = np.zeros(NC * 32, dtype=np.float32)
        pm[:c * 32] = 1.0
        m["pmrow"] = np.ascontiguousarray(pm.reshape(1, NC * 32))
        m["haloz"] = np.array([[0.0 if c == 0 else 1.0]], dtype=np.float32)
        m["aoff"] = np.asarray([[cc * 128 * T + c * Tc for cc in range(NCH)]],
                               dtype=np.int32)
        m["halooff"] = np.asarray(
            [[((c - 1) % NC) * CONV_CH * 3 + cc * 128 * 3
              for cc in range(24)]], dtype=np.int32)
        in_maps.append(m)
    return in_maps


def kernel(**inputs):
    NC, T = 8, 8192
    mk = MK(NC=NC, T=T)
    nc = mk.build()
    in_maps = make_in_maps(inputs, NC, T)
    res = run_bass_kernel_spmd(nc, in_maps, core_ids=list(range(NC)))
    out = np.concatenate([np.asarray(r["y_sh"]) for r in res.results], axis=0)
    return out.reshape(1, T, D_MODEL)


# version 07
# speedup vs baseline: 1.0845x; 1.0845x over previous
import numpy as np
import ml_dtypes
import concourse.bass as bass
import concourse.mybir as mybir
from concourse.ap import AP
from concourse.bass_utils import run_bass_kernel_spmd

F32 = mybir.dt.float32
BF16 = mybir.dt.bfloat16
AF = mybir.ActivationFunctionType
ALU = mybir.AluOpType
AX = mybir.AxisListType

D_MODEL = 2048
NCH = 16
D_FF = 8192
NFH = 64
N_HEAD_A = 16
DH = 128
D_INNER = 2048
D_XB = 512
NH_M = 32
HD = 64
DS = 64
DCONV = 4
D_PROJ = 2 * D_INNER + 2 * D_XB + NH_M
CONV_CH = 2 * D_XB + D_INNER
LN_EPS = 1e-5
RMS_EPS_A = 1.1920929e-07
EPS_M = 1e-5
MASK_NEG = -30000.0

ENGINES = ("sync", "tensor", "scalar", "vector", "gpsimd")

_CE = [h for h in range(NH_M) if (h // 4) % 2 == 0]
_CO = [h for h in range(NH_M) if (h // 4) % 2 == 1]


def c_head_slot(h):
    if (h // 4) % 2 == 0:
        return _CE.index(h), 0
    return _CO.index(h), 64


def c_perm_rows():
    perm = np.zeros(D_INNER, dtype=np.int64)
    for h in range(NH_M):
        chunk, base = c_head_slot(h)
        for n in range(DS):
            perm[chunk * 128 + base + n] = h * DS + n
    return perm


class Prog:

    def __init__(self, nc):
        self.nc = nc
        self.q = {e: [] for e in ENGINES}
        self.sems = {}
        self.cnt = {}

    def sem(self, name):
        self.sems[name] = self.nc.alloc_semaphore(name)
        self.cnt[name] = 0
        return name

    def c(self, name):
        return self.cnt[name]

    def snap(self):
        return dict(self.cnt)

    def emit(self, eng, fn, wait=(), inc=None, amt=1):
        tgt = None
        if inc is not None:
            self.cnt[inc] += amt
            tgt = (self.sems[inc], amt)
        wl = tuple((s, v) for (s, v) in wait if v > 0)
        self.q[eng].append((fn, wl, tgt))

    def dma(self, eng, out, in_, wait=(), inc=None, **kw):
        self.emit(eng, lambda e: e.dma_start(out=out, in_=in_, **kw),
                  wait=wait, inc=inc, amt=16)

    def mm(self, out, lhsT, rhs, start, stop, wait=(), inc=None):
        self.emit("tensor",
                  lambda e: e.matmul(out, lhsT, rhs, start=start, stop=stop),
                  wait=wait, inc=inc)

    def trans(self, out, in_, ident, wait=(), inc=None):
        self.emit("tensor", lambda e: e.transpose(out, in_, ident),
                  wait=wait, inc=inc)

    def act(self, out, in_, func, bias=0.0, scale=1.0, accum_out=None,
            wait=(), inc="act"):
        self.emit("scalar",
                  lambda e: e.activation(out, in_, func, bias=bias, scale=scale,
                                         accum_out=accum_out),
                  wait=wait, inc=inc)

    def v(self, meth, *args, wait=(), inc="dve", **kw):
        self.emit("vector", lambda e: getattr(e, meth)(*args, **kw),
                  wait=wait, inc=inc)

    def cchandler(self, kind, ins, outs, rg, wait=()):
        self.emit("gpsimd", lambda e: e.collective_compute(
            kind, ALU.bypass, replica_groups=rg, ins=ins, outs=outs),
            wait=wait, inc="cc")

    def replay(self):
        nc = self.nc
        with nc.Block() as blk:
            for ename in ENGINES:
                ops = self.q[ename]
                if not ops:
                    continue

                def make_body(ops):
                    def body(eng):
                        for fn, waits, inctgt in ops:
                            for s, vl in waits:
                                eng.wait_ge(self.sems[s], vl)
                            inst = fn(eng)
                            if inctgt is not None:
                                inst.then_inc(inctgt[0], inctgt[1])
                    return body

                getattr(blk, ename)(make_body(ops))


def wmax(*waitlists):
    d = {}
    for wl in waitlists:
        for s, v in wl:
            d[s] = max(d.get(s, 0), v)
    return list(d.items())


class MK:
    def __init__(self, NC=8, T=8192, dbg=False):
        self.NC = NC
        self.T = T
        self.Tc = T // NC
        self.TT = min(512, self.Tc)
        self.QT = min(512, T)
        self.Ha = N_HEAD_A // NC
        self.NJ = self.Tc // 128
        self.dbg = dbg
        self.nc = bass.Bass("TRN2", target_bir_lowering=False)
        self.p = Prog(self.nc)
        for s in ("pe", "act", "dve", "gp", "cc",
                  "qc", "q0x0", "q0x1", "q0w", "q0b0", "q0b1", "q0s0", "q0s1",
                  "gw", "q1x0", "q1x1", "q1s0", "q1s1", "q2m", "q2o",
                  "gA", "gw30", "gw31", "q3r0", "q3r1", "q3s0", "q3s1",
                  "qf0", "qf1", "flx", "fw0", "fw1", "fd0", "fd1",
                  "fra0", "fra1", "frb0", "frb1", "fs0", "fs1",
                  "ml0", "ml1", "mz0", "mz1", "mh", "mhi", "msum",
                  "msr0", "msr1", "mza0", "mza1", "mo0", "mo1"):
            self.p.sem(s)
        self._declare_dram()
        self._declare_sbuf()
        self._declare_psum()
        self.greg = self.nc.gpsimd.alloc_register("goff")

    def _dram(self, name, shape, dtype, kind="Internal"):
        return self.nc.dram_tensor(name, list(shape), dtype, kind=kind)

    def _declare_dram(self):
        Tc, T = self.Tc, self.T
        I, O = "ExternalInput", "ExternalOutput"
        dbgk = O if self.dbg else "Internal"
        self.x_sh = self._dram("x_sh", [Tc, D_MODEL], F32, I)
        self.wqkvT = self._dram("wqkvT", [D_MODEL, self.Ha * 3 * DH], F32, I)
        self.woT = self._dram("woT", [D_MODEL, D_MODEL], F32, I)
        self.sincosT = self._dram("sincosT", [256, T], F32, I)
        nmr = self.QT // 128
        self.maskadd = self._dram("maskadd", [128, nmr * self.QT], F32, I)
        self.ident_bf = self._dram("ident_bf", [128, 128], BF16, I)
        self.ident_f32 = self._dram("ident_f32", [128, 128], F32, I)
        self.pt_bf = self._dram("pt_bf", [128, 128], BF16, I)
        self.trineg = self._dram("trineg", [128, 128], F32, I)
        self.i32sel = self._dram("i32sel", [32, 32], F32, I)
        self.onesrow = self._dram("onesrow", [1, 128], F32, I)
        self.onescol_bf = self._dram("onescol_bf", [128, 1], BF16, I)
        self.ones128 = self._dram("ones128", [128, 128], F32, I)
        self.gateT = self._dram("gateT", [D_MODEL, D_FF], F32, I)
        self.upT = self._dram("upT", [D_MODEL, D_FF], F32, I)
        self.downT = self._dram("downT", [D_FF, D_MODEL], F32, I)
        self.inprojT = self._dram("inprojT", [D_MODEL, D_PROJ], F32, I)
        self.outprojT = self._dram("outprojT", [D_INNER, D_MODEL], F32, I)
        self.bgateT = self._dram("bgateT", [D_MODEL, D_FF], F32, I)
        self.bupT = self._dram("bupT", [D_MODEL, D_FF], F32, I)
        self.bdownT = self._dram("bdownT", [D_FF, D_MODEL], F32, I)
        self.convw = self._dram("convw", [CONV_CH, DCONV], F32, I)
        self.convb = self._dram("convb", [CONV_CH, 1], F32, I)
        self.dtbias = self._dram("dtbias", [NH_M, 1], F32, I)
        self.alog = self._dram("alog", [NH_M, 1], F32, I)
        self.dmrow = self._dram("dmrow", [1, NH_M], F32, I)
        self.ssmw = self._dram("ssmw", [NCH, 128], F32, I)
        self.inlnw = self._dram("inlnw", [NCH, 128], F32, I)
        self.postlnw = self._dram("postlnw", [NCH, 128], F32, I)
        self.pmrow = self._dram("pmrow", [1, self.NC * 32], F32, I)
        self.haloz = self._dram("haloz", [1, 1], F32, I)
        self.halooff = self._dram("halooff", [1, 24], mybir.dt.int32, I)
        self.y_sh = self._dram("y_sh", [Tc, D_MODEL], F32, O)
        self.r0d = self._dram("r0d", [D_MODEL, Tc], F32)
        self.r1d = self._dram("r1d", [D_MODEL, Tc], F32, dbgk)
        self.r2d = self._dram("r2d", [D_MODEL, Tc], F32, dbgk)
        self.z1d = self._dram("z1d", [D_MODEL, Tc], F32, dbgk)
        self.zsd = self._dram("zsd", [D_INNER, Tc], BF16)
        self.inner2d = self._dram("inner2d", [D_MODEL, Tc], F32, dbgk)
        self.r4d = self._dram("r4d", [D_MODEL, Tc], F32, dbgk)
        self.ccl_in = self._dram("ccl_in", [D_MODEL, Tc], BF16)
        self.ccl_out = self._dram("ccl_out", [self.NC * D_MODEL, Tc], BF16)
        self.cco_in = self._dram("cco_in", [self.Ha * DH, T], BF16)
        self.cco_out = self._dram("cco_out", [self.NC * self.Ha * DH, T], BF16)
        self.aoff = self._dram("aoff", [1, NCH], mybir.dt.int32, I)
        self.cch_in = self._dram("cch_in", [CONV_CH, 3], BF16)
        self.cch_out = self._dram("cch_out", [self.NC * CONV_CH, 3], BF16)
        self.ccs_in = self._dram("ccs_in", [DS + 1, NH_M * HD], F32)
        self.ccs_out = self._dram("ccs_out", [self.NC * (DS + 1), NH_M * HD], F32)

    def _declare_sbuf(self):
        nc, Tc, T, TT = self.nc, self.Tc, self.T, self.TT
        sb = lambda n, s, d: nc.alloc_sbuf_tensor(n, list(s), d)
        self.c_identbf = sb("c_identbf", [128, 128], BF16)
        self.c_identf = sb("c_identf", [128, 128], F32)
        self.c_ptbf = sb("c_ptbf", [128, 128], BF16)
        self.c_trineg = sb("c_trineg", [128, 128], F32)
        self.c_i32sel = sb("c_i32sel", [32, 32], F32)
        self.c_onesrow = sb("c_onesrow", [1, 128], F32)
        self.c_onescol = sb("c_onescol", [128, 1], BF16)
        self.c_ones128 = sb("c_ones128", [128, 128], F32)
        self.c_ones32 = sb("c_ones32", [32, 128], F32)
        self.c_convw = sb("c_convw", [128, 24 * DCONV], F32)
        self.c_convb = sb("c_convb", [128, 24], F32)
        self.c_dtb = sb("c_dtb", [NH_M, 1], F32)
        self.c_nege = sb("c_nege", [NH_M, 1], F32)
        self.c_dm64 = sb("c_dm64", [128, NH_M], F32)
        self.c_hz128 = sb("c_hz128", [128, 1], F32)
        self.c_ssmw = sb("c_ssmw", [128, NCH], F32)
        self.c_inlnw = sb("c_inlnw", [128, NCH], F32)
        self.c_postlnw = sb("c_postlnw", [128, NCH], F32)
        self.c_prow = sb("c_prow", [1, self.NC * 32], F32)
        self.c_pmrow = sb("c_pmrow", [1, self.NC * 32], F32)
        self.c_haloz = sb("c_haloz", [1, 1], F32)
        self.c_lneps = sb("c_lneps", [128, 1], F32)
        self.c_rmseps = sb("c_rmseps", [128, 1], F32)
        self.c_one = sb("c_one", [128, 1], F32)
        self.c_qsc = sb("c_qsc", [128, 1], F32)
        self.c_halooff = sb("c_halooff", [1, 24], mybir.dt.int32)
        self.c_aoff = sb("c_aoff", [1, NCH], mybir.dt.int32)

        self.arena_size = 200 * 1024
        self.arena = sb("arena", [128, self.arena_size], mybir.dt.uint8)
        self.arena0 = nc.lookup_mloc(self.arena).addr
        assert self.arena0 + self.arena_size <= 229344 - 64, self.arena0

        def A(name, shape, dtype, off):
            per = int(np.prod(shape[1:])) * mybir.dt.size(dtype)
            assert off + per <= self.arena_size, (name, off, per)
            return nc.alloc_sbuf_tensor_at(name, list(shape), dtype,
                                           offset=self.arena0 + off)

        HI = 174 * 1024
        self.p0_x = [A(f"p0_x{i}", [128, D_MODEL], F32, i * 8192) for i in (0, 1)]
        self.p0_ln = A("p0_ln", [128, D_MODEL], BF16, 16384)
        self.p0_sq = A("p0_sq", [128, D_MODEL], BF16, 20480)
        self.p0_fm = A("p0_fm", [128, NCH * Tc], BF16, 24576)
        self.p0_st = A("p0_st", [128, 16], F32, 24576 + NCH * Tc * 2)
        self.b_x = [A(f"b_x{i}", [128, D_MODEL], F32, HI + i * 8192)
                    for i in (0, 1)]
        self.b_st = [A(f"b_st{i}", [128, 512], F32, HI + 16384 + i * 2048)
                     for i in (0, 1)]
        hTb = self.Ha * T * 2
        self.q_reg = A("q_reg", [128, self.Ha * T], BF16, 0)
        self.k_reg = A("k_reg", [128, self.Ha * T], BF16, hTb)
        self.v_reg = A("v_reg", [128, self.Ha * T], BF16, 2 * hTb)
        o1 = 3 * hTb
        ow = self.Ha * 3 * DH
        self.wqkv_sb = A("wqkv_sb", [128, NCH * ow], BF16, o1)
        o2 = o1 + NCH * ow * 2
        self.xg = [A(f"xg{i}", [128, NCH * TT], BF16, o2 + i * NCH * TT * 2)
                   for i in (0, 1)]
        o3 = o2 + 2 * NCH * TT * 2
        self.sc_sin = [A(f"sc_sin{i}", [128, TT], F32, o3 + i * TT * 4)
                       for i in (0, 1)]
        self.sc_cos = [A(f"sc_cos{i}", [128, TT], F32, o3 + (2 + i) * TT * 4)
                       for i in (0, 1)]
        o4 = o3 + 4 * TT * 4
        self.p1_q2 = [A(f"p1_q2{i}", [128, TT], BF16, o4 + i * TT * 2)
                      for i in (0, 1)]
        self.p1_qraw = [A(f"p1_qraw{i}", [128, TT], BF16, o4 + (2 + i) * TT * 2)
                        for i in (0, 1)]
        o5 = o4 + 4 * TT * 2
        self.p1_ssrow = A("p1_ssrow", [1, TT], F32, o5)
        self.p1_t1 = A("p1_t1", [128, TT], F32, o5 + TT * 4)
        self.p1_t2 = A("p1_t2", [128, TT], F32, o5 + 2 * TT * 4)
        self.p1_rstd = A("p1_rstd", [128, TT], F32, o5 + 3 * TT * 4)
        self.p1_tmp = A("p1_tmp", [128, TT], F32, o5 + 4 * TT * 4)
        p1_end = o5 + 5 * TT * 4
        assert p1_end <= HI, p1_end
        self.o_reg = A("o_reg", [128, self.Ha * T], BF16, o1)
        oo = o1 + hTb
        self.p2_exp = [A(f"p2_exp{i}", [128, 512], BF16, oo + i * 1024)
                       for i in range(6)]
        self.p2_acc = [A(f"p2_acc{i}", [128, 512], F32, oo + 6144 + i * 2048)
                       for i in (0, 1)]
        self.p2_tmp = A("p2_tmp", [128, self.QT], F32, oo + 10240)
        self.p2_recip = [A(f"p2_recip{i}", [128, self.QT], F32,
                           oo + 12288 + i * 2048) for i in (0, 1)]
        self.p2_mask = A("p2_mask", [128, (self.QT // 128) * self.QT], F32,
                         oo + 16384)
        p2_end = oo + 16384 + (self.QT // 128) * self.QT * 4
        assert p2_end <= HI, p2_end
        self.p3_a = A("p3_a", [128, NCH * Tc], BF16, 0)
        self.p3_wo = [A(f"p3_wo{i}", [128, NCH * 128], BF16, 32768 + i * 4096)
                      for i in (0, 1)]
        self.p3_r = [A(f"p3_r{i}", [128, TT], F32, 40960 + i * TT * 4)
                     for i in (0, 1)]
        self.p3_st = [A(f"p3_st{i}", [128, TT], F32, 40960 + (2 + i) * TT * 4)
                      for i in (0, 1)]
        self.f_xb = A("f_xb", [128, NCH * Tc], BF16, 0)
        self.f_hid = A("f_hid", [128, NFH * Tc], BF16, 32768)
        _Tc3 = Tc + 3
        _r1 = max(24 * _Tc3 * 2, NCH * Tc * 2 + 2 * NH_M * HD * 4)
        _mend = (32768 + (_r1 + 63) // 64 * 64 + 24 * Tc * 2 + Tc * 4 +
                 2 * self.NJ * 256 + max(2 * self.NJ * 1024, Tc * 8) +
                 NH_M * HD * 4)
        fo = max(32768 + NFH * Tc * 2, _mend)
        self.f_x2 = [A(f"f_x2{i}", [128, TT], BF16, fo + i * TT * 2)
                     for i in (0, 1)]
        lo = fo + 2 * TT * 2
        self.f_rows = A("f_rows", [1, 8 * TT], F32, lo)
        lo2 = lo + 8 * TT * 4
        self.f_mub = [A(f"f_mub{i}", [128, TT], F32, lo2 + i * TT * 4)
                      for i in (0, 1)]
        self.f_rb = [A(f"f_rb{i}", [128, TT], F32, lo2 + (2 + i) * TT * 4)
                     for i in (0, 1)]
        self.f_scr = [A(f"f_scr{i}", [128, TT], F32, lo2 + (4 + i) * TT * 4)
                      for i in (0, 1)]
        ln_end = lo2 + 6 * TT * 4
        assert ln_end <= self.arena_size
        self.f_w = [A(f"f_w{i}", [128, NCH * 128], BF16, fo + i * 4096)
                    for i in (0, 1, 2, 3)]
        self.f_sg = [A(f"f_sg{i}", [128, Tc], BF16, fo + 16384 + i * Tc * 2)
                     for i in (0, 1)]
        gu_end = fo + 16384 + 2 * Tc * 2
        assert gu_end <= self.arena_size, gu_end
        self.f_wd = [A(f"f_wd{i}", [128, NFH * 128], BF16, i * 16384)
                     for i in (0, 1)]
        self.f_res = [A(f"f_res{i}", [128, TT], F32, fo + i * TT * 4)
                      for i in (0, 1)]
        self.f_res2 = [A(f"f_res2{i}", [128, TT], F32, fo + (2 + i) * TT * 4)
                       for i in (0, 1)]
        self.f_st = [A(f"f_st{i}", [128, TT], F32, fo + (4 + i) * TT * 4)
                     for i in (0, 1)]
        self.f_ost = [A(f"f_ost{i}", [128, 512], F32, fo + (6 + i) * TT * 4)
                      for i in (0, 1)]
        Tc3 = Tc + 3
        MB = 32768
        srb = NH_M * HD * 4
        r1 = max(24 * Tc3 * 2, NCH * Tc * 2 + 2 * srb)
        self.m_xbc = A("m_xbc", [128, 24 * Tc3], BF16, MB)
        self.m_y = A("m_y", [128, NCH * Tc], BF16, MB)
        self.m_sr = [A(f"m_sr{i}", [64, NH_M * HD], F32,
                       MB + NCH * Tc * 2 + i * srb) for i in (0, 1)]
        mo = MB + (r1 + 63) // 64 * 64
        self.m_xc = A("m_xc", [128, 24 * Tc], BF16, mo)
        mo2 = mo + 24 * Tc * 2
        self.m_ld = A("m_ld", [64, Tc], F32, mo2)
        mo3 = mo2 + Tc * 4
        self.m_ldt = A("m_ldt", [128, self.NJ * 64], F32, mo3)
        self.m_nldt = A("m_nldt", [128, self.NJ * 64], F32, mo3 + self.NJ * 256)
        mo4 = mo3 + 2 * self.NJ * 256
        self.m_dtraw = A("m_dtraw", [32, Tc], F32, mo4)
        self.m_zs = [A(f"m_zs{i}", [128, Tc], BF16, mo4 + Tc * 4 + i * Tc * 2)
                     for i in (0, 1)]
        self.m_xt = A("m_xt", [128, self.NJ * 512], BF16, mo4)
        self.m_bt = A("m_bt", [128, self.NJ * 512], BF16, mo4 + self.NJ * 1024)
        mo5 = mo4 + max(2 * self.NJ * 1024, Tc * 8)
        self.m_hst = A("m_hst", [64, NH_M * HD], F32, mo5)
        m_end = mo5 + NH_M * HD * 4
        fo_b = max(32768 + NFH * Tc * 2, m_end)
        assert m_end <= fo_b, (m_end, fo_b)
        self.m_hb = A("m_hb", [64, NH_M * HD], BF16, fo_b)
        t0 = fo_b + 8192
        NMB = 4
        self.m_esb = [A(f"m_esb{i}", [128, 128], F32, t0 + i * 512)
                      for i in range(NMB)]
        self.m_arg = [A(f"m_arg{i}", [128, 128], F32, t0 + 2048 + i * 512)
                      for i in range(NMB)]
        self.m_dtile = [A(f"m_dtile{i}", [128, 128], F32, t0 + 4096 + i * 512)
                        for i in range(NMB)]
        self.m_bw = [A(f"m_bw{i}", [128, 64], BF16, t0 + 6144 + i * 128)
                     for i in range(NMB)]
        self.m_cs = [A(f"m_cs{i}", [64, 128], BF16, t0 + 6656 + i * 256)
                     for i in range(NMB)]
        self.m_wt = [A(f"m_wt{i}", [128, 128], BF16, t0 + 7680 + i * 256)
                     for i in range(NMB)]
        t1 = t0 + 8704
        self.m_lacc = A("m_lacc", [32, 1], F32, t1)
        self.m_pcore = A("m_pcore", [32, 1], F32, t1 + 64)
        self.m_pb = A("m_pb", [64, self.NC * 32], F32, t1 + 128)
        self.m_wrow = A("m_wrow", [1, self.NC * 32], F32,
                        t1 + 128 + self.NC * 128)
        self.m_grow = A("m_grow", [1, self.NC * 32], F32,
                        t1 + 128 + 2 * self.NC * 128)
        assert t1 + 128 + 3 * self.NC * 128 <= fo_b + 16384 + 2 * Tc * 2
        self.m_cscr = [A(f"m_cscr{i}", [128, Tc], F32, fo_b + i * 2 * TT * 4)
                       for i in (0, 1)]
        self.m_x2b = [A(f"m_x2b{i}", [128, TT], BF16,
                        fo_b + (6 + i) * TT * 4) for i in (0, 1)]

    def _declare_psum(self):
        nc = self.nc
        self.ps = [nc.alloc_psum_tensor(f"ps{i}", [128, 1024], F32)
                   for i in range(3)]
        self.ps3b = nc.alloc_psum_tensor("ps3b", [128, 512], F32)
        self.pbf = nc.alloc_psum_tensor("pbf", [128, 1024], BF16)

    def const_load(self):
        p = self.p
        pairs = [(self.c_identbf, self.ident_bf), (self.c_identf, self.ident_f32),
                 (self.c_ptbf, self.pt_bf), (self.c_trineg, self.trineg),
                 (self.c_i32sel, self.i32sel), (self.c_onesrow, self.onesrow),
                 (self.c_onescol, self.onescol_bf), (self.c_dtb, self.dtbias),
                 (self.c_ones128, self.ones128),
                 (self.c_pmrow, self.pmrow),
                 (self.c_haloz, self.haloz), (self.c_halooff, self.halooff),
                 (self.c_aoff, self.aoff)]
        for sbuf, dr in pairs:
            h = dr.shape
            p.dma("sync", sbuf[:], dr[0:h[0], 0:h[1]], inc="qc")
        cw = AP(self.convw, 0, [[DCONV, 128], [128 * DCONV, 24], [1, DCONV]])
        p.dma("sync", self.c_convw.ap().rearrange("p (c k) -> p c k", k=DCONV),
              cw, inc="qc")
        cb = AP(self.convb, 0, [[1, 128], [128, 24]])
        p.dma("sync", self.c_convb[:], cb, inc="qc",
              allow_slow_non_contiguous=True)
        for sbuf, dr in ((self.c_ssmw, self.ssmw), (self.c_inlnw, self.inlnw),
                         (self.c_postlnw, self.postlnw)):
            p.dma("sync", sbuf[:], AP(dr, 0, [[1, 128], [128, NCH]]),
                  inc="qc", allow_slow_non_contiguous=True)
        p.dma("sync", self.c_nege[:], self.alog[0:NH_M, 0:1], inc="qc")
        p.dma("sync", self.c_prow[0:1, 0:NH_M], self.dmrow[0:1, 0:NH_M],
              inc="qc")
        w0 = p.c("qc")
        p.act(self.c_nege[:], self.c_nege[:], AF.Exp, wait=[("qc", w0)])
        p.v("tensor_scalar_mul", self.c_nege[:], self.c_nege[:], -1.0,
            wait=[("act", p.c("act"))])
        p.v("memset", self.c_ones32[:], 1.0)
        p.v("memset", self.c_lneps[:], LN_EPS)
        p.v("memset", self.c_rmseps[:], RMS_EPS_A)
        p.v("memset", self.c_one[:], 1.0)
        p.v("memset", self.c_qsc[:], -0.5 * float(np.log(128.0)))
        p.mm(self.ps3b[0:128, 0:NH_M], self.c_onesrow[0:1, 0:128],
             self.c_prow[0:1, 0:NH_M], True, True,
             wait=[("qc", w0)], inc="pe")
        p.v("tensor_copy", self.c_dm64[:], self.ps3b[0:128, 0:NH_M],
            wait=[("pe", p.c("pe"))])
        p.mm(self.ps3b[0:128, 64:65], self.c_onesrow[0:1, 0:128],
             self.c_haloz[0:1, 0:1], True, True,
             wait=[("qc", w0)], inc="pe")
        p.v("tensor_copy", self.c_hz128[:], self.ps3b[0:128, 64:65],
            wait=[("pe", p.c("pe"))])
        self.const_w = [("qc", p.c("qc")), ("dve", p.c("dve"))]

    def phase0(self):
        p, Tc = self.p, self.Tc
        nt = Tc // 128
        hist = []
        for i in range(nt):
            xb = self.p0_x[i % 2]
            qx = f"q0x{i % 2}"
            wt = list(self.const_w)
            if i >= 2:
                h2 = hist[i - 2]
                wt += [("pe", h2["pe"]), ("act", h2["act"]), ("dve", h2["dve"])]
            p.dma("sync", xb[:], self.x_sh[i * 128:(i + 1) * 128, :],
                  wait=wt, inc=qx)
            ld = p.c(qx)
            st = self.p0_st
            p.v("reduce_sum", st[:, 0:1], xb[:], AX.X,
                wait=[(qx, ld), ("dve", p.c("dve"))])
            p.act(self.p0_sq[:], xb[:], AF.Square, accum_out=st[:, 1:2],
                  wait=[(qx, ld), ("dve", p.c("dve")), ("act", p.c("act"))])
            a_sq = p.c("act")
            p.v("tensor_scalar_mul", st[:, 2:3], st[:, 0:1], 1.0 / D_MODEL,
                wait=[("dve", p.c("dve"))])
            p.v("tensor_mul", st[:, 3:4], st[:, 2:3], st[:, 2:3],
                wait=[("dve", p.c("dve"))])
            p.v("scalar_tensor_tensor", st[:, 4:5], st[:, 1:2], 1.0 / D_MODEL,
                st[:, 3:4], ALU.mult, ALU.subtract,
                wait=[("act", a_sq), ("dve", p.c("dve"))])
            p.act(st[:, 5:6], st[:, 4:5], AF.Sqrt, bias=self.c_lneps[:, 0:1],
                  wait=[("dve", p.c("dve"))])
            p.v("reciprocal", st[:, 6:7], st[:, 5:6], wait=[("act", p.c("act"))])
            p.v("scalar_tensor_tensor", st[:, 7:8], st[:, 2:3], -1.0,
                st[:, 6:7], ALU.mult, ALU.mult, wait=[("dve", p.c("dve"))])
            st_done = p.c("dve")
            p.v("tensor_scalar", self.p0_ln[:], xb[:], st[:, 6:7],
                st[:, 7:8], ALU.mult, ALU.add, wait=[("dve", st_done)])
            lnd = p.c("dve")
            for g in range(2):
                for c8 in range(8):
                    c = g * 8 + c8
                    wtr = [("dve", lnd)]
                    if i > 0 or g > 0:
                        wtr.append(("dve", self._p0ev))
                    p.trans(self.pbf[:, c8 * 128:(c8 + 1) * 128],
                            self.p0_ln[:, c * 128:(c + 1) * 128],
                            self.c_identbf[:], wait=wtr if c8 == 0 else [],
                            inc="pe")
                tp = p.c("pe")
                for c8 in range(8):
                    c = g * 8 + c8
                    p.v("tensor_copy",
                        self.p0_fm[:, c * Tc + i * 128:c * Tc + (i + 1) * 128],
                        self.pbf[:, c8 * 128:(c8 + 1) * 128],
                        wait=[("pe", tp)] if c8 == 0 else [])
                self._p0ev = p.c("dve")
            hist.append(p.snap())
        evd = p.c("dve")
        for c in range(NCH):
            p.dma("sync", self.ccl_in[c * 128:(c + 1) * 128, :],
                  self.p0_fm[:, c * Tc:(c + 1) * Tc],
                  wait=[("dve", evd)] if c == 0 else [], inc="q0w")
        wrote = p.c("q0w")
        p.cchandler("AllGather", [self.ccl_in.ap().opt()],
                    [self.ccl_out.ap().opt()], [list(range(self.NC))],
                    wait=[("q0w", wrote)])
        self.cc_ln = p.c("cc")
        bhist = []
        for i in range(nt):
            xb = self.b_x[i % 2]
            qx = f"q0b{i % 2}"
            wt = [("pe", hist[-1]["pe"]), ("act", hist[-1]["act"])]
            if i >= 2:
                wt.append(("pe", bhist[i - 2]["pe"]))
            p.dma("sync", xb[:], self.x_sh[i * 128:(i + 1) * 128, :],
                  wait=wt, inc=qx)
            ld = p.c(qx)
            for g in range(NCH // 4):
                for c4 in range(4):
                    c = g * 4 + c4
                    wtr = [(qx, ld)]
                    if i > 0 or g > 0:
                        wtr.append(("dve", self._bev))
                    p.trans(self.ps[2][:, 512 + c4 * 128:512 + (c4 + 1) * 128],
                            xb[:, c * 128:(c + 1) * 128], self.c_identf[:],
                            wait=wtr if c4 == 0 else [], inc="pe")
                tp = p.c("pe")
                sbuf = self.b_st[g % 2]
                qs = f"q0s{g % 2}"
                wev = [("pe", tp), (qs, self._bst2.get(g % 2, 0))]
                for c4 in range(4):
                    p.v("tensor_copy", sbuf[:, c4 * 128:(c4 + 1) * 128],
                        self.ps[2][:, 512 + c4 * 128:512 + (c4 + 1) * 128],
                        wait=wev if c4 == 0 else [])
                ev = p.c("dve")
                self._bev = ev
                for c4 in range(4):
                    c = g * 4 + c4
                    p.dma("sync", self.r0d[c * 128:(c + 1) * 128,
                                           i * 128:(i + 1) * 128],
                          sbuf[:, c4 * 128:(c4 + 1) * 128],
                          wait=[("dve", ev)] if c4 == 0 else [], inc=qs)
                self._bst2[g % 2] = p.c(qs)
            bhist.append(p.snap())
        self.r0_done = [("q0s0", p.c("q0s0")), ("q0s1", p.c("q0s1"))]

    def phase1(self):
        p, T, Tc, TT, Ha = self.p, self.T, self.Tc, self.TT, self.Ha
        ow = Ha * 3 * DH
        wdst = self.wqkv_sb.ap().rearrange("p (c o) -> p c o", o=ow)
        for grp in range(4):
            p.dma("gpsimd", wdst[:, grp * 4:(grp + 1) * 4, :],
                  AP(self.wqkvT, grp * 4 * 128 * ow,
                     [[ow, 128], [128 * ow, 4], [1, ow]]), inc="gw")
        wq_ld = p.c("gw")
        units = [(s, tt) for s in range(self.NC) for tt in range(Tc // TT)]
        qk_ps = [self.ps[0][:, 0:TT], self.ps[0][:, 512:512 + TT]]
        rot_ps = self.ps[2][:, 0:TT]
        ss_ps = self.ps3b[0:1, 0:TT]
        bc_ps = self.ps3b[:, 0:TT]
        uhist = []
        shist = []
        for u, (s, tt) in enumerate(units):
            xgb = self.xg[u % 2]
            qx = f"q1x{u % 2}"
            qs = f"q1s{u % 2}"
            g0 = s * Tc + tt * TT
            wt = [("cc", self.cc_ln)]
            if u >= 2:
                wt.append(("pe", uhist[u - 2]["pe"]))
            for c in range(NCH):
                p.dma("sync", xgb[:, c * TT:(c + 1) * TT],
                      self.ccl_out[s * D_MODEL + c * 128:
                                   s * D_MODEL + (c + 1) * 128,
                                   tt * TT:(tt + 1) * TT],
                      wait=wt if c == 0 else [], inc=qx)
            wsc = [("dve", uhist[u - 2]["dve"])] if u >= 2 else []
            p.dma("sync", self.sc_sin[u % 2][:], self.sincosT[0:128, g0:g0 + TT],
                  wait=wsc, inc=qs)
            p.dma("sync", self.sc_cos[u % 2][:], self.sincosT[128:256, g0:g0 + TT],
                  inc=qs)
            ld, lds = p.c(qx), p.c(qs)
            scs, scc = self.sc_sin[u % 2], self.sc_cos[u % 2]
            for slot in range(2 * Ha):
                qk, hh = divmod(slot, Ha)
                psb = qk_ps[slot % 2]
                wmm = [(qx, ld), ("gw", wq_ld)]
                gsl = u * 2 * Ha + slot
                if gsl >= 2:
                    h2 = shist[gsl - 2]
                    wmm += [("dve", h2["dve"]), ("act", h2["act"])]
                for c in range(NCH):
                    p.mm(psb, self.wqkv_sb[:, c * ow + slot * DH:
                                           c * ow + (slot + 1) * DH],
                         xgb[:, c * TT:(c + 1) * TT], c == 0, c == NCH - 1,
                         wait=wmm if c == 0 else [], inc="pe")
                mmd = p.c("pe")
                sq, qraw = self.p1_q2[slot % 2], self.p1_qraw[slot % 2]
                p.act(sq[:], psb, AF.Square, wait=[("pe", mmd)])
                p.v("tensor_copy", qraw[:], psb, wait=[("act", p.c("act"))])
                cpd = p.c("dve")
                p.mm(ss_ps, self.c_onescol[:, 0:1], sq[:], True, True,
                     wait=[("act", p.c("act")), ("dve", cpd),
                           ("act", self._p1_prev_rstd)], inc="pe")
                p.mm(rot_ps, self.c_ptbf[:], qraw[:], True, True,
                     wait=[("dve", cpd)], inc="pe")
                mm2 = p.c("pe")
                p.v("tensor_copy", self.p1_ssrow[:], ss_ps, wait=[("pe", mm2)])
                p.mm(bc_ps, self.c_onesrow[:], self.p1_ssrow[:], True, True,
                     wait=[("dve", p.c("dve"))], inc="pe")
                bcd = p.c("pe")
                p.act(self.p1_tmp[:], bc_ps, AF.Ln, bias=self.c_rmseps[:, 0:1],
                      scale=1.0 / DH, wait=[("pe", bcd), ("dve", p.c("dve"))])
                p.act(self.p1_rstd[:], self.p1_tmp[:], AF.Exp, scale=-0.5,
                      bias=(self.c_qsc[:, 0:1] if qk == 0 else 0.0),
                      wait=[("act", p.c("act"))])
                rstd_done = p.c("act")
                self._p1_prev_rstd = rstd_done
                p.v("tensor_mul", self.p1_t1[:], qraw[:], scc[:],
                    wait=[("dve", cpd), (qs, lds)])
                p.v("tensor_mul", self.p1_t2[:], rot_ps, scs[:],
                    wait=[("pe", mm2), ("dve", p.c("dve"))])
                p.v("tensor_add", self.p1_t1[:], self.p1_t1[:], self.p1_t2[:],
                    wait=[("dve", p.c("dve"))])
                dst = self.q_reg if qk == 0 else self.k_reg
                p.v("tensor_mul", dst[:, hh * T + g0:hh * T + g0 + TT],
                    self.p1_t1[:], self.p1_rstd[:],
                    wait=[("dve", p.c("dve")), ("act", rstd_done)])
                shist.append(p.snap())
            for vv in range(TT // 128):
                wmm = [(qx, ld), ("gw", wq_ld)]
                if self._v_hist:
                    wmm.append(("dve", self._v_hist[-1]))
                nsl = (Ha * DH + 511) // 512
                for c in range(NCH):
                    for ns in range(nsl):
                        n0, n1 = ns * 512, min((ns + 1) * 512, Ha * DH)
                        p.mm(self.ps[1][:, n0:n1],
                             xgb[:, c * TT + vv * 128:c * TT + vv * 128 + 128],
                             self.wqkv_sb[:, c * ow + 2 * Ha * DH + n0:
                                          c * ow + 2 * Ha * DH + n1],
                             c == 0, c == NCH - 1,
                             wait=wmm if (c == 0 and ns == 0) else [], inc="pe")
                mmd = p.c("pe")
                gtc = (g0 + vv * 128) // 128
                for hh in range(Ha):
                    p.v("tensor_copy",
                        self.v_reg[:, hh * T + gtc * 128:hh * T + (gtc + 1) * 128],
                        self.ps[1][:, hh * DH:(hh + 1) * DH],
                        wait=[("pe", mmd)] if hh == 0 else [])
                self._v_hist.append(p.c("dve"))
            uhist.append(p.snap())
        self.p1_done = p.snap()

    def phase2(self):
        p, T, QT, Ha, Tc = self.p, self.T, self.QT, self.Ha, self.Tc
        nmr = QT // 128
        D = 4
        NS = 4
        NE = 6
        p.dma("sync", self.p2_mask[:], self.maskadd[0:128, 0:nmr * QT],
              wait=[("pe", self.p1_done["pe"]), ("dve", self.p1_done["dve"])],
              inc="q2m")
        mload = p.c("q2m")
        s_ps = [self.ps[0][:, 0:512], self.ps[0][:, 512:1024],
                self.ps[1][:, 0:512], self.ps[1][:, 512:1024]]
        av_ps = [self.ps[2][:, 0:QT], self.ps[2][:, 512:512 + QT]]
        bc = self.ps3b[:, 0:QT]
        rdy = [("dve", self.p1_done["dve"]), ("q2m", mload)] + self.r0_done
        nq = T // QT
        blocks = []
        for hh in range(Ha):
            for qt in range(nq):
                qi = hh * nq + qt
                nkc = (qt + 1) * QT // 128
                for kc in range(nkc):
                    blocks.append((qi, hh, qt, kc, kc == 0, kc == nkc - 1))
        NB = len(blocks)
        exp_act = {}
        av_pe = {}
        acc_dve = {}
        bcast_pe = {}
        ln_act = {}
        expr_act = {}
        omul_dve = {}

        def emit_back(gb):
            qi, hh, qt, kc, first, last = blocks[gb]
            eb = self.p2_exp[gb % NE]
            wav = [("act", exp_act[gb])]
            if first:
                wav.append(("dve", omul_dve.get(qi - 2, 0)))
            p.mm(av_ps[qi % 2],
                 self.v_reg[:, hh * T + kc * 128:hh * T + (kc + 1) * 128],
                 eb[:], first, last, wait=wav, inc="pe")
            av_pe[gb] = p.c("pe")
            acc = self.p2_acc[qi % 2]
            if first:
                p.v("tensor_copy", acc[:], eb[:],
                    wait=[("act", exp_act[gb]),
                          ("pe", bcast_pe.get(qi - 2, 0)),
                          ("dve", p.c("dve"))])
            else:
                p.v("tensor_add", acc[:], acc[:], eb[:],
                    wait=[("act", exp_act[gb]), ("dve", p.c("dve"))])
            acc_dve[gb] = p.c("dve")
            if last:
                p.mm(bc, self.c_ones128[:], acc[:], True, True,
                     wait=[("dve", acc_dve[gb]),
                           ("act", ln_act.get(qi - 1, 0))], inc="pe")
                bcast_pe[qi] = p.c("pe")
                p.act(self.p2_tmp[:], bc, AF.Ln, wait=[("pe", bcast_pe[qi])])
                ln_act[qi] = p.c("act")
                p.act(self.p2_recip[qi % 2][:], self.p2_tmp[:], AF.Exp,
                      scale=-1.0, wait=[("dve", omul_dve.get(qi - 2, 0))])
                expr_act[qi] = p.c("act")
                p.v("tensor_mul",
                    self.o_reg[:, hh * T + qt * QT:hh * T + (qt + 1) * QT],
                    av_ps[qi % 2], self.p2_recip[qi % 2][:],
                    wait=[("act", expr_act[qi]), ("pe", av_pe[gb]),
                          ("dve", p.c("dve"))])
                omul_dve[qi] = p.c("dve")

        for gb in range(NB):
            qi, hh, qt, kc, first, last = blocks[gb]
            sp = s_ps[gb % NS]
            wmm = list(rdy) if gb < NS else [("act", exp_act[gb - NS])]
            p.mm(sp,
                 self.k_reg[:, hh * T + kc * 128:hh * T + (kc + 1) * 128],
                 self.q_reg[:, hh * T + qt * QT:hh * T + (qt + 1) * QT],
                 True, True, wait=wmm, inc="pe")
            qk_pe = p.c("pe")
            r = kc - qt * nmr
            wexp = [("pe", qk_pe)]
            if r >= 0:
                p.v("tensor_add", sp, sp, self.p2_mask[:, r * QT:(r + 1) * QT],
                    wait=[("pe", qk_pe), ("dve", p.c("dve"))])
                wexp = [("pe", qk_pe), ("dve", p.c("dve"))]
            if gb >= NE:
                wexp += [("pe", av_pe[gb - NE]), ("dve", acc_dve[gb - NE])]
            p.act(self.p2_exp[gb % NE][:], sp, AF.Exp, wait=wexp)
            exp_act[gb] = p.c("act")
            if gb >= D:
                emit_back(gb - D)
        for gb in range(max(NB - D, 0), NB):
            emit_back(gb)
        odone = p.c("dve")
        for hh in range(Ha):
            p.dma("sync", self.cco_in[hh * DH:(hh + 1) * DH, :],
                  self.o_reg[:, hh * T:(hh + 1) * T],
                  wait=[("dve", odone)] if hh == 0 else [], inc="q2o")
        wrote = p.c("q2o")
        p.cchandler("AllGather", [self.cco_in.ap().opt()],
                    [self.cco_out.ap().opt()], [list(range(self.NC))],
                    wait=[("q2o", wrote)])
        self.cc_o = p.c("cc")

    def phase3(self):
        p, Tc, TT = self.p, self.Tc, self.TT
        ntt = Tc // TT
        for c in range(NCH):
            p.emit("gpsimd",
                   lambda e, c=c: e.reg_load(self.greg,
                                             self.c_aoff[0:1, c:c + 1]),
                   wait=[("cc", self.cc_o), ("pe", self.p1_done["pe"]),
                         ("qc", self.const_w[0][1])] if c == 0 else [])
            p.dma("gpsimd", self.p3_a[:, c * Tc:(c + 1) * Tc],
                  AP(self.cco_out, self.greg, [[self.T, 128], [1, Tc]]),
                  inc="gA")
        a_ld = p.c("gA")
        cnt = 0
        hist = {}
        whist = {}
        for oc in range(NCH):
            wb = self.p3_wo[oc % 2]
            gw3 = f"gw3{oc % 2}"
            wwt = [("cc", self.cc_o)]
            if oc >= 2:
                wwt.append(("pe", whist[oc - 2]))
            for grp in range(4):
                p.dma("gpsimd", wb[:, grp * 512:(grp + 1) * 512],
                      AP(self.woT, oc * 128 + grp * 4 * 128 * D_MODEL,
                         [[D_MODEL, 128], [128 * D_MODEL, 4], [1, 128]]),
                      wait=wwt if grp == 0 else [], inc=gw3)
            wld = p.c(gw3)
            for tt in range(ntt):
                psb = self.ps[0][:, 0:TT] if cnt % 2 == 0 else self.ps[1][:, 0:TT]
                rb = self.p3_r[tt % 2]
                q3r = f"q3r{tt % 2}"
                q3s = f"q3s{tt % 2}"
                wr = [("dve", hist.get(cnt - ntt, {"dve": 0})["dve"])] +                     self.r0_done
                p.dma("sync", rb[:], self.r0d[oc * 128:(oc + 1) * 128,
                                              tt * TT:(tt + 1) * TT],
                      wait=wr, inc=q3r)
                rld = p.c(q3r)
                wmm = [("gA", a_ld), (gw3, wld),
                       ("dve", hist.get(cnt - 2, {"dve": 0})["dve"])]
                for c in range(NCH):
                    p.mm(psb, wb[:, c * 128:(c + 1) * 128],
                         self.p3_a[:, c * Tc + tt * TT:c * Tc + (tt + 1) * TT],
                         c == 0, c == NCH - 1, wait=wmm if c == 0 else [],
                         inc="pe")
                mmd = p.c("pe")
                st = self.p3_st[tt % 2]
                wev = [("pe", mmd), (q3r, rld),
                       (q3s, hist.get(cnt - ntt, {q3s: 0}).get(q3s, 0))]
                p.v("tensor_add", st[:], psb, rb[:], wait=wev)
                p.dma("sync", self.r1d[oc * 128:(oc + 1) * 128,
                                       tt * TT:(tt + 1) * TT], st[:],
                      wait=[("dve", p.c("dve"))], inc=q3s)
                hist[cnt] = p.snap()
                cnt += 1
            whist[oc] = p.c("pe")
        self.r1_done = [("q3s0", p.c("q3s0")), ("q3s1", p.c("q3s1"))]


    def mamba(self, ready, upto=None):
        p, Tc, TT, NJ, NC = self.p, self.Tc, self.TT, self.NJ, self.NC
        nth = Tc // TT
        Tc3 = Tc + 3
        s1p = [self.ps[0][0:1, 0:TT], self.ps[0][0:1, 512:512 + TT]]
        s2p = [self.ps[1][0:1, 0:TT], self.ps[1][0:1, 512:512 + TT]]
        s3p = [self.ps[0][0:1, 0:TT], self.ps[0][0:1, 512:512 + TT]]
        R = self.f_rows
        lnact = {}
        x2pe = {}
        for c in range(NCH):
            xt = self.m_cscr[c % 2]
            xb = self.m_zs[c % 2]
            x2a = self.m_y[:, (c % 2) * Tc:(c % 2 + 1) * Tc]
            ql = f"ml{c % 2}"
            wl = list(ready)
            if c >= 2:
                wl += [("act", lnact.get(c % 2, 0)), ("pe", x2pe.get(c % 2, 0)),
                       ("dve", self._m_cpd.get(c % 2, 0))]
            p.dma("sync", xt[:], self.r2d[c * 128:(c + 1) * 128, :],
                  wait=wl, inc=ql)
            ld = p.c(ql)
            p.v("tensor_copy", xb[:], xt[:],
                wait=[(ql, ld), ("dve", p.c("dve"))] + (ready if c == 0 else []))
            self._m_cpd[c % 2] = p.c("dve")
            p.act(x2a, xb[:], AF.Square,
                  wait=[("dve", p.c("dve")), ("act", lnact.get(c % 2, 0))] +
                  (ready if c == 0 else []))
            aq = p.c("act")
            lnact[c % 2] = aq
            for th in range(nth):
                p.mm(s1p[th], self.c_onescol[:, 0:1],
                     xb[:, th * TT:(th + 1) * TT], c == 0, c == NCH - 1,
                     wait=[("dve", self._m_cpd[c % 2]), ("act", aq)]
                     if th == 0 else [], inc="pe")
                p.mm(s2p[th], self.c_onescol[:, 0:1],
                     x2a[:, th * TT:(th + 1) * TT],
                     c == 0, c == NCH - 1, inc="pe")
            x2pe[c % 2] = p.c("pe")
        smm = p.c("pe")
        if upto == "m0a":
            return []
        for th in range(nth):
            mb = self.ps[2][:, 0:TT]
            vb = self.ps[2][:, 512:512 + TT]
            p.v("tensor_copy", R[0:1, 0:TT], s1p[th],
                wait=[("pe", max(smm, self._f_bcd)), ("dve", p.c("dve"))])
            p.v("tensor_copy", R[0:1, TT:2 * TT], s2p[th],
                wait=[("dve", p.c("dve"))])
            p.v("tensor_scalar_mul", R[0:1, 2 * TT:3 * TT], R[0:1, 0:TT],
                1.0 / D_MODEL, wait=[("dve", p.c("dve"))])
            p.v("tensor_mul", R[0:1, 3 * TT:4 * TT], R[0:1, 2 * TT:3 * TT],
                R[0:1, 2 * TT:3 * TT], wait=[("dve", p.c("dve"))])
            p.v("scalar_tensor_tensor", R[0:1, 4 * TT:5 * TT],
                R[0:1, TT:2 * TT], 1.0 / D_MODEL, R[0:1, 3 * TT:4 * TT],
                ALU.mult, ALU.subtract, wait=[("dve", p.c("dve"))])
            rowd = p.c("dve")
            p.mm(mb, self.c_onesrow[:], R[0:1, 2 * TT:3 * TT], True, True,
                 wait=[("dve", rowd), ("act", self._f_bcact)], inc="pe")
            p.mm(vb, self.c_onesrow[:], R[0:1, 4 * TT:5 * TT], True, True,
                 inc="pe")
            bcd = p.c("pe")
            self._f_bcd = bcd
            p.act(self.f_scr[0][:], vb, AF.Ln, bias=self.c_lneps[:, 0:1],
                  wait=[("pe", bcd), ("dve", self._f_aplydve)])
            p.act(self.f_rb[th][:], self.f_scr[0][:], AF.Exp, scale=-0.5,
                  wait=[("act", p.c("act"))])
            p.act(self.f_mub[th][:], mb, AF.Copy, wait=[("act", p.c("act"))])
            self._f_bcact = p.c("act")
        nrm = p.c("act")
        rows_dve = p.c("dve")
        if upto == "m0b":
            return []
        zh = {}
        for c in range(NCH):
            xt = self.m_cscr[c % 2]
            ql = f"ml{c % 2}"
            wl = [("act", nrm), ("dve", p.c("dve")), ("pe", smm)]
            if c >= 2:
                wl.append(("dve", zh[c - 2]["dve"]))
                wl.append(("pe", zh[c - 2]["pe"]))
            p.dma("sync", xt[:], self.r2d[c * 128:(c + 1) * 128, :],
                  wait=wl, inc=ql)
            ld = p.c(ql)
            for th in range(nth):
                st = self.f_st[th % 2]
                qz = f"mz{th % 2}"
                p.v("tensor_sub", self.f_scr[th % 2][:],
                    xt[:, th * TT:(th + 1) * TT], self.f_mub[th][:],
                    wait=[(ql, ld), ("act", nrm), ("dve", p.c("dve"))])
                p.v("tensor_mul", st[:], self.f_scr[th % 2][:],
                    self.f_rb[th][:],
                    wait=[("dve", p.c("dve")), (qz, p.c(qz))])
                zst = p.c("dve")
                p.dma("sync", self.z1d[c * 128:(c + 1) * 128,
                                       th * TT:(th + 1) * TT], st[:],
                      wait=[("dve", zst)], inc=qz)
                p.v("tensor_copy",
                    self.f_xb[:, c * Tc + th * TT:c * Tc + (th + 1) * TT],
                    st[:], wait=[("dve", zst)])
            cpd = p.c("dve")
            x2a = self.m_y[:, (c % 2) * Tc:(c % 2 + 1) * Tc]
            p.act(x2a, self.f_xb[:, c * Tc:(c + 1) * Tc], AF.Square,
                  wait=[("dve", cpd), ("act", lnact.get(c % 2, 0))])
            aq = p.c("act")
            lnact[c % 2] = aq
            for th in range(nth):
                p.mm(s3p[th], self.c_onescol[:, 0:1],
                     x2a[:, th * TT:(th + 1) * TT], c == 0, c == NCH - 1,
                     wait=[("act", aq), ("dve", rows_dve)] if th == 0 else [],
                     inc="pe")
            zh[c] = p.snap()
        smm2 = p.c("pe")
        if upto == "m0c":
            return []
        for th in range(nth):
            vb = self.ps[2][:, 512:512 + TT]
            p.v("tensor_copy", R[0:1, TT:2 * TT], s3p[th],
                wait=[("pe", max(smm2, self._f_bcd)), ("dve", p.c("dve"))])
            p.mm(vb, self.c_onesrow[:], R[0:1, TT:2 * TT], True, True,
                 wait=[("dve", p.c("dve")), ("act", self._f_bcact)], inc="pe")
            self._f_bcd = p.c("pe")
            p.act(self.f_scr[0][:], vb, AF.Ln, bias=self.c_lneps[:, 0:1],
                  scale=1.0 / D_MODEL,
                  wait=[("pe", p.c("pe")), ("dve", p.c("dve"))])
            p.act(self.f_rb[th][:], self.f_scr[0][:], AF.Exp, scale=-0.5,
                  wait=[("act", p.c("act"))])
            self._f_bcact = p.c("act")
        nrm2 = p.c("act")
        if upto == "m0d":
            return []
        for c in range(NCH):
            for th in range(nth):
                p.v("tensor_mul", self.f_scr[th % 2][:],
                    self.f_xb[:, c * Tc + th * TT:c * Tc + (th + 1) * TT],
                    self.f_rb[th][:],
                    wait=[("act", nrm2), ("dve", p.c("dve"))])
                p.v("tensor_scalar_mul",
                    self.f_xb[:, c * Tc + th * TT:c * Tc + (th + 1) * TT],
                    self.f_scr[th % 2][:], self.c_inlnw[:, c:c + 1],
                    wait=[("dve", p.c("dve"))])
        rms3_done = p.c("dve")
        if upto == "m0":
            return []
        if self.dbg:
            d6 = self._dram("dbg_rms3", [NCH * 128, Tc], F32, "ExternalOutput")
            p.dma("gpsimd", d6.ap().rearrange("(c p) t -> p c t", p=128),
                  self.f_xb.ap().rearrange("p (c t) -> p c t", t=Tc),
                  wait=[("dve", rms3_done)], inc="mhi")
            self._rms3dump = p.c("mhi")
        ghist = {}
        for oc in range(41):
            ncol = 128 if oc < 40 else 32
            fwq = f"fw{oc % 2}"
            wb = self.f_w[oc % 2]
            wwt = [("dve", rms3_done), ("mz0", p.c("mz0")),
                   ("mz1", p.c("mz1"))]
            if self.dbg and oc == 0:
                wwt.append(("mhi", self._rms3dump))
            if oc >= 2:
                wwt.append(("pe", ghist[oc - 2]))
            for grp in range(4):
                p.dma("gpsimd", wb[:, grp * 4 * ncol:(grp + 1) * 4 * ncol],
                      AP(self.inprojT, oc * 128 + grp * 4 * 128 * D_PROJ,
                         [[D_PROJ, 128], [128 * D_PROJ, 4], [1, ncol]]),
                      wait=wwt if grp == 0 else [], inc=fwq)
            wld = p.c(fwq)
            psA = self.ps[oc % 2]
            wA = [("dve", rms3_done), (fwq, wld),
                  ("act", self._m_psact.get(oc % 2, 0)),
                  ("dve", self._m_psdve.get(oc % 2, 0))]
            M = ncol
            for th in range(nth):
                for c in range(NCH):
                    p.mm(psA[0:M, th * TT:th * TT + TT],
                         wb[:, c * ncol:(c + 1) * ncol],
                         self.f_xb[:, c * Tc + th * TT:c * Tc + (th + 1) * TT],
                         c == 0, c == NCH - 1,
                         wait=wA if (th == 0 and c == 0) else [], inc="pe")
            mmd = p.c("pe")
            ghist[oc] = mmd
            if oc < 16:
                sg = self.f_sg[oc % 2]
                qza = f"mza{oc % 2}"
                p.act(sg[:, 0:Tc], psA[:, 0:Tc], AF.Silu,
                      wait=[("pe", mmd),
                            (qza, self._m_zast.get(oc % 2, 0))])
                self._m_psact[oc % 2] = p.c("act")
                p.dma("sync", self.zsd[oc * 128:(oc + 1) * 128, :],
                      sg[:, 0:Tc], wait=[("act", p.c("act"))], inc=qza)
                self._m_zast[oc % 2] = p.c(qza)
            elif oc < 40:
                ch = oc - 16
                p.v("tensor_copy",
                    self.m_xbc[:, ch * Tc3 + 3:ch * Tc3 + 3 + Tc],
                    psA[:, 0:Tc], wait=[("pe", mmd)])
                self._m_psdve[oc % 2] = p.c("dve")
            else:
                p.v("tensor_copy", self.m_dtraw[:, 0:Tc], psA[0:32, 0:Tc],
                    wait=[("pe", mmd)])
                self._m_psdve[oc % 2] = p.c("dve")
        inproj_done = [("dve", p.c("dve")), ("act", p.c("act")),
                       ("pe", p.c("pe"))]
        if upto == "m1":
            return []
        for cc in range(24):
            p.dma("sync", self.cch_in[cc * 128:(cc + 1) * 128, 0:3],
                  self.m_xbc[:, cc * Tc3 + Tc:cc * Tc3 + Tc + 3],
                  wait=inproj_done if cc == 0 else [], inc="mh")
        p.cchandler("AllGather", [self.cch_in.ap().opt()],
                    [self.cch_out.ap().opt()], [list(range(NC))],
                    wait=[("mh", p.c("mh"))])
        cch = p.c("cc")
        xbc3 = self.m_xbc.ap().rearrange("p (c t) -> p c t", t=Tc3)
        for grp in range(4):
            p.emit("gpsimd",
                   lambda e, grp=grp: e.reg_load(
                       self.greg, self.c_halooff[0:1, 6 * grp:6 * grp + 1]),
                   wait=[("cc", cch), ("qc", self.const_w[0][1])]
                   if grp == 0 else [])
            p.dma("gpsimd", xbc3[:, 6 * grp:6 * (grp + 1), 0:3],
                  AP(self.cch_out, self.greg,
                     [[3, 128], [128 * 3, 6], [1, 3]]), inc="mhi")
        hld = p.c("mhi")
        p.v("tensor_scalar_mul", xbc3[:, 0:24, 0:3], xbc3[:, 0:24, 0:3],
            self.c_hz128[:, 0:1], wait=[("mhi", hld), ("dve", p.c("dve"))])
        halo_done = p.c("dve")
        convh = {}
        for cc in range(24):
            scr = self.m_cscr[cc % 2]
            w0 = [("dve", halo_done)]
            if cc >= 2:
                w0.append(("act", convh[cc - 2]))
            p.v("tensor_scalar", scr[:],
                self.m_xbc[:, cc * Tc3 + 0:cc * Tc3 + Tc],
                self.c_convw[:, cc * DCONV:cc * DCONV + 1],
                self.c_convb[:, cc:cc + 1], ALU.mult, ALU.add,
                wait=w0 + [("dve", p.c("dve"))])
            for k in range(1, DCONV):
                p.v("scalar_tensor_tensor", scr[:],
                    self.m_xbc[:, cc * Tc3 + k:cc * Tc3 + k + Tc],
                    self.c_convw[:, cc * DCONV + k:cc * DCONV + k + 1],
                    scr[:], ALU.mult, ALU.add, wait=[("dve", p.c("dve"))])
            p.act(self.m_xc[:, cc * Tc:(cc + 1) * Tc], scr[:], AF.Silu,
                  wait=[("dve", p.c("dve"))])
            convh[cc] = p.c("act")
        conv_done = p.c("act")
        if upto == "m3":
            return []
        dtb = self.c_dtb[0:NH_M, 0:1]
        sA = self.m_cscr[0][0:32, 0:Tc]
        sB = self.m_cscr[1][0:32, 0:Tc]
        p.act(sA, self.m_dtraw[:, 0:Tc], AF.Abs, bias=dtb,
              wait=[("act", conv_done), ("dve", p.c("dve"))])
        p.act(sB, sA, AF.Exp, scale=-1.0, wait=[("act", p.c("act"))])
        p.act(sA, sB, AF.Ln, bias=self.c_one[0:32, 0:1],
              wait=[("act", p.c("act"))])
        p.act(sB, self.m_dtraw[:, 0:Tc], AF.Relu, bias=dtb,
              wait=[("act", p.c("act"))])
        p.v("tensor_add", self.m_ld[32:64, 0:Tc], sA, sB,
            wait=[("dve", p.c("dve")), ("act", p.c("act"))])
        dtp_done = p.c("dve")
        if self.dbg:
            d5 = self._dram("dbg_dtraw", [32, Tc], F32, "ExternalOutput")
            p.dma("gpsimd", d5[:, :], self.m_dtraw[:, 0:Tc],
                  wait=[("dve", dtp_done)], inc="mhi")
            self._dtrawdump = p.c("mhi")
        p.v("tensor_scalar_mul", sA, self.m_ld[32:64, 0:Tc],
            self.c_nege[0:NH_M, 0:1], wait=[("dve", dtp_done)])
        p.v("memset", self.m_lacc[:], 0.0,
            wait=[("dve", p.c("dve")), ("mza0", p.c("mza0")),
                  ("mza1", p.c("mza1"))])
        for j in range(NJ):
            p.v("tensor_tensor_scan", self.m_ld[0:32, j * 128:(j + 1) * 128],
                self.c_ones32[:, 0:128],
                self.m_cscr[0][0:32, j * 128:(j + 1) * 128],
                0.0, ALU.mult, ALU.add, wait=[("dve", p.c("dve"))])
            p.v("tensor_add", self.m_lacc[:], self.m_lacc[:],
                self.m_ld[0:32, j * 128 + 127:j * 128 + 128],
                wait=[("dve", p.c("dve"))])
        p.act(self.m_pcore[:], self.m_lacc[:], AF.Exp,
              wait=[("dve", p.c("dve"))])
        pcore_done = p.c("act")
        L_done = p.c("dve")
        tev = 0
        for j in range(NJ):
            wtr = [("dve", L_done), ("act", conv_done)]
            if self.dbg and j == 0:
                wtr.append(("mhi", self._dtrawdump))
            if j > 0:
                wtr.append(("dve", tev))
            p.trans(self.ps[2][:, 0:64], self.m_ld[:, j * 128:(j + 1) * 128],
                    self.c_identf[0:64, 0:64], wait=wtr, inc="pe")
            for g2 in range(4):
                p.trans(self.pbf[:, g2 * 128:(g2 + 1) * 128],
                        self.m_xc[:, g2 * Tc + j * 128:g2 * Tc + (j + 1) * 128],
                        self.c_identbf[:], inc="pe")
                p.trans(self.pbf[:, (4 + g2) * 128:(5 + g2) * 128],
                        self.m_xc[:, (4 + g2) * Tc + j * 128:
                                  (4 + g2) * Tc + (j + 1) * 128],
                        self.c_identbf[:], inc="pe")
            tp = p.c("pe")
            p.v("tensor_copy", self.m_ldt[:, j * 64:(j + 1) * 64],
                self.ps[2][:, 0:64], wait=[("pe", tp)])
            p.v("tensor_scalar_mul", self.m_nldt[:, j * 64:(j + 1) * 64],
                self.m_ldt[:, j * 64:(j + 1) * 64], -1.0,
                wait=[("dve", p.c("dve"))])
            for g2 in range(4):
                p.v("tensor_copy",
                    self.m_xt[:, j * 512 + g2 * 128:j * 512 + (g2 + 1) * 128],
                    self.pbf[:, g2 * 128:(g2 + 1) * 128],
                    wait=[("pe", tp)] if g2 == 0 else [])
                p.v("tensor_copy",
                    self.m_bt[:, j * 512 + g2 * 128:j * 512 + (g2 + 1) * 128],
                    self.pbf[:, (4 + g2) * 128:(5 + g2) * 128])
            tev = p.c("dve")
        p.v("memset", self.m_hst[:], 0.0, wait=[("dve", tev)])
        prep_done = p.c("dve")
        if upto == "m4":
            return []
        LK = 4
        bL = [self.ps[2][:, 0:128], self.ps[2][:, 128:256],
              self.ps[2][:, 512:640], self.ps[2][:, 640:768]]
        Gp = [self.ps[0][:, 0:128], self.ps[0][:, 128:256],
              self.ps[0][:, 512:640], self.ps[0][:, 640:768]]
        Yp = [self.ps[1][0:64, 0:128], self.ps[1][0:64, 128:256],
              self.ps[1][0:64, 512:640], self.ps[1][0:64, 640:768]]
        Sps = [self.ps3b[0:64, 64 * i:64 * (i + 1)] for i in range(LK)]
        ih = {}
        lvl = 9
        if upto is not None and upto.startswith("m5") and len(upto) > 2:
            lvl = int(upto[2])
        u = 0
        for j in range(NJ):
            for h in range(NH_M):
                g = h // 4
                pb = bL[u % LK]
                Sp = Sps[u % LK]
                wbl = [("dve", prep_done), ("qc", self.const_w[0][1]),
                       ("mz0", p.c("mz0")), ("mz1", p.c("mz1"))]
                if u >= LK:
                    wbl += [("act", ih[u - LK]["act"]),
                            ("dve", ih[u - LK]["dve"])]
                p.mm(pb, AP(self.c_i32sel, h, [[32, 32], [0, 128]]),
                     self.m_ld[0:32, j * 128:(j + 1) * 128], True, True,
                     wait=wbl, inc="pe")
                blm = p.c("pe")
                esb = self.m_esb[u % LK]
                p.act(esb[:], pb, AF.Exp, wait=[("pe", blm)])
                ea = p.c("act")
                if lvl >= 2:
                    arg = self.m_arg[u % LK]
                    p.v("scalar_tensor_tensor", arg[:], pb,
                        self.m_nldt[:, j * 64 + h:j * 64 + h + 1],
                        self.c_trineg[:], ALU.add, ALU.add,
                        wait=[("pe", blm), ("act", ea), ("dve", p.c("dve"))])
                    p.act(self.m_dtile[u % LK][:], arg[:], AF.Exp,
                          wait=[("dve", p.c("dve")), ("act", ea)])
                dta = p.c("act")
                if lvl >= 3:
                    bw = self.m_bw[u % LK]
                    bts = self.m_bt[:, j * 512 + (g // 2) * 128 + (g % 2) * 64:
                                    j * 512 + (g // 2) * 128 + (g % 2) * 64 + 64]
                    p.v("tensor_scalar", bw[:], bts,
                        self.m_ldt[:, j * 64 + 32 + h:j * 64 + 32 + h + 1],
                        self.m_dtile[u % LK][:, 127:128], ALU.mult, ALU.mult,
                        wait=[("act", dta), ("dve", p.c("dve"))])
                    bwd = p.c("dve")
                    xts = self.m_xt[:, j * 512 + (g // 2) * 128 + (g % 2) * 64:
                                    j * 512 + (g // 2) * 128 + (g % 2) * 64 + 64]
                    p.mm(Sp, bw[:], xts, True, True,
                         wait=[("dve", bwd),
                               ("dve", self._m_sprd.get(u % LK, 0))], inc="pe")
                sm = p.c("pe")
                if lvl >= 4:
                    p.v("scalar_tensor_tensor",
                        self.m_hst[:, h * HD:(h + 1) * HD],
                        self.m_hst[:, h * HD:(h + 1) * HD],
                        esb[0:64, 127:128], Sp, ALU.mult, ALU.add,
                        wait=[("pe", sm), ("act", ea), ("dve", p.c("dve"))])
                    self._m_sprd[u % LK] = p.c("dve")
                ih[u] = p.snap()
                u += 1
        pass1_done = p.c("dve")
        if upto is not None and upto.startswith("m5"):
            return []
        p.dma("sync", self.ccs_in[0:64, :], self.m_hst[:],
              wait=[("dve", pass1_done)], inc="msum")
        p.dma("sync", AP(self.ccs_in, 64 * NH_M * HD, [[1, NH_M], [1, 1]]),
              self.m_pcore[:], wait=[("act", pcore_done)], inc="msum",
              allow_slow_non_contiguous=True)
        p.v("memset", self.f_scr[0][0:1, 0:TT], 0.0,
            wait=[("dve", p.c("dve")), ("act", nrm2)])
        zmd = p.c("dve")
        nz = NH_M * HD - NH_M
        for zi in range((nz + TT - 1) // TT):
            z0, z1 = zi * TT, min((zi + 1) * TT, nz)
            p.dma("sync", AP(self.ccs_in, 64 * NH_M * HD + NH_M + z0,
                             [[1, 1], [1, z1 - z0]]),
                  self.f_scr[0][0:1, 0:z1 - z0],
                  wait=[("dve", zmd)], inc="msum")
        p.cchandler("AllGather", [self.ccs_in.ap().opt()],
                    [self.ccs_out.ap().opt()], [list(range(NC))],
                    wait=[("msum", p.c("msum"))])
        ccs = p.c("cc")
        p.dma("sync", self.m_grow.ap().rearrange("o (r h) -> o r h", h=32),
              AP(self.ccs_out, 64 * NH_M * HD,
                 [[1, 1], [(DS + 1) * NH_M * HD, NC], [1, 32]]),
              wait=[("cc", ccs)], inc="msum")
        gld = p.c("msum")
        p.v("tensor_mul", R[0:1, 0:NC * 32], self.m_grow[0:1, :],
            self.c_pmrow[0:1, :], wait=[("msum", gld), ("dve", p.c("dve"))])
        p.v("tensor_scalar", R[0:1, NC * 32:2 * NC * 32], self.c_pmrow[0:1, :],
            -1.0, 1.0, ALU.mult, ALU.add, wait=[("dve", p.c("dve"))])
        p.v("tensor_add", R[0:1, 0:NC * 32], R[0:1, 0:NC * 32],
            R[0:1, NC * 32:2 * NC * 32], wait=[("dve", p.c("dve"))])
        p.v("memset", self.m_grow[0:1, (NC - 1) * 32:NC * 32], 1.0,
            wait=[("dve", p.c("dve"))])
        for r in range(NC - 2, -1, -1):
            p.v("tensor_mul", self.m_grow[0:1, r * 32:(r + 1) * 32],
                self.m_grow[0:1, (r + 1) * 32:(r + 2) * 32],
                R[0:1, (r + 1) * 32:(r + 2) * 32], wait=[("dve", p.c("dve"))])
        p.v("tensor_mul", self.m_wrow[0:1, :], self.m_grow[0:1, :],
            self.c_pmrow[0:1, :], wait=[("dve", p.c("dve"))])
        wrd = p.c("dve")
        p.mm(self.ps[2][0:64, 0:NC * 32], self.c_onesrow[0:1, 0:64],
             self.m_wrow[0:1, :], True, True, wait=[("dve", wrd)], inc="pe")
        p.v("tensor_copy", self.m_pb[:, 0:NC * 32],
            self.ps[2][0:64, 0:NC * 32], wait=[("pe", p.c("pe"))])
        p.v("memset", self.m_hst[:], 0.0,
            wait=[("dve", p.c("dve")), ("msum", p.c("msum"))])
        pbd = p.c("dve")
        for r in range(NC):
            sr = self.m_sr[r % 2]
            qs = f"msr{r % 2}"
            wl = [("cc", ccs), ("dve", pbd)]
            if r >= 2:
                wl.append(("dve", self._m_srd.get(r % 2, 0)))
            p.dma("sync", sr[:],
                  self.ccs_out[r * (DS + 1):r * (DS + 1) + 64, :],
                  wait=wl, inc=qs)
            srl = p.c(qs)
            for h in range(NH_M):
                p.v("scalar_tensor_tensor", self.m_hst[:, h * HD:(h + 1) * HD],
                    sr[:, h * HD:(h + 1) * HD],
                    self.m_pb[:, r * 32 + h:r * 32 + h + 1],
                    self.m_hst[:, h * HD:(h + 1) * HD], ALU.mult, ALU.add,
                    wait=[(qs, srl), ("dve", p.c("dve"))] if h == 0 else
                    [("dve", p.c("dve"))])
            self._m_srd[r % 2] = p.c("dve")
        hstart_done = p.c("dve")
        if upto == "m6":
            return []
        u = 0
        ih2 = {}
        for j in range(NJ):
            p.v("tensor_copy", self.m_hb[:], self.m_hst[:],
                wait=[("dve", p.c("dve"))])
            hbd = p.c("dve")
            for h in range(NH_M):
                g = h // 4
                cc, cb = c_head_slot(h)
                pb = bL[u % LK]
                Sp = Sps[u % LK]
                wbl = [("dve", hstart_done), ("dve", hbd)]
                if u >= LK:
                    wbl += [("act", ih2[u - LK]["act"]),
                            ("dve", ih2[u - LK]["dve"])]
                p.mm(pb, AP(self.c_i32sel, h, [[32, 32], [0, 128]]),
                     self.m_ld[0:32, j * 128:(j + 1) * 128], True, True,
                     wait=wbl, inc="pe")
                blm = p.c("pe")
                esb = self.m_esb[u % LK]
                p.act(esb[:], pb, AF.Exp, wait=[("pe", blm)])
                ea = p.c("act")
                arg = self.m_arg[u % LK]
                p.v("scalar_tensor_tensor", arg[:], pb,
                    self.m_nldt[:, j * 64 + h:j * 64 + h + 1],
                    self.c_trineg[:], ALU.add, ALU.add,
                    wait=[("pe", blm), ("act", ea), ("dve", p.c("dve"))])
                p.act(self.m_dtile[u % LK][:], arg[:], AF.Exp,
                      wait=[("dve", p.c("dve")), ("act", ea)])
                dta = p.c("act")
                Gb = Gp[u % LK]
                bsec = self.m_xc[(g % 2) * 64:(g % 2) * 64 + 64,
                                 (4 + g // 2) * Tc + j * 128:
                                 (4 + g // 2) * Tc + (j + 1) * 128]
                csec = self.m_xc[cb:cb + 64,
                                 (8 + cc) * Tc + j * 128:
                                 (8 + cc) * Tc + (j + 1) * 128]
                bts = self.m_bt[:, j * 512 + (g // 2) * 128 + (g % 2) * 64:
                                j * 512 + (g // 2) * 128 + (g % 2) * 64 + 64]
                p.mm(Gb, bsec, csec, True, True,
                     wait=[("act", conv_done),
                           ("dve", self._m_wtd.get(u % LK, 0))], inc="pe")
                gm = p.c("pe")
                wt = self.m_wt[u % LK]
                p.v("scalar_tensor_tensor", wt[:], Gb,
                    self.m_ldt[:, j * 64 + 32 + h:j * 64 + 32 + h + 1],
                    self.m_dtile[u % LK][:], ALU.mult, ALU.mult,
                    wait=[("pe", gm), ("act", dta), ("dve", p.c("dve"))])
                wtd = p.c("dve")
                self._m_wtd[u % LK] = wtd
                cs = self.m_cs[u % LK]
                p.v("tensor_mul", cs[:], csec, esb[cb:cb + 64, :],
                    wait=[("act", ea), ("dve", p.c("dve"))])
                csd = p.c("dve")
                Yb = Yp[u % LK]
                xts = self.m_xt[:, j * 512 + (g // 2) * 128 + (g % 2) * 64:
                                j * 512 + (g // 2) * 128 + (g % 2) * 64 + 64]
                p.mm(Yb, xts, wt[:], True, False,
                     wait=[("dve", wtd), ("dve", self._m_yev.get(u % LK, 0))],
                     inc="pe")
                p.mm(Yb, self.m_hb[:, h * HD:(h + 1) * HD], cs[:],
                     False, True, wait=[("dve", csd)], inc="pe")
                ym = p.c("pe")
                xsec = self.m_xc[(g % 2) * 64:(g % 2) * 64 + 64,
                                 (g // 2) * Tc + j * 128:
                                 (g // 2) * Tc + (j + 1) * 128]
                p.v("scalar_tensor_tensor",
                    self.m_y[(h % 2) * 64:(h % 2) * 64 + 64,
                             (h // 2) * Tc + j * 128:
                             (h // 2) * Tc + (j + 1) * 128],
                    xsec, self.c_dm64[(g % 2) * 64:(g % 2) * 64 + 64, h:h + 1],
                    Yb, ALU.mult, ALU.add,
                    wait=[("pe", ym), ("dve", p.c("dve"))])
                self._m_yev[u % LK] = p.c("dve")
                bw = self.m_bw[u % LK]
                p.v("tensor_scalar", bw[:], bts,
                    self.m_ldt[:, j * 64 + 32 + h:j * 64 + 32 + h + 1],
                    self.m_dtile[u % LK][:, 127:128], ALU.mult, ALU.mult,
                    wait=[("act", dta), ("dve", p.c("dve"))])
                bwd = p.c("dve")
                p.mm(Sp, bw[:], xts, True, True,
                     wait=[("dve", bwd),
                           ("dve", self._m_sprd.get(u % LK, 0))], inc="pe")
                sm = p.c("pe")
                p.v("scalar_tensor_tensor", self.m_hst[:, h * HD:(h + 1) * HD],
                    self.m_hst[:, h * HD:(h + 1) * HD],
                    esb[0:64, 127:128], Sp, ALU.mult, ALU.add,
                    wait=[("pe", sm), ("dve", p.c("dve"))])
                self._m_sprd[u % LK] = p.c("dve")
                ih2[u] = p.snap()
                u += 1
        y_done = p.c("dve")
        if upto == "m7":
            return []
        if self.dbg:
            d1 = self._dram("dbg_xc", [24 * 128, Tc], F32, "ExternalOutput")
            p.dma("gpsimd", d1.ap().rearrange("(c p) t -> p c t", p=128),
                  self.m_xc.ap().rearrange("p (c t) -> p c t", t=Tc),
                  wait=[("act", conv_done)], inc="mhi")
            d2 = self._dram("dbg_ld", [64, Tc], F32, "ExternalOutput")
            p.dma("gpsimd", d2[:, :], self.m_ld[:], wait=[("dve", y_done)],
                  inc="mhi")
            d3 = self._dram("dbg_y", [NCH * 128, Tc], F32, "ExternalOutput")
            p.dma("gpsimd", d3.ap().rearrange("(c p) t -> p c t", p=128),
                  self.m_y.ap().rearrange("p (c t) -> p c t", t=Tc),
                  wait=[("dve", y_done)], inc="mhi")
            d4 = self._dram("dbg_hst", [64, NH_M * HD], F32, "ExternalOutput")
            p.dma("gpsimd", d4[:, :], self.m_hst[:],
                  wait=[("dve", hstart_done)], inc="mhi")

        gh = {}
        for c in range(NCH):
            zs = self.m_zs[c % 2]
            ql = f"ml{c % 2}"
            wl = [("dve", y_done), ("mza0", p.c("mza0")),
                  ("mza1", p.c("mza1")), ("mhi", p.c("mhi"))]
            if c >= 2:
                wl.append(("dve", gh[c - 2]["dve"]))
            p.dma("sync", zs[:], self.zsd[c * 128:(c + 1) * 128, :],
                  wait=wl, inc=ql)
            ld = p.c(ql)
            p.v("tensor_mul", self.m_y[:, c * Tc:(c + 1) * Tc],
                self.m_y[:, c * Tc:(c + 1) * Tc], zs[:],
                wait=[(ql, ld), ("dve", p.c("dve"))])
            yzd = p.c("dve")
            x2b = self.m_x2b[c % 2]
            for th in range(nth):
                p.act(x2b[:], self.m_y[:, c * Tc + th * TT:
                                       c * Tc + (th + 1) * TT], AF.Square,
                      wait=[("dve", yzd), ("act", p.c("act")),
                            ("pe", self._m_g2pe.get(c % 2, 0))])
                p.mm(s1p[th], self.c_onescol[:, 0:1], x2b[:],
                     c == 0, c == NCH - 1,
                     wait=[("act", p.c("act"))], inc="pe")
            self._m_g2pe[c % 2] = p.c("pe")
            gh[c] = p.snap()
        smm3 = p.c("pe")
        for th in range(nth):
            vb = self.ps[2][:, 512:512 + TT]
            p.v("tensor_copy", R[0:1, TT:2 * TT], s1p[th],
                wait=[("pe", max(smm3, self._f_bcd)), ("dve", p.c("dve"))])
            p.mm(vb, self.c_onesrow[:], R[0:1, TT:2 * TT], True, True,
                 wait=[("dve", p.c("dve")), ("act", self._f_bcact)], inc="pe")
            self._f_bcd = p.c("pe")
            p.act(self.f_scr[0][:], vb, AF.Ln, bias=self.c_lneps[:, 0:1],
                  scale=1.0 / D_MODEL,
                  wait=[("pe", p.c("pe")), ("dve", p.c("dve"))])
            p.act(self.f_rb[th][:], self.f_scr[0][:], AF.Exp, scale=-0.5,
                  wait=[("act", p.c("act"))])
            self._f_bcact = p.c("act")
        nrm3 = p.c("act")
        for c in range(NCH):
            for th in range(nth):
                p.v("tensor_mul", self.f_scr[th % 2][:],
                    self.m_y[:, c * Tc + th * TT:c * Tc + (th + 1) * TT],
                    self.f_rb[th][:], wait=[("act", nrm3), ("dve", p.c("dve"))])
                p.v("tensor_scalar_mul",
                    self.m_y[:, c * Tc + th * TT:c * Tc + (th + 1) * TT],
                    self.f_scr[th % 2][:], self.c_ssmw[:, c:c + 1],
                    wait=[("dve", p.c("dve"))])
        ynorm_done = p.c("dve")
        cnt = 0
        hist = {}
        wh = {}
        for oc in range(NCH):
            fwq = f"fw{oc % 2}"
            wb = self.f_wd[oc % 2]
            wwt = [("dve", ynorm_done)]
            if oc >= 2:
                wwt.append(("pe", wh[oc - 2]))
            for grp in range(4):
                p.dma("gpsimd", wb[:, grp * 512:(grp + 1) * 512],
                      AP(self.outprojT, oc * 128 + grp * 4 * 128 * D_MODEL,
                         [[D_MODEL, 128], [128 * D_MODEL, 4], [1, 128]]),
                      wait=wwt if grp == 0 else [], inc=fwq)
            wld = p.c(fwq)
            for th in range(nth):
                psb = self.ps[0][:, 0:TT] if cnt % 2 == 0 else self.ps[1][:, 0:TT]
                wmm = [("dve", ynorm_done), (fwq, wld),
                       ("dve", hist.get(cnt - 2, {"dve": 0})["dve"])]
                for c in range(NCH):
                    p.mm(psb, wb[:, c * 128:(c + 1) * 128],
                         self.m_y[:, c * Tc + th * TT:c * Tc + (th + 1) * TT],
                         c == 0, c == NCH - 1, wait=wmm if c == 0 else [],
                         inc="pe")
                mmd = p.c("pe")
                rz = self.f_res[th % 2]
                qra = f"fra{th % 2}"
                p.dma("sync", rz[:], self.z1d[oc * 128:(oc + 1) * 128,
                                              th * TT:(th + 1) * TT],
                      wait=[("dve", hist.get(cnt - nth, {"dve": 0})["dve"]),
                            ("dve", ynorm_done),
                            ("mz0", p.c("mz0")), ("mz1", p.c("mz1"))],
                      inc=qra)
                rld = p.c(qra)
                st = self.f_st[th % 2]
                qs = f"mo{th % 2}"
                p.v("tensor_add", st[:], psb, rz[:],
                    wait=[("pe", mmd), (qra, rld),
                          (qs, hist.get(cnt - nth, {qs: 0}).get(qs, 0))])
                p.dma("sync", self.inner2d[oc * 128:(oc + 1) * 128,
                                           th * TT:(th + 1) * TT], st[:],
                      wait=[("dve", p.c("dve"))], inc=qs)
                hist[cnt] = p.snap()
                cnt += 1
            wh[oc] = p.c("pe")
        return [("mo0", p.c("mo0")), ("mo1", p.c("mo1"))]

    def ffn_block(self, in_dram, wg, wu, wd, res_drams, out_dram, ready,
                  norm="ln", normw=None, out_tm=False):
        p, Tc, TT = self.p, self.Tc, self.TT
        nth = Tc // TT
        for c in range(NCH):
            p.dma("gpsimd", self.f_xb[:, c * Tc:(c + 1) * Tc],
                  in_dram[c * 128:(c + 1) * 128, :],
                  wait=ready if c == 0 else [], inc="flx")
        lx = p.c("flx")
        s1p = [self.ps[0][0:1, 0:TT], self.ps[0][0:1, 512:512 + TT]]
        s2p = [self.ps[1][0:1, 0:TT], self.ps[1][0:1, 512:512 + TT]]
        for c in range(NCH):
            for th in range(nth):
                up = (c * nth + th) % 2
                x2 = self.f_x2[up]
                wsq = [("flx", lx)] + (ready if (c == 0 and th == 0) else [])
                wsq.append(("pe", self._f_x2pe.get(up, 0)))
                p.act(x2[:], self.f_xb[:, c * Tc + th * TT:
                                       c * Tc + (th + 1) * TT], AF.Square,
                      wait=wsq)
                aq = p.c("act")
                if norm == "ln":
                    p.mm(s1p[th], self.c_onescol[:, 0:1],
                         self.f_xb[:, c * Tc + th * TT:c * Tc + (th + 1) * TT],
                         c == 0, c == NCH - 1,
                         wait=[("flx", lx), ("act", aq)], inc="pe")
                p.mm(s2p[th], self.c_onescol[:, 0:1],
                     x2[:], c == 0, c == NCH - 1,
                     wait=[("act", aq)], inc="pe")
                self._f_x2pe[up] = p.c("pe")
        smm = p.c("pe")
        R = self.f_rows
        appw = []
        for th in range(nth):
            mb = self.ps[2][:, 0:TT]
            vb = self.ps[2][:, 512:512 + TT]
            if norm == "ln":
                p.v("tensor_copy", R[0:1, 0:TT], s1p[th],
                    wait=[("pe", max(smm, self._f_bcd)),
                          ("dve", self._f_rowdve)])
                p.v("tensor_copy", R[0:1, TT:2 * TT], s2p[th],
                    wait=[("dve", p.c("dve"))])
                p.v("tensor_scalar_mul", R[0:1, 2 * TT:3 * TT], R[0:1, 0:TT],
                    1.0 / D_MODEL, wait=[("dve", p.c("dve"))])
                p.v("tensor_mul", R[0:1, 3 * TT:4 * TT],
                    R[0:1, 2 * TT:3 * TT], R[0:1, 2 * TT:3 * TT],
                    wait=[("dve", p.c("dve"))])
                p.v("scalar_tensor_tensor", R[0:1, 4 * TT:5 * TT],
                    R[0:1, TT:2 * TT], 1.0 / D_MODEL, R[0:1, 3 * TT:4 * TT],
                    ALU.mult, ALU.subtract, wait=[("dve", p.c("dve"))])
                rowd = p.c("dve")
                self._f_rowdve = rowd
                wbc = [("dve", rowd), ("act", self._f_bcact)]
                p.mm(mb, self.c_onesrow[:], R[0:1, 2 * TT:3 * TT], True, True,
                     wait=wbc, inc="pe")
                p.mm(vb, self.c_onesrow[:], R[0:1, 4 * TT:5 * TT], True, True,
                     inc="pe")
                bcd = p.c("pe")
                self._f_bcd = bcd
                p.act(self.f_scr[0][:], vb, AF.Ln, bias=self.c_lneps[:, 0:1],
                      wait=[("pe", bcd), ("dve", self._f_aplydve)])
                p.act(self.f_rb[th][:], self.f_scr[0][:], AF.Exp, scale=-0.5,
                      wait=[("act", p.c("act"))])
                p.act(self.f_mub[th][:], mb, AF.Copy,
                      wait=[("act", p.c("act"))])
                self._f_bcact = p.c("act")
            else:
                p.v("tensor_copy", R[0:1, TT:2 * TT], s2p[th],
                    wait=[("pe", max(smm, self._f_bcd)),
                          ("dve", self._f_rowdve)])
                rowd = p.c("dve")
                self._f_rowdve = rowd
                wbc = [("dve", rowd), ("act", self._f_bcact)]
                p.mm(vb, self.c_onesrow[:], R[0:1, TT:2 * TT], True, True,
                     wait=wbc, inc="pe")
                bcd = p.c("pe")
                self._f_bcd = bcd
                p.act(self.f_scr[0][:], vb, AF.Ln, bias=self.c_lneps[:, 0:1],
                      scale=1.0 / D_MODEL,
                      wait=[("pe", bcd), ("dve", self._f_aplydve)])
                p.act(self.f_rb[th][:], self.f_scr[0][:], AF.Exp, scale=-0.5,
                      wait=[("act", p.c("act"))])
                self._f_bcact = p.c("act")
            nrm = p.c("act")
            for c in range(NCH):
                scr = self.f_scr[c % 2]
                wap = [("act", nrm), ("dve", p.c("dve"))]
                if norm == "ln":
                    p.v("tensor_sub",
                        scr[:], self.f_xb[:, c * Tc + th * TT:
                                          c * Tc + (th + 1) * TT],
                        self.f_mub[th][:], wait=wap)
                    p.v("tensor_mul",
                        self.f_xb[:, c * Tc + th * TT:c * Tc + (th + 1) * TT],
                        scr[:], self.f_rb[th][:], wait=[("dve", p.c("dve"))])
                else:
                    p.v("tensor_mul", scr[:],
                        self.f_xb[:, c * Tc + th * TT:c * Tc + (th + 1) * TT],
                        self.f_rb[th][:], wait=wap)
                    p.v("tensor_scalar_mul",
                        self.f_xb[:, c * Tc + th * TT:c * Tc + (th + 1) * TT],
                        scr[:], normw[:, c:c + 1], wait=[("dve", p.c("dve"))])
            self._f_aplydve = p.c("dve")
        xb_ready = p.c("dve")
        ghist = {}
        for oc in range(NFH):
            fwq = f"fw{oc % 2}"
            wgb = self.f_w[oc % 2]
            wub = self.f_w[2 + oc % 2]
            wwt = [("dve", xb_ready)]
            if oc >= 2:
                wwt.append(("pe", ghist[oc - 2]))
            for grp in range(4):
                p.dma("gpsimd", wgb[:, grp * 512:(grp + 1) * 512],
                      AP(wg, oc * 128 + grp * 4 * 128 * D_FF,
                         [[D_FF, 128], [128 * D_FF, 4], [1, 128]]),
                      wait=wwt if grp == 0 else [], inc=fwq)
            for grp in range(4):
                p.dma("gpsimd", wub[:, grp * 512:(grp + 1) * 512],
                      AP(wu, oc * 128 + grp * 4 * 128 * D_FF,
                         [[D_FF, 128], [128 * D_FF, 4], [1, 128]]),
                      inc=fwq)
            wld = p.c(fwq)
            psA = self.ps[(2 * oc) % 3]
            psB = self.ps[(2 * oc + 1) % 3]
            wA = [("dve", xb_ready), (fwq, wld),
                  ("act", self._f_psact.get((2 * oc) % 3, 0)),
                  ("dve", self._f_psdve.get((2 * oc) % 3, 0))]
            for th in range(nth):
                for c in range(NCH):
                    p.mm(psA[:, th * TT:th * TT + TT],
                         wgb[:, c * 128:(c + 1) * 128],
                         self.f_xb[:, c * Tc + th * TT:c * Tc + (th + 1) * TT],
                         c == 0, c == NCH - 1,
                         wait=wA if (th == 0 and c == 0) else [], inc="pe")
            gmm = p.c("pe")
            wB = [(fwq, wld), ("act", self._f_psact.get((2 * oc + 1) % 3, 0)),
                  ("dve", self._f_psdve.get((2 * oc + 1) % 3, 0))]
            for th in range(nth):
                for c in range(NCH):
                    p.mm(psB[:, th * TT:th * TT + TT],
                         wub[:, c * 128:(c + 1) * 128],
                         self.f_xb[:, c * Tc + th * TT:c * Tc + (th + 1) * TT],
                         c == 0, c == NCH - 1,
                         wait=wB if (th == 0 and c == 0) else [], inc="pe")
            umm = p.c("pe")
            sg = self.f_sg[oc % 2]
            p.act(sg[:, 0:Tc], psA[:, 0:Tc], AF.Silu,
                  wait=[("pe", gmm), ("dve", self._f_sgdve.get(oc % 2, 0))])
            self._f_psact[(2 * oc) % 3] = p.c("act")
            p.v("tensor_mul", self.f_hid[:, oc * Tc:(oc + 1) * Tc],
                psB[:, 0:Tc], sg[:, 0:Tc],
                wait=[("pe", umm), ("act", p.c("act"))])
            self._f_psdve[(2 * oc + 1) % 3] = p.c("dve")
            self._f_psdve[(2 * oc) % 3] = p.c("dve")
            self._f_sgdve[oc % 2] = p.c("dve")
            ghist[oc] = p.c("pe")
        hid_done = p.c("dve")
        gu_pe = p.c("pe")
        cnt = 0
        hist = {}
        dwh = {}
        for oc2 in range(NCH):
            fdq = f"fd{oc2 % 2}"
            wdb = self.f_wd[oc2 % 2]
            wwt = [("pe", gu_pe)]
            if oc2 >= 2:
                wwt.append(("pe", dwh[oc2 - 2]))
            for grp in range(16):
                p.dma("gpsimd", wdb[:, grp * 512:(grp + 1) * 512],
                      AP(wd, oc2 * 128 + grp * 4 * 128 * D_MODEL,
                         [[D_MODEL, 128], [128 * D_MODEL, 4], [1, 128]]),
                      wait=wwt if grp == 0 else [], inc=fdq)
            wld = p.c(fdq)
            for th in range(nth):
                psb = self.ps[0][:, 0:TT] if cnt % 2 == 0 else self.ps[1][:, 0:TT]
                wmm = [("dve", hid_done), (fdq, wld),
                       ("dve", hist.get(cnt - 2, {"dve": 0})["dve"])]
                for hc in range(NFH):
                    p.mm(psb, wdb[:, hc * 128:(hc + 1) * 128],
                         self.f_hid[:, hc * Tc + th * TT:
                                    hc * Tc + (th + 1) * TT],
                         hc == 0, hc == NFH - 1,
                         wait=wmm if hc == 0 else [], inc="pe")
                mmd = p.c("pe")
                ra = self.f_res[th % 2]
                qra = f"fra{th % 2}"
                wr = [("dve", hist.get(cnt - nth, {"dve": 0})["dve"]),
                      ("pe", gu_pe), ("dve", hid_done)]
                p.dma("sync", ra[:], res_drams[0][oc2 * 128:(oc2 + 1) * 128,
                                                  th * TT:(th + 1) * TT],
                      wait=wr, inc=qra)
                rld = [(qra, p.c(qra))]
                if len(res_drams) > 1:
                    rb2 = self.f_res2[th % 2]
                    qrb = f"frb{th % 2}"
                    p.dma("sync", rb2[:],
                          res_drams[1][oc2 * 128:(oc2 + 1) * 128,
                                       th * TT:(th + 1) * TT],
                          wait=wr, inc=qrb)
                    rld.append((qrb, p.c(qrb)))
                st = self.f_st[th % 2]
                qs = f"fs{th % 2}"
                wev = [("pe", mmd),
                       (qs, hist.get(cnt - nth, {qs: 0}).get(qs, 0))] + rld
                if len(res_drams) > 1:
                    p.v("tensor_add", st[:], psb, self.f_res2[th % 2][:],
                        wait=wev)
                    p.v("tensor_add", st[:], st[:], ra[:],
                        wait=[("dve", p.c("dve"))])
                else:
                    p.v("tensor_add", st[:], psb, ra[:], wait=wev)
                evd = p.c("dve")
                if not out_tm:
                    p.dma("sync", out_dram[oc2 * 128:(oc2 + 1) * 128,
                                           th * TT:(th + 1) * TT], st[:],
                          wait=[("dve", evd)], inc=qs)
                else:
                    for c4 in range(TT // 128):
                        wtr = [("dve", evd)]
                        if cnt > 0 or c4 > 0:
                            wtr.append(("dve", self._f_tmev))
                        p.trans(self.ps[2][:, c4 * 128:(c4 + 1) * 128],
                                st[:, c4 * 128:(c4 + 1) * 128],
                                self.c_identf[:],
                                wait=wtr if c4 == 0 else [], inc="pe")
                    tp = p.c("pe")
                    ob = self.f_ost[th % 2]
                    for c4 in range(TT // 128):
                        p.v("tensor_copy", ob[:, c4 * 128:(c4 + 1) * 128],
                            self.ps[2][:, c4 * 128:(c4 + 1) * 128],
                            wait=[("pe", tp),
                                  (qs, hist.get(cnt - nth, {qs: 0}).get(qs, 0))]
                            if c4 == 0 else [])
                    self._f_tmev = p.c("dve")
                    for c4 in range(TT // 128):
                        p.dma("sync",
                              out_dram[th * TT + c4 * 128:
                                       th * TT + (c4 + 1) * 128,
                                       oc2 * 128:(oc2 + 1) * 128],
                              ob[:, c4 * 128:(c4 + 1) * 128],
                              wait=[("dve", self._f_tmev)] if c4 == 0 else [],
                              inc=qs)
                hist[cnt] = p.snap()
                cnt += 1
            dwh[oc2] = p.c("pe")
        return [("fs0", p.c("fs0")), ("fs1", p.c("fs1"))]

    def build(self, upto=None):
        self._p0ev = 0
        self._bev = 0
        self._bst2 = {}
        self._v_hist = []
        self._p1_prev_rstd = 0
        self._f_rowdve = 0
        self._f_bcact = 0
        self._f_aplydve = 0
        self._f_x2pe = {}
        self._f_psact = {}
        self._f_psdve = {}
        self._f_sgdve = {}
        self._f_tmev = 0
        self._f_bcd = 0
        self._m_cpd = {}
        self._m_psact = {}
        self._m_psdve = {}
        self._m_zast = {}
        self._m_sprd = {}
        self._m_srd = {}
        self._m_wtd = {}
        self._m_yev = {}
        self._m_g2pe = {}
        self.const_load()
        self.phase0()
        if upto == "p0":
            self.dump_end()
            return self._fin()
        self.phase1()
        if upto == "p1":
            self.dump_qkv()
            self.dump_end()
            return self._fin()
        self.phase2()
        if upto == "p2":
            self.dump_o()
            self.dump_end()
            return self._fin()
        self.phase3()
        if upto == "p3":
            self.finish(self.r1d)
            return self._fin()
        ready = list(self.r1_done) + [("dve", self.p.c("dve")),
                                      ("pe", self.p.c("pe"))]
        self.r2_done = self.ffn_block(self.r1d, self.gateT, self.upT,
                                      self.downT, [self.r1d], self.r2d,
                                      ready, norm="ln")
        if upto == "ffn1":
            self.finish(self.r2d)
            return self._fin()
        ready = list(self.r2_done) + [("dve", self.p.c("dve")),
                                      ("pe", self.p.c("pe")),
                                      ("act", self.p.c("act"))]
        self.i2_done = self.mamba(ready, upto=upto)
        if upto is not None and upto.startswith("m") and upto != "mamba":
            self.dump_end()
            return self._fin()
        if upto == "mamba":
            self.finish(self.inner2d)
            return self._fin()
        ready = list(self.i2_done) + [("dve", self.p.c("dve")),
                                      ("pe", self.p.c("pe")),
                                      ("act", self.p.c("act"))]
        self.r4_done = self.ffn_block(self.inner2d, self.bgateT, self.bupT,
                                      self.bdownT, [self.inner2d, self.r2d],
                                      self.r4d, ready, norm="rms",
                                      normw=self.c_postlnw)
        if upto == "ffnb":
            self.finish(self.r4d)
            return self._fin()
        ready = list(self.r4_done) + [("dve", self.p.c("dve")),
                                      ("pe", self.p.c("pe")),
                                      ("act", self.p.c("act"))]
        self.y_done = self.ffn_block(self.r4d, self.gateT, self.upT,
                                     self.downT, [self.r4d], self.y_sh,
                                     ready, norm="ln", out_tm=True)
        self.p.emit("sync", lambda e: e.nop(),
                    wait=[("fs0", self.p.c("fs0")), ("fs1", self.p.c("fs1"))])
        return self._fin()

    def _fin(self):
        self.p.replay()
        self.nc.finalize()
        return self.nc

    def dump_qkv(self):
        p, T = self.p, self.T
        d = self._dram("dbg_qkv", [3 * 128, self.Ha * T], F32,
                       "ExternalOutput")
        w = [("dve", self.p1_done["dve"]), ("pe", self.p1_done["pe"])]
        p.dma("gpsimd", d[0:128, :], self.q_reg[:], wait=w, inc="gA")
        p.dma("gpsimd", d[128:256, :], self.k_reg[:], inc="gA")
        p.dma("gpsimd", d[256:384, :], self.v_reg[:], inc="gA")

    def dump_o(self):
        p, T = self.p, self.T
        d = self._dram("dbg_o", [self.NC * self.Ha * DH, T], F32,
                       "ExternalOutput")
        p.dma("gpsimd", d[:, :], self.cco_out.ap(),
              wait=[("cc", self.cc_o)], inc="gA")

    def dump_end(self):
        p = self.p
        p.emit("sync", lambda e: e.nop(),
               wait=[(s, self.p.c(s)) for s in self.p.cnt if self.p.c(s) > 0])

    def finish(self, src_dram):
        p, Tc = self.p, self.Tc
        nt = Tc // 128
        hist = {}
        fst = {}
        fev = 0
        for i in range(nt):
            xb = self.b_x[i % 2]
            qf = f"qf{i % 2}"
            wt = [(s, self.p.c(s)) for s in
                  ("q3s0", "q3s1", "fs0", "fs1", "mo0", "mo1", "mz0", "mz1")]
            wt += [("dve", self.p.c("dve")), ("pe", self.p.c("pe"))] \
                if i == 0 else []
            if i >= 2:
                wt.append(("pe", hist[i - 2]["pe"]))
            xbr = xb.ap().rearrange("p (c j) -> p c j", j=128)
            for grp in range(4):
                p.dma("sync", xbr[:, grp * 4:(grp + 1) * 4, :],
                      AP(src_dram, i * 128 + grp * 4 * 128 * Tc,
                         [[Tc, 128], [128 * Tc, 4], [1, 128]]),
                      wait=wt if grp == 0 else [], inc=qf)
            ld = p.c(qf)
            for g in range(NCH // 4):
                for c4 in range(4):
                    c = g * 4 + c4
                    wtr = [(qf, ld)]
                    if i > 0 or g > 0:
                        wtr.append(("dve", fev))
                    p.trans(self.ps[1][:, c4 * 128:(c4 + 1) * 128],
                            xb[:, c * 128:(c + 1) * 128], self.c_identf[:],
                            wait=wtr if c4 == 0 else [], inc="pe")
                tp = p.c("pe")
                sbuf = self.b_st[g % 2]
                qs = f"q0s{g % 2}"
                wev = [("pe", tp), (qs, fst.get(g % 2, self._bst2.get(g % 2, 0)))]
                for c4 in range(4):
                    p.v("tensor_copy", sbuf[:, c4 * 128:(c4 + 1) * 128],
                        self.ps[1][:, c4 * 128:(c4 + 1) * 128],
                        wait=wev if c4 == 0 else [])
                ev = p.c("dve")
                fev = ev
                for c4 in range(4):
                    c = g * 4 + c4
                    p.dma("sync", self.y_sh[i * 128:(i + 1) * 128,
                                            c * 128:(c + 1) * 128],
                          sbuf[:, c4 * 128:(c4 + 1) * 128],
                          wait=[("dve", ev)] if c4 == 0 else [], inc=qs)
                fst[g % 2] = p.c(qs)
            hist[i] = p.snap()
        p.emit("sync", lambda e: e.nop(),
               wait=[("q0s0", p.c("q0s0")), ("q0s1", p.c("q0s1"))])


def _rope_tables(T, D):
    inv_freq = 1.0 / 10000.0 ** (np.arange(0, D, 2, dtype=np.float32) / D)
    t = np.arange(T, dtype=np.float32)
    freqs = t[:, None] * inv_freq[None, :]
    sin = np.repeat(np.sin(freqs), 2, axis=-1).astype(np.float32)
    cos = np.repeat(np.cos(freqs), 2, axis=-1).astype(np.float32)
    return sin, cos


def _host_constants(NC, T):
    QT = min(512, T)
    sin, cos = _rope_tables(T, DH)
    sincosT = np.ascontiguousarray(
        np.concatenate([sin.T, cos.T], axis=0).astype(np.float32))
    nmr = QT // 128
    r = np.arange(128)[:, None]
    j = np.arange(QT)[None, :]
    mask = np.concatenate(
        [np.where(128 * rr + r > j, np.float32(MASK_NEG), np.float32(0.0))
         for rr in range(nmr)], axis=1).astype(np.float32)
    ident = np.eye(128)
    ptm = np.zeros((128, 128), dtype=np.float32)
    for i in range(64):
        ptm[2 * i, 2 * i + 1] = -1.0
        ptm[2 * i + 1, 2 * i] = 1.0
    s = np.arange(128)[:, None]
    trineg = np.where(s > j[:, 0:128], np.float32(MASK_NEG),
                      np.float32(0.0)).astype(np.float32)
    return {
        "sincosT": sincosT,
        "maskadd": np.ascontiguousarray(mask),
        "ident_bf": ident.astype(ml_dtypes.bfloat16),
        "ident_f32": ident.astype(np.float32),
        "pt_bf": np.ascontiguousarray(ptm.T).astype(ml_dtypes.bfloat16),
        "trineg": trineg,
        "i32sel": np.eye(32, dtype=np.float32),
        "onesrow": np.ones((1, 128), dtype=np.float32),
        "onescol_bf": np.ones((128, 1), dtype=ml_dtypes.bfloat16),
        "ones128": np.ones((128, 128), dtype=np.float32),
    }


def make_in_maps(inputs, NC, T):
    Tc = T // NC
    Ha = N_HEAD_A // NC
    g = {k: np.ascontiguousarray(np.asarray(v)) for k, v in inputs.items()}
    x = g["L0_attn_a_x"].reshape(T, D_MODEL)
    consts = _host_constants(NC, T)
    wqkv = g["L0_attn_a_W_qkv"]
    perm = c_perm_rows()
    inproj = g["L0_attn_b_in_proj_w"]
    inproj_p = inproj.copy()
    inproj_p[3072:5120] = inproj[3072:5120][perm]
    convw = g["L0_attn_b_conv_w"].copy()
    convb = g["L0_attn_b_conv_b"].copy()
    convw[1024:3072] = convw[1024:3072][perm]
    convb[1024:3072] = convb[1024:3072][perm]
    base = {
        "woT": np.ascontiguousarray(g["L0_attn_a_W_o"].T),
        "gateT": np.ascontiguousarray(g["L0_ffn_w_gate"].T),
        "upT": np.ascontiguousarray(g["L0_ffn_w_up"].T),
        "downT": np.ascontiguousarray(g["L0_ffn_w_down"].T),
        "bgateT": np.ascontiguousarray(g["L0_attn_b_gate_w"].T),
        "bupT": np.ascontiguousarray(g["L0_attn_b_up_w"].T),
        "bdownT": np.ascontiguousarray(g["L0_attn_b_down_w"].T),
        "inprojT": np.ascontiguousarray(inproj_p.T),
        "outprojT": np.ascontiguousarray(g["L0_attn_b_out_proj_w"].T),
        "convw": np.ascontiguousarray(convw),
        "convb": np.ascontiguousarray(convb.reshape(CONV_CH, 1)),
        "dtbias": np.ascontiguousarray(
            g["L0_attn_b_dt_bias"].reshape(NH_M, 1)),
        "alog": np.ascontiguousarray(g["L0_attn_b_A_log"].reshape(NH_M, 1)),
        "dmrow": np.ascontiguousarray(g["L0_attn_b_D"].reshape(1, NH_M)),
        "ssmw": np.ascontiguousarray(
            g["L0_attn_b_ssm_norm_w"].reshape(NCH, 128)),
        "inlnw": np.ascontiguousarray(
            g["L0_attn_b_input_ln_w"].reshape(NCH, 128)),
        "postlnw": np.ascontiguousarray(
            g["L0_attn_b_post_ln_w"].reshape(NCH, 128)),
    }
    base.update(consts)
    in_maps = []
    for c in range(NC):
        m = dict(base)
        m["x_sh"] = np.ascontiguousarray(x[c * Tc:(c + 1) * Tc])
        rows = []
        for s in range(3):
            for hh in range(Ha):
                h = c * Ha + hh
                rows.append(wqkv[s * D_MODEL + h * DH:s * D_MODEL + (h + 1) * DH])
        m["wqkvT"] = np.ascontiguousarray(np.concatenate(rows, 0).T)
        pm = np.zeros(NC * 32, dtype=np.float32)
        pm[:c * 32] = 1.0
        m["pmrow"] = np.ascontiguousarray(pm.reshape(1, NC * 32))
        m["haloz"] = np.array([[0.0 if c == 0 else 1.0]], dtype=np.float32)
        m["aoff"] = np.asarray([[cc * 128 * T + c * Tc for cc in range(NCH)]],
                               dtype=np.int32)
        m["halooff"] = np.asarray(
            [[((c - 1) % NC) * CONV_CH * 3 + cc * 128 * 3
              for cc in range(24)]], dtype=np.int32)
        in_maps.append(m)
    return in_maps


def kernel(**inputs):
    NC, T = 8, 8192
    mk = MK(NC=NC, T=T)
    nc = mk.build()
    in_maps = make_in_maps(inputs, NC, T)
    res = run_bass_kernel_spmd(nc, in_maps, core_ids=list(range(NC)))
    out = np.concatenate([np.asarray(r["y_sh"]) for r in res.results], axis=0)
    return out.reshape(1, T, D_MODEL)


# version 09 (final)
# speedup vs baseline: 1.1864x; 1.0939x over previous
import numpy as np
import ml_dtypes
import concourse.bass as bass
import concourse.mybir as mybir
from concourse.ap import AP
from concourse.bass_utils import run_bass_kernel_spmd

F32 = mybir.dt.float32
BF16 = mybir.dt.bfloat16
AF = mybir.ActivationFunctionType
ALU = mybir.AluOpType
AX = mybir.AxisListType

D_MODEL = 2048
NCH = 16
D_FF = 8192
NFH = 64
N_HEAD_A = 16
DH = 128
D_INNER = 2048
D_XB = 512
NH_M = 32
HD = 64
DS = 64
DCONV = 4
D_PROJ = 2 * D_INNER + 2 * D_XB + NH_M
CONV_CH = 2 * D_XB + D_INNER
LN_EPS = 1e-5
RMS_EPS_A = 1.1920929e-07
EPS_M = 1e-5
MASK_NEG = -30000.0

ENGINES = ("sync", "tensor", "scalar", "vector", "gpsimd")

_CE = [h for h in range(NH_M) if (h // 4) % 2 == 0]
_CO = [h for h in range(NH_M) if (h // 4) % 2 == 1]


def c_head_slot(h):
    if (h // 4) % 2 == 0:
        return _CE.index(h), 0
    return _CO.index(h), 64


def c_perm_rows():
    perm = np.zeros(D_INNER, dtype=np.int64)
    for h in range(NH_M):
        chunk, base = c_head_slot(h)
        for n in range(DS):
            perm[chunk * 128 + base + n] = h * DS + n
    return perm


class Prog:

    def __init__(self, nc):
        self.nc = nc
        self.q = {e: [] for e in ENGINES}
        self.sems = {}
        self.cnt = {}

    def sem(self, name):
        self.sems[name] = self.nc.alloc_semaphore(name)
        self.cnt[name] = 0
        return name

    def c(self, name):
        return self.cnt[name]

    def snap(self):
        return dict(self.cnt)

    def emit(self, eng, fn, wait=(), inc=None, amt=1):
        tgt = None
        if inc is not None:
            self.cnt[inc] += amt
            tgt = (self.sems[inc], amt)
        wl = tuple((s, v) for (s, v) in wait if v > 0)
        self.q[eng].append((fn, wl, tgt))

    def dma(self, eng, out, in_, wait=(), inc=None, **kw):
        self.emit(eng, lambda e: e.dma_start(out=out, in_=in_, **kw),
                  wait=wait, inc=inc, amt=16)

    def mm(self, out, lhsT, rhs, start, stop, wait=(), inc=None):
        self.emit("tensor",
                  lambda e: e.matmul(out, lhsT, rhs, start=start, stop=stop),
                  wait=wait, inc=inc)

    def trans(self, out, in_, ident, wait=(), inc=None):
        self.emit("tensor", lambda e: e.transpose(out, in_, ident),
                  wait=wait, inc=inc)

    def act(self, out, in_, func, bias=0.0, scale=1.0, accum_out=None,
            wait=(), inc="act"):
        self.emit("scalar",
                  lambda e: e.activation(out, in_, func, bias=bias, scale=scale,
                                         accum_out=accum_out),
                  wait=wait, inc=inc)

    def v(self, meth, *args, wait=(), inc="dve", **kw):
        self.emit("vector", lambda e: getattr(e, meth)(*args, **kw),
                  wait=wait, inc=inc)

    def cchandler(self, kind, ins, outs, rg, wait=()):
        self.emit("gpsimd", lambda e: e.collective_compute(
            kind, ALU.bypass, replica_groups=rg, ins=ins, outs=outs),
            wait=wait, inc="cc")

    def replay(self):
        nc = self.nc
        with nc.Block() as blk:
            for ename in ENGINES:
                ops = self.q[ename]
                if not ops:
                    continue

                def make_body(ops):
                    def body(eng):
                        for fn, waits, inctgt in ops:
                            for s, vl in waits:
                                eng.wait_ge(self.sems[s], vl)
                            inst = fn(eng)
                            if inctgt is not None:
                                inst.then_inc(inctgt[0], inctgt[1])
                    return body

                getattr(blk, ename)(make_body(ops))


def wmax(*waitlists):
    d = {}
    for wl in waitlists:
        for s, v in wl:
            d[s] = max(d.get(s, 0), v)
    return list(d.items())


class MK:
    def __init__(self, NC=8, T=8192, dbg=False):
        self.NC = NC
        self.T = T
        self.Tc = T // NC
        self.TT = min(512, self.Tc)
        self.QT = min(512, T)
        self.Ha = N_HEAD_A // NC
        self.NJ = self.Tc // 128
        self.dbg = dbg
        self.nc = bass.Bass("TRN2", target_bir_lowering=False)
        self.p = Prog(self.nc)
        for s in ("pe", "act", "dve", "gp", "cc",
                  "qc", "q0x0", "q0x1", "q0w", "q0b0", "q0b1", "q0s0", "q0s1",
                  "gw", "q1x0", "q1x1", "q1s0", "q1s1", "q2m", "q2o",
                  "gA", "gw30", "gw31", "q3r0", "q3r1", "q3s0", "q3s1",
                  "qf0", "qf1", "flx", "fw0", "fw1", "fd0", "fd1",
                  "fra0", "fra1", "frb0", "frb1", "fs0", "fs1",
                  "ml0", "ml1", "mz0", "mz1", "mh", "mhi", "msum",
                  "msr0", "msr1", "mza0", "mza1", "mo0", "mo1"):
            self.p.sem(s)
        self._declare_dram()
        self._declare_sbuf()
        self._declare_psum()
        self.greg = self.nc.gpsimd.alloc_register("goff")

    def _dram(self, name, shape, dtype, kind="Internal"):
        return self.nc.dram_tensor(name, list(shape), dtype, kind=kind)

    def _declare_dram(self):
        Tc, T = self.Tc, self.T
        I, O = "ExternalInput", "ExternalOutput"
        dbgk = O if self.dbg else "Internal"
        self.x_sh = self._dram("x_sh", [Tc, D_MODEL], F32, I)
        self.wqkvT = self._dram("wqkvT", [D_MODEL, self.Ha * 3 * DH], F32, I)
        self.woT = self._dram("woT", [D_MODEL, D_MODEL], F32, I)
        self.sincosT = self._dram("sincosT", [256, T], F32, I)
        nmr = self.QT // 128
        self.maskadd = self._dram("maskadd", [128, nmr * self.QT], F32, I)
        self.ident_bf = self._dram("ident_bf", [128, 128], BF16, I)
        self.ident_f32 = self._dram("ident_f32", [128, 128], F32, I)
        self.pt_bf = self._dram("pt_bf", [128, 128], BF16, I)
        self.trineg = self._dram("trineg", [128, 128], F32, I)
        self.i32sel = self._dram("i32sel", [32, 32], F32, I)
        self.onesrow = self._dram("onesrow", [1, 128], F32, I)
        self.onescol_bf = self._dram("onescol_bf", [128, 1], BF16, I)
        self.ones128 = self._dram("ones128", [128, 128], F32, I)
        self.gateT = self._dram("gateT", [D_MODEL, D_FF], F32, I)
        self.upT = self._dram("upT", [D_MODEL, D_FF], F32, I)
        self.downT = self._dram("downT", [D_FF, D_MODEL], F32, I)
        self.inprojT = self._dram("inprojT", [D_MODEL, D_PROJ], F32, I)
        self.outprojT = self._dram("outprojT", [D_INNER, D_MODEL], F32, I)
        self.bgateT = self._dram("bgateT", [D_MODEL, D_FF], F32, I)
        self.bupT = self._dram("bupT", [D_MODEL, D_FF], F32, I)
        self.bdownT = self._dram("bdownT", [D_FF, D_MODEL], F32, I)
        self.convw = self._dram("convw", [CONV_CH, DCONV], F32, I)
        self.convb = self._dram("convb", [CONV_CH, 1], F32, I)
        self.dtbias = self._dram("dtbias", [NH_M, 1], F32, I)
        self.alog = self._dram("alog", [NH_M, 1], F32, I)
        self.dmrow = self._dram("dmrow", [1, NH_M], F32, I)
        self.ssmw = self._dram("ssmw", [NCH, 128], F32, I)
        self.inlnw = self._dram("inlnw", [NCH, 128], F32, I)
        self.postlnw = self._dram("postlnw", [NCH, 128], F32, I)
        self.pmrow = self._dram("pmrow", [1, self.NC * 32], F32, I)
        self.haloz = self._dram("haloz", [1, 1], F32, I)
        self.halooff = self._dram("halooff", [1, 24], mybir.dt.int32, I)
        self.y_sh = self._dram("y_sh", [Tc, D_MODEL], F32, O)
        self.r0d = self._dram("r0d", [D_MODEL, Tc], F32)
        self.r1d = self._dram("r1d", [D_MODEL, Tc], F32, dbgk)
        self.r2d = self._dram("r2d", [D_MODEL, Tc], F32, dbgk)
        self.z1d = self._dram("z1d", [D_MODEL, Tc], F32, dbgk)
        self.zsd = self._dram("zsd", [D_INNER, Tc], BF16)
        self.inner2d = self._dram("inner2d", [D_MODEL, Tc], F32, dbgk)
        self.r4d = self._dram("r4d", [D_MODEL, Tc], F32, dbgk)
        self.ccl_in = self._dram("ccl_in", [D_MODEL, Tc], BF16)
        self.ccl_out = self._dram("ccl_out", [self.NC * D_MODEL, Tc], BF16)
        self.cco_in = self._dram("cco_in", [self.Ha * DH, T], BF16)
        self.cco_out = self._dram("cco_out", [self.NC * self.Ha * DH, T], BF16)
        self.aoff = self._dram("aoff", [1, NCH], mybir.dt.int32, I)
        self.cch_in = self._dram("cch_in", [CONV_CH, 3], BF16)
        self.cch_out = self._dram("cch_out", [self.NC * CONV_CH, 3], BF16)
        self.ccs_in = self._dram("ccs_in", [DS + 1, NH_M * HD], F32)
        self.ccs_out = self._dram("ccs_out", [self.NC * (DS + 1), NH_M * HD], F32)

    def _declare_sbuf(self):
        nc, Tc, T, TT = self.nc, self.Tc, self.T, self.TT
        sb = lambda n, s, d: nc.alloc_sbuf_tensor(n, list(s), d)
        self.c_identbf = sb("c_identbf", [128, 128], BF16)
        self.c_identf = sb("c_identf", [128, 128], F32)
        self.c_ptbf = sb("c_ptbf", [128, 128], BF16)
        self.c_trineg = sb("c_trineg", [128, 128], F32)
        self.c_i32sel = sb("c_i32sel", [32, 32], F32)
        self.c_onesrow = sb("c_onesrow", [1, 128], F32)
        self.c_onescol = sb("c_onescol", [128, 1], BF16)
        self.c_ones128 = sb("c_ones128", [128, 128], F32)
        self.c_ones32 = sb("c_ones32", [32, 128], F32)
        self.c_convw = sb("c_convw", [128, 24 * DCONV], F32)
        self.c_convb = sb("c_convb", [128, 24], F32)
        self.c_dtb = sb("c_dtb", [NH_M, 1], F32)
        self.c_nege = sb("c_nege", [NH_M, 1], F32)
        self.c_dm64 = sb("c_dm64", [128, NH_M], F32)
        self.c_hz128 = sb("c_hz128", [128, 1], F32)
        self.c_ssmw = sb("c_ssmw", [128, NCH], F32)
        self.c_inlnw = sb("c_inlnw", [128, NCH], F32)
        self.c_postlnw = sb("c_postlnw", [128, NCH], F32)
        self.c_prow = sb("c_prow", [1, self.NC * 32], F32)
        self.c_pmrow = sb("c_pmrow", [1, self.NC * 32], F32)
        self.c_haloz = sb("c_haloz", [1, 1], F32)
        self.c_lneps = sb("c_lneps", [128, 1], F32)
        self.c_rmseps = sb("c_rmseps", [128, 1], F32)
        self.c_one = sb("c_one", [128, 1], F32)
        self.c_qsc = sb("c_qsc", [128, 1], F32)
        self.c_halooff = sb("c_halooff", [1, 24], mybir.dt.int32)
        self.c_aoff = sb("c_aoff", [1, NCH], mybir.dt.int32)

        self.arena_size = 200 * 1024
        self.arena = sb("arena", [128, self.arena_size], mybir.dt.uint8)
        self.arena0 = nc.lookup_mloc(self.arena).addr
        assert self.arena0 + self.arena_size <= 229344 - 64, self.arena0

        def A(name, shape, dtype, off):
            per = int(np.prod(shape[1:])) * mybir.dt.size(dtype)
            assert off + per <= self.arena_size, (name, off, per)
            return nc.alloc_sbuf_tensor_at(name, list(shape), dtype,
                                           offset=self.arena0 + off)

        HI = 174 * 1024
        self.p0_x = [A(f"p0_x{i}", [128, D_MODEL], F32, i * 8192) for i in (0, 1)]
        self.p0_ln = A("p0_ln", [128, D_MODEL], BF16, 16384)
        self.p0_sq = A("p0_sq", [128, D_MODEL], BF16, 20480)
        self.p0_fm = A("p0_fm", [128, NCH * Tc], BF16, 24576)
        self.p0_st = A("p0_st", [128, 16], F32, 24576 + NCH * Tc * 2)
        self.b_x = [A(f"b_x{i}", [128, D_MODEL], F32, HI + i * 8192)
                    for i in (0, 1)]
        self.b_st = [A(f"b_st{i}", [128, 512], F32, HI + 16384 + i * 2048)
                     for i in (0, 1)]
        hTb = self.Ha * T * 2
        self.q_reg = A("q_reg", [128, self.Ha * T], BF16, 0)
        self.k_reg = A("k_reg", [128, self.Ha * T], BF16, hTb)
        self.v_reg = A("v_reg", [128, self.Ha * T], BF16, 2 * hTb)
        o1 = 3 * hTb
        ow = self.Ha * 3 * DH
        self.wqkv_sb = A("wqkv_sb", [128, NCH * ow], BF16, o1)
        o2 = o1 + NCH * ow * 2
        self.xg = [A(f"xg{i}", [128, NCH * TT], BF16, o2 + i * NCH * TT * 2)
                   for i in (0, 1)]
        o3 = o2 + 2 * NCH * TT * 2
        self.sc_sin = [A(f"sc_sin{i}", [128, TT], F32, o3 + i * TT * 4)
                       for i in (0, 1)]
        self.sc_cos = [A(f"sc_cos{i}", [128, TT], F32, o3 + (2 + i) * TT * 4)
                       for i in (0, 1)]
        o4 = o3 + 4 * TT * 4
        self.p1_q2 = [A(f"p1_q2{i}", [128, TT], BF16, o4 + i * TT * 2)
                      for i in (0, 1)]
        self.p1_qraw = [A(f"p1_qraw{i}", [128, TT], BF16, o4 + (2 + i) * TT * 2)
                        for i in (0, 1)]
        o5 = o4 + 4 * TT * 2
        self.p1_ssrow = A("p1_ssrow", [1, TT], F32, o5)
        self.p1_t1 = A("p1_t1", [128, TT], F32, o5 + TT * 4)
        self.p1_t2 = A("p1_t2", [128, TT], F32, o5 + 2 * TT * 4)
        self.p1_rstd = A("p1_rstd", [128, TT], F32, o5 + 3 * TT * 4)
        self.p1_tmp = A("p1_tmp", [128, TT], F32, o5 + 4 * TT * 4)
        p1_end = o5 + 5 * TT * 4
        assert p1_end <= HI, p1_end
        self.o_reg = A("o_reg", [128, self.Ha * T], BF16, o1)
        oo = o1 + hTb
        self.p2_exp = [A(f"p2_exp{i}", [128, 512], BF16, oo + i * 1024)
                       for i in range(6)]
        self.p2_acc = [A(f"p2_acc{i}", [128, 512], F32, oo + 6144 + i * 2048)
                       for i in (0, 1)]
        self.p2_tmp = A("p2_tmp", [128, self.QT], F32, oo + 10240)
        self.p2_recip = [A(f"p2_recip{i}", [128, self.QT], F32,
                           oo + 12288 + i * 2048) for i in (0, 1)]
        self.p2_mask = A("p2_mask", [128, (self.QT // 128) * self.QT], F32,
                         oo + 16384)
        p2_end = oo + 16384 + (self.QT // 128) * self.QT * 4
        assert p2_end <= HI, p2_end
        self.p3_a = A("p3_a", [128, NCH * Tc], BF16, 0)
        self.p3_wo = [A(f"p3_wo{i}", [128, NCH * 128], BF16, 32768 + i * 4096)
                      for i in (0, 1)]
        self.p3_r = [A(f"p3_r{i}", [128, TT], F32, 40960 + i * TT * 4)
                     for i in (0, 1)]
        self.p3_st = [A(f"p3_st{i}", [128, TT], F32, 40960 + (2 + i) * TT * 4)
                      for i in (0, 1)]
        self.f_xb = A("f_xb", [128, NCH * Tc], BF16, 0)
        self.f_hid = A("f_hid", [128, NFH * Tc], BF16, 32768)
        _Tc3 = Tc + 3
        _r1 = max(24 * _Tc3 * 2, NCH * Tc * 2 + 2 * NH_M * HD * 4)
        _mend = (32768 + (_r1 + 63) // 64 * 64 + 24 * Tc * 2 + Tc * 4 +
                 2 * self.NJ * 256 + max(2 * self.NJ * 1024, Tc * 8) +
                 NH_M * HD * 4)
        fo = max(32768 + NFH * Tc * 2, _mend)
        self.f_x2 = [A(f"f_x2{i}", [128, TT], BF16, fo + i * TT * 2)
                     for i in (0, 1)]
        lo = fo + 2 * TT * 2
        self.f_rows = A("f_rows", [1, 8 * TT], F32, lo)
        lo2 = lo + 8 * TT * 4
        self.f_mub = [A(f"f_mub{i}", [128, TT], F32, lo2 + i * TT * 4)
                      for i in (0, 1)]
        self.f_rb = [A(f"f_rb{i}", [128, TT], F32, lo2 + (2 + i) * TT * 4)
                     for i in (0, 1)]
        self.f_scr = [A(f"f_scr{i}", [128, TT], F32, lo2 + (4 + i) * TT * 4)
                      for i in (0, 1)]
        ln_end = lo2 + 6 * TT * 4
        assert ln_end <= self.arena_size
        self.f_w = [A(f"f_w{i}", [128, NCH * 128], BF16, fo + i * 4096)
                    for i in (0, 1, 2, 3)]
        self.f_sg = [A(f"f_sg{i}", [128, Tc], BF16, fo + 16384 + i * Tc * 2)
                     for i in (0, 1)]
        gu_end = fo + 16384 + 2 * Tc * 2
        assert gu_end <= self.arena_size, gu_end
        self.f_wd = [A(f"f_wd{i}", [128, NFH * 128], BF16, i * 16384)
                     for i in (0, 1)]
        self.f_res = [A(f"f_res{i}", [128, TT], F32, fo + i * TT * 4)
                      for i in (0, 1)]
        self.f_res2 = [A(f"f_res2{i}", [128, TT], F32, fo + (2 + i) * TT * 4)
                       for i in (0, 1)]
        self.f_st = [A(f"f_st{i}", [128, TT], F32, fo + (4 + i) * TT * 4)
                     for i in (0, 1)]
        self.f_ost = [A(f"f_ost{i}", [128, 512], F32, fo + (6 + i) * TT * 4)
                      for i in (0, 1)]
        Tc3 = Tc + 3
        MB = 32768
        srb = NH_M * HD * 4
        r1 = max(24 * Tc3 * 2, NCH * Tc * 2 + 2 * srb)
        self.m_xbc = A("m_xbc", [128, 24 * Tc3], BF16, MB)
        self.m_y = A("m_y", [128, NCH * Tc], BF16, MB)
        self.m_sr = [A(f"m_sr{i}", [64, NH_M * HD], F32,
                       MB + NCH * Tc * 2 + i * srb) for i in (0, 1)]
        mo = MB + (r1 + 63) // 64 * 64
        self.m_xc = A("m_xc", [128, 24 * Tc], BF16, mo)
        mo2 = mo + 24 * Tc * 2
        self.m_ld = A("m_ld", [64, Tc], F32, mo2)
        mo3 = mo2 + Tc * 4
        self.m_ldt = A("m_ldt", [128, self.NJ * 64], F32, mo3)
        self.m_nldt = A("m_nldt", [128, self.NJ * 64], F32, mo3 + self.NJ * 256)
        mo4 = mo3 + 2 * self.NJ * 256
        self.m_dtraw = A("m_dtraw", [32, Tc], F32, mo4)
        self.m_zs = [A(f"m_zs{i}", [128, Tc], BF16, mo4 + Tc * 4 + i * Tc * 2)
                     for i in (0, 1)]
        self.m_xt = A("m_xt", [128, self.NJ * 512], BF16, mo4)
        self.m_bt = A("m_bt", [128, self.NJ * 512], BF16, mo4 + self.NJ * 1024)
        mo5 = mo4 + max(2 * self.NJ * 1024, Tc * 8)
        self.m_hst = A("m_hst", [64, NH_M * HD], F32, mo5)
        m_end = mo5 + NH_M * HD * 4
        fo_b = max(32768 + NFH * Tc * 2, m_end)
        assert m_end <= fo_b, (m_end, fo_b)
        self.m_hb = A("m_hb", [64, NH_M * HD], BF16, fo_b)
        t0 = fo_b + 8192
        NMB = 4
        self.m_esb = [A(f"m_esb{i}", [128, 128], F32, t0 + i * 512)
                      for i in range(NMB)]
        self.m_arg = [A(f"m_arg{i}", [128, 128], F32, t0 + 2048 + i * 512)
                      for i in range(NMB)]
        self.m_dtile = [A(f"m_dtile{i}", [128, 128], F32, t0 + 4096 + i * 512)
                        for i in range(NMB)]
        self.m_bw = [A(f"m_bw{i}", [128, 64], BF16, t0 + 6144 + i * 128)
                     for i in range(NMB)]
        self.m_cs = [A(f"m_cs{i}", [64, 128], BF16, t0 + 6656 + i * 256)
                     for i in range(NMB)]
        self.m_wt = [A(f"m_wt{i}", [128, 128], BF16, t0 + 7680 + i * 256)
                     for i in range(NMB)]
        t1 = t0 + 8704
        self.m_lacc = A("m_lacc", [32, 1], F32, t1)
        self.m_pcore = A("m_pcore", [32, 1], F32, t1 + 64)
        self.m_pb = A("m_pb", [64, self.NC * 32], F32, t1 + 128)
        self.m_wrow = A("m_wrow", [1, self.NC * 32], F32,
                        t1 + 128 + self.NC * 128)
        self.m_grow = A("m_grow", [1, self.NC * 32], F32,
                        t1 + 128 + 2 * self.NC * 128)
        assert t1 + 128 + 3 * self.NC * 128 <= fo_b + 16384 + 2 * Tc * 2
        self.m_cscr = [A(f"m_cscr{i}", [128, Tc], F32, fo_b + i * 2 * TT * 4)
                       for i in (0, 1)]
        self.m_x2b = [A(f"m_x2b{i}", [128, TT], BF16,
                        fo_b + (6 + i) * TT * 4) for i in (0, 1)]
        self.m_dt2e = A("m_dt2e", [128, self.NJ * 32], F32, 0)
        self.m_cde = A("m_cde", [128, self.NJ * 32], F32, self.NJ * 128)

    def _declare_psum(self):
        nc = self.nc
        self.ps = [nc.alloc_psum_tensor(f"ps{i}", [128, 1024], F32)
                   for i in range(3)]
        self.ps3b = nc.alloc_psum_tensor("ps3b", [128, 512], F32)
        self.pbf = nc.alloc_psum_tensor("pbf", [128, 1024], BF16)

    def const_load(self):
        p = self.p
        pairs = [(self.c_identbf, self.ident_bf), (self.c_identf, self.ident_f32),
                 (self.c_ptbf, self.pt_bf), (self.c_trineg, self.trineg),
                 (self.c_i32sel, self.i32sel), (self.c_onesrow, self.onesrow),
                 (self.c_onescol, self.onescol_bf), (self.c_dtb, self.dtbias),
                 (self.c_ones128, self.ones128),
                 (self.c_pmrow, self.pmrow),
                 (self.c_haloz, self.haloz), (self.c_halooff, self.halooff),
                 (self.c_aoff, self.aoff)]
        for sbuf, dr in pairs:
            h = dr.shape
            p.dma("sync", sbuf[:], dr[0:h[0], 0:h[1]], inc="qc")
        cw = AP(self.convw, 0, [[DCONV, 128], [128 * DCONV, 24], [1, DCONV]])
        p.dma("sync", self.c_convw.ap().rearrange("p (c k) -> p c k", k=DCONV),
              cw, inc="qc")
        cb = AP(self.convb, 0, [[1, 128], [128, 24]])
        p.dma("sync", self.c_convb[:], cb, inc="qc",
              allow_slow_non_contiguous=True)
        for sbuf, dr in ((self.c_ssmw, self.ssmw), (self.c_inlnw, self.inlnw),
                         (self.c_postlnw, self.postlnw)):
            p.dma("sync", sbuf[:], AP(dr, 0, [[1, 128], [128, NCH]]),
                  inc="qc", allow_slow_non_contiguous=True)
        p.dma("sync", self.c_nege[:], self.alog[0:NH_M, 0:1], inc="qc")
        p.dma("sync", self.c_prow[0:1, 0:NH_M], self.dmrow[0:1, 0:NH_M],
              inc="qc")
        w0 = p.c("qc")
        p.act(self.c_nege[:], self.c_nege[:], AF.Exp, wait=[("qc", w0)])
        p.v("tensor_scalar_mul", self.c_nege[:], self.c_nege[:], -1.0,
            wait=[("act", p.c("act"))])
        p.v("memset", self.c_ones32[:], 1.0)
        p.v("memset", self.c_lneps[:], LN_EPS)
        p.v("memset", self.c_rmseps[:], RMS_EPS_A)
        p.v("memset", self.c_one[:], 1.0)
        p.v("memset", self.c_qsc[:], -0.5 * float(np.log(128.0)))
        p.mm(self.ps3b[0:128, 0:NH_M], self.c_onesrow[0:1, 0:128],
             self.c_prow[0:1, 0:NH_M], True, True,
             wait=[("qc", w0)], inc="pe")
        p.v("tensor_copy", self.c_dm64[:], self.ps3b[0:128, 0:NH_M],
            wait=[("pe", p.c("pe"))])
        p.mm(self.ps3b[0:128, 64:65], self.c_onesrow[0:1, 0:128],
             self.c_haloz[0:1, 0:1], True, True,
             wait=[("qc", w0)], inc="pe")
        p.v("tensor_copy", self.c_hz128[:], self.ps3b[0:128, 64:65],
            wait=[("pe", p.c("pe"))])
        self.const_w = [("qc", p.c("qc")), ("dve", p.c("dve"))]

    def phase0(self):
        p, Tc = self.p, self.Tc
        nt = Tc // 128
        hist = []
        for i in range(nt):
            xb = self.p0_x[i % 2]
            qx = f"q0x{i % 2}"
            wt = list(self.const_w)
            if i >= 2:
                h2 = hist[i - 2]
                wt += [("pe", h2["pe"]), ("act", h2["act"]), ("dve", h2["dve"])]
            p.dma("sync", xb[:], self.x_sh[i * 128:(i + 1) * 128, :],
                  wait=wt, inc=qx)
            ld = p.c(qx)
            st = self.p0_st
            p.v("reduce_sum", st[:, 0:1], xb[:], AX.X,
                wait=[(qx, ld), ("dve", p.c("dve"))])
            p.act(self.p0_sq[:], xb[:], AF.Square, accum_out=st[:, 1:2],
                  wait=[(qx, ld), ("dve", p.c("dve")), ("act", p.c("act"))])
            a_sq = p.c("act")
            p.v("tensor_scalar_mul", st[:, 2:3], st[:, 0:1], 1.0 / D_MODEL,
                wait=[("dve", p.c("dve"))])
            p.v("tensor_mul", st[:, 3:4], st[:, 2:3], st[:, 2:3],
                wait=[("dve", p.c("dve"))])
            p.v("scalar_tensor_tensor", st[:, 4:5], st[:, 1:2], 1.0 / D_MODEL,
                st[:, 3:4], ALU.mult, ALU.subtract,
                wait=[("act", a_sq), ("dve", p.c("dve"))])
            p.act(st[:, 5:6], st[:, 4:5], AF.Sqrt, bias=self.c_lneps[:, 0:1],
                  wait=[("dve", p.c("dve"))])
            p.v("reciprocal", st[:, 6:7], st[:, 5:6], wait=[("act", p.c("act"))])
            p.v("scalar_tensor_tensor", st[:, 7:8], st[:, 2:3], -1.0,
                st[:, 6:7], ALU.mult, ALU.mult, wait=[("dve", p.c("dve"))])
            st_done = p.c("dve")
            p.v("tensor_scalar", self.p0_ln[:], xb[:], st[:, 6:7],
                st[:, 7:8], ALU.mult, ALU.add, wait=[("dve", st_done)])
            lnd = p.c("dve")
            for g in range(2):
                for c8 in range(8):
                    c = g * 8 + c8
                    wtr = [("dve", lnd)]
                    if i > 0 or g > 0:
                        wtr.append(("dve", self._p0ev))
                    p.trans(self.pbf[:, c8 * 128:(c8 + 1) * 128],
                            self.p0_ln[:, c * 128:(c + 1) * 128],
                            self.c_identbf[:], wait=wtr if c8 == 0 else [],
                            inc="pe")
                tp = p.c("pe")
                for c8 in range(8):
                    c = g * 8 + c8
                    p.v("tensor_copy",
                        self.p0_fm[:, c * Tc + i * 128:c * Tc + (i + 1) * 128],
                        self.pbf[:, c8 * 128:(c8 + 1) * 128],
                        wait=[("pe", tp)] if c8 == 0 else [])
                self._p0ev = p.c("dve")
            hist.append(p.snap())
        evd = p.c("dve")
        for c in range(NCH):
            p.dma("sync", self.ccl_in[c * 128:(c + 1) * 128, :],
                  self.p0_fm[:, c * Tc:(c + 1) * Tc],
                  wait=[("dve", evd)] if c == 0 else [], inc="q0w")
        wrote = p.c("q0w")
        p.cchandler("AllGather", [self.ccl_in.ap().opt()],
                    [self.ccl_out.ap().opt()], [list(range(self.NC))],
                    wait=[("q0w", wrote)])
        self.cc_ln = p.c("cc")
        bhist = []
        for i in range(nt):
            xb = self.b_x[i % 2]
            qx = f"q0b{i % 2}"
            wt = [("pe", hist[-1]["pe"]), ("act", hist[-1]["act"])]
            if i >= 2:
                wt.append(("pe", bhist[i - 2]["pe"]))
            p.dma("sync", xb[:], self.x_sh[i * 128:(i + 1) * 128, :],
                  wait=wt, inc=qx)
            ld = p.c(qx)
            for g in range(NCH // 4):
                for c4 in range(4):
                    c = g * 4 + c4
                    wtr = [(qx, ld)]
                    if i > 0 or g > 0:
                        wtr.append(("dve", self._bev))
                    p.trans(self.ps[2][:, 512 + c4 * 128:512 + (c4 + 1) * 128],
                            xb[:, c * 128:(c + 1) * 128], self.c_identf[:],
                            wait=wtr if c4 == 0 else [], inc="pe")
                tp = p.c("pe")
                sbuf = self.b_st[g % 2]
                qs = f"q0s{g % 2}"
                wev = [("pe", tp), (qs, self._bst2.get(g % 2, 0))]
                for c4 in range(4):
                    p.v("tensor_copy", sbuf[:, c4 * 128:(c4 + 1) * 128],
                        self.ps[2][:, 512 + c4 * 128:512 + (c4 + 1) * 128],
                        wait=wev if c4 == 0 else [])
                ev = p.c("dve")
                self._bev = ev
                for c4 in range(4):
                    c = g * 4 + c4
                    p.dma("sync", self.r0d[c * 128:(c + 1) * 128,
                                           i * 128:(i + 1) * 128],
                          sbuf[:, c4 * 128:(c4 + 1) * 128],
                          wait=[("dve", ev)] if c4 == 0 else [], inc=qs)
                self._bst2[g % 2] = p.c(qs)
            bhist.append(p.snap())
        self.r0_done = [("q0s0", p.c("q0s0")), ("q0s1", p.c("q0s1"))]

    def phase1(self):
        p, T, Tc, TT, Ha = self.p, self.T, self.Tc, self.TT, self.Ha
        ow = Ha * 3 * DH
        wdst = self.wqkv_sb.ap().rearrange("p (c o) -> p c o", o=ow)
        for grp in range(4):
            p.dma("gpsimd", wdst[:, grp * 4:(grp + 1) * 4, :],
                  AP(self.wqkvT, grp * 4 * 128 * ow,
                     [[ow, 128], [128 * ow, 4], [1, ow]]), inc="gw")
        wq_ld = p.c("gw")
        units = [(s, tt) for s in range(self.NC) for tt in range(Tc // TT)]
        qk_ps = [self.ps[0][:, 0:TT], self.ps[0][:, 512:512 + TT]]
        rot_ps = self.ps[2][:, 0:TT]
        ss_ps = self.ps3b[0:1, 0:TT]
        bc_ps = self.ps3b[:, 0:TT]
        uhist = []
        shist = []
        for u, (s, tt) in enumerate(units):
            xgb = self.xg[u % 2]
            qx = f"q1x{u % 2}"
            qs = f"q1s{u % 2}"
            g0 = s * Tc + tt * TT
            wt = [("cc", self.cc_ln)]
            if u >= 2:
                wt.append(("pe", uhist[u - 2]["pe"]))
            for c in range(NCH):
                p.dma("sync", xgb[:, c * TT:(c + 1) * TT],
                      self.ccl_out[s * D_MODEL + c * 128:
                                   s * D_MODEL + (c + 1) * 128,
                                   tt * TT:(tt + 1) * TT],
                      wait=wt if c == 0 else [], inc=qx)
            wsc = [("dve", uhist[u - 2]["dve"])] if u >= 2 else []
            p.dma("sync", self.sc_sin[u % 2][:], self.sincosT[0:128, g0:g0 + TT],
                  wait=wsc, inc=qs)
            p.dma("sync", self.sc_cos[u % 2][:], self.sincosT[128:256, g0:g0 + TT],
                  inc=qs)
            ld, lds = p.c(qx), p.c(qs)
            scs, scc = self.sc_sin[u % 2], self.sc_cos[u % 2]
            for slot in range(2 * Ha):
                qk, hh = divmod(slot, Ha)
                psb = qk_ps[slot % 2]
                wmm = [(qx, ld), ("gw", wq_ld)]
                gsl = u * 2 * Ha + slot
                if gsl >= 2:
                    h2 = shist[gsl - 2]
                    wmm += [("dve", h2["dve"]), ("act", h2["act"])]
                for c in range(NCH):
                    p.mm(psb, self.wqkv_sb[:, c * ow + slot * DH:
                                           c * ow + (slot + 1) * DH],
                         xgb[:, c * TT:(c + 1) * TT], c == 0, c == NCH - 1,
                         wait=wmm if c == 0 else [], inc="pe")
                mmd = p.c("pe")
                sq, qraw = self.p1_q2[slot % 2], self.p1_qraw[slot % 2]
                p.act(sq[:], psb, AF.Square, wait=[("pe", mmd)])
                p.v("tensor_copy", qraw[:], psb, wait=[("act", p.c("act"))])
                cpd = p.c("dve")
                p.mm(ss_ps, self.c_onescol[:, 0:1], sq[:], True, True,
                     wait=[("act", p.c("act")), ("dve", cpd),
                           ("act", self._p1_prev_rstd)], inc="pe")
                p.mm(rot_ps, self.c_ptbf[:], qraw[:], True, True,
                     wait=[("dve", cpd)], inc="pe")
                mm2 = p.c("pe")
                p.v("tensor_copy", self.p1_ssrow[:], ss_ps, wait=[("pe", mm2)])
                p.mm(bc_ps, self.c_onesrow[:], self.p1_ssrow[:], True, True,
                     wait=[("dve", p.c("dve"))], inc="pe")
                bcd = p.c("pe")
                p.act(self.p1_tmp[:], bc_ps, AF.Ln, bias=self.c_rmseps[:, 0:1],
                      scale=1.0 / DH, wait=[("pe", bcd), ("dve", p.c("dve"))])
                p.act(self.p1_rstd[:], self.p1_tmp[:], AF.Exp, scale=-0.5,
                      bias=(self.c_qsc[:, 0:1] if qk == 0 else 0.0),
                      wait=[("act", p.c("act"))])
                rstd_done = p.c("act")
                self._p1_prev_rstd = rstd_done
                p.v("tensor_mul", self.p1_t1[:], qraw[:], scc[:],
                    wait=[("dve", cpd), (qs, lds)])
                p.v("tensor_mul", self.p1_t2[:], rot_ps, scs[:],
                    wait=[("pe", mm2), ("dve", p.c("dve"))])
                p.v("tensor_add", self.p1_t1[:], self.p1_t1[:], self.p1_t2[:],
                    wait=[("dve", p.c("dve"))])
                dst = self.q_reg if qk == 0 else self.k_reg
                p.v("tensor_mul", dst[:, hh * T + g0:hh * T + g0 + TT],
                    self.p1_t1[:], self.p1_rstd[:],
                    wait=[("dve", p.c("dve")), ("act", rstd_done)])
                shist.append(p.snap())
            for vv in range(TT // 128):
                wmm = [(qx, ld), ("gw", wq_ld)]
                if self._v_hist:
                    wmm.append(("dve", self._v_hist[-1]))
                nsl = (Ha * DH + 511) // 512
                for c in range(NCH):
                    for ns in range(nsl):
                        n0, n1 = ns * 512, min((ns + 1) * 512, Ha * DH)
                        p.mm(self.ps[1][:, n0:n1],
                             xgb[:, c * TT + vv * 128:c * TT + vv * 128 + 128],
                             self.wqkv_sb[:, c * ow + 2 * Ha * DH + n0:
                                          c * ow + 2 * Ha * DH + n1],
                             c == 0, c == NCH - 1,
                             wait=wmm if (c == 0 and ns == 0) else [], inc="pe")
                mmd = p.c("pe")
                gtc = (g0 + vv * 128) // 128
                for hh in range(Ha):
                    p.v("tensor_copy",
                        self.v_reg[:, hh * T + gtc * 128:hh * T + (gtc + 1) * 128],
                        self.ps[1][:, hh * DH:(hh + 1) * DH],
                        wait=[("pe", mmd)] if hh == 0 else [])
                self._v_hist.append(p.c("dve"))
            uhist.append(p.snap())
        self.p1_done = p.snap()

    def phase2(self):
        p, T, QT, Ha, Tc = self.p, self.T, self.QT, self.Ha, self.Tc
        nmr = QT // 128
        D = 4
        NS = 4
        NE = 6
        p.dma("sync", self.p2_mask[:], self.maskadd[0:128, 0:nmr * QT],
              wait=[("pe", self.p1_done["pe"]), ("dve", self.p1_done["dve"])],
              inc="q2m")
        mload = p.c("q2m")
        s_ps = [self.ps[0][:, 0:512], self.ps[0][:, 512:1024],
                self.ps[1][:, 0:512], self.ps[1][:, 512:1024]]
        av_ps = [self.ps[2][:, 0:QT], self.ps[2][:, 512:512 + QT]]
        bc = self.ps3b[:, 0:QT]
        rdy = [("dve", self.p1_done["dve"]), ("q2m", mload)] + self.r0_done
        nq = T // QT
        blocks = []
        for hh in range(Ha):
            for qt in range(nq):
                qi = hh * nq + qt
                nkc = (qt + 1) * QT // 128
                for kc in range(nkc):
                    blocks.append((qi, hh, qt, kc, kc == 0, kc == nkc - 1))
        NB = len(blocks)
        exp_act = {}
        av_pe = {}
        acc_dve = {}
        bcast_pe = {}
        ln_act = {}
        expr_act = {}
        omul_dve = {}

        def emit_back(gb):
            qi, hh, qt, kc, first, last = blocks[gb]
            eb = self.p2_exp[gb % NE]
            wav = [("act", exp_act[gb])]
            if first:
                wav.append(("dve", omul_dve.get(qi - 2, 0)))
            p.mm(av_ps[qi % 2],
                 self.v_reg[:, hh * T + kc * 128:hh * T + (kc + 1) * 128],
                 eb[:], first, last, wait=wav, inc="pe")
            av_pe[gb] = p.c("pe")
            acc = self.p2_acc[qi % 2]
            if first:
                p.v("tensor_copy", acc[:], eb[:],
                    wait=[("act", exp_act[gb]),
                          ("pe", bcast_pe.get(qi - 2, 0)),
                          ("dve", p.c("dve"))])
            else:
                p.v("tensor_add", acc[:], acc[:], eb[:],
                    wait=[("act", exp_act[gb]), ("dve", p.c("dve"))])
            acc_dve[gb] = p.c("dve")
            if last:
                p.mm(bc, self.c_ones128[:], acc[:], True, True,
                     wait=[("dve", acc_dve[gb]),
                           ("act", ln_act.get(qi - 1, 0))], inc="pe")
                bcast_pe[qi] = p.c("pe")
                p.act(self.p2_tmp[:], bc, AF.Ln, wait=[("pe", bcast_pe[qi])])
                ln_act[qi] = p.c("act")
                p.act(self.p2_recip[qi % 2][:], self.p2_tmp[:], AF.Exp,
                      scale=-1.0, wait=[("dve", omul_dve.get(qi - 2, 0))])
                expr_act[qi] = p.c("act")
                p.v("tensor_mul",
                    self.o_reg[:, hh * T + qt * QT:hh * T + (qt + 1) * QT],
                    av_ps[qi % 2], self.p2_recip[qi % 2][:],
                    wait=[("act", expr_act[qi]), ("pe", av_pe[gb]),
                          ("dve", p.c("dve"))])
                omul_dve[qi] = p.c("dve")

        for gb in range(NB):
            qi, hh, qt, kc, first, last = blocks[gb]
            sp = s_ps[gb % NS]
            wmm = list(rdy) if gb < NS else [("act", exp_act[gb - NS])]
            p.mm(sp,
                 self.k_reg[:, hh * T + kc * 128:hh * T + (kc + 1) * 128],
                 self.q_reg[:, hh * T + qt * QT:hh * T + (qt + 1) * QT],
                 True, True, wait=wmm, inc="pe")
            qk_pe = p.c("pe")
            r = kc - qt * nmr
            wexp = [("pe", qk_pe)]
            if r >= 0:
                p.v("tensor_add", sp, sp, self.p2_mask[:, r * QT:(r + 1) * QT],
                    wait=[("pe", qk_pe), ("dve", p.c("dve"))])
                wexp = [("pe", qk_pe), ("dve", p.c("dve"))]
            if gb >= NE:
                wexp += [("pe", av_pe[gb - NE]), ("dve", acc_dve[gb - NE])]
            p.act(self.p2_exp[gb % NE][:], sp, AF.Exp, wait=wexp)
            exp_act[gb] = p.c("act")
            if gb >= D:
                emit_back(gb - D)
        for gb in range(max(NB - D, 0), NB):
            emit_back(gb)
        odone = p.c("dve")
        for hh in range(Ha):
            p.dma("sync", self.cco_in[hh * DH:(hh + 1) * DH, :],
                  self.o_reg[:, hh * T:(hh + 1) * T],
                  wait=[("dve", odone)] if hh == 0 else [], inc="q2o")
        wrote = p.c("q2o")
        p.cchandler("AllGather", [self.cco_in.ap().opt()],
                    [self.cco_out.ap().opt()], [list(range(self.NC))],
                    wait=[("q2o", wrote)])
        self.cc_o = p.c("cc")

    def phase3(self):
        p, Tc, TT = self.p, self.Tc, self.TT
        ntt = Tc // TT
        for c in range(NCH):
            p.emit("gpsimd",
                   lambda e, c=c: e.reg_load(self.greg,
                                             self.c_aoff[0:1, c:c + 1]),
                   wait=[("cc", self.cc_o), ("pe", self.p1_done["pe"]),
                         ("qc", self.const_w[0][1])] if c == 0 else [])
            p.dma("gpsimd", self.p3_a[:, c * Tc:(c + 1) * Tc],
                  AP(self.cco_out, self.greg, [[self.T, 128], [1, Tc]]),
                  inc="gA")
        a_ld = p.c("gA")
        cnt = 0
        hist = {}
        whist = {}
        for oc in range(NCH):
            wb = self.p3_wo[oc % 2]
            gw3 = f"gw3{oc % 2}"
            wwt = [("cc", self.cc_o)]
            if oc >= 2:
                wwt.append(("pe", whist[oc - 2]))
            for grp in range(4):
                p.dma("gpsimd", wb[:, grp * 512:(grp + 1) * 512],
                      AP(self.woT, oc * 128 + grp * 4 * 128 * D_MODEL,
                         [[D_MODEL, 128], [128 * D_MODEL, 4], [1, 128]]),
                      wait=wwt if grp == 0 else [], inc=gw3)
            wld = p.c(gw3)
            for tt in range(ntt):
                psb = self.ps[0][:, 0:TT] if cnt % 2 == 0 else self.ps[1][:, 0:TT]
                rb = self.p3_r[tt % 2]
                q3r = f"q3r{tt % 2}"
                q3s = f"q3s{tt % 2}"
                wr = [("dve", hist.get(cnt - ntt, {"dve": 0})["dve"])] +                     self.r0_done
                p.dma("sync", rb[:], self.r0d[oc * 128:(oc + 1) * 128,
                                              tt * TT:(tt + 1) * TT],
                      wait=wr, inc=q3r)
                rld = p.c(q3r)
                wmm = [("gA", a_ld), (gw3, wld),
                       ("dve", hist.get(cnt - 2, {"dve": 0})["dve"])]
                for c in range(NCH):
                    p.mm(psb, wb[:, c * 128:(c + 1) * 128],
                         self.p3_a[:, c * Tc + tt * TT:c * Tc + (tt + 1) * TT],
                         c == 0, c == NCH - 1, wait=wmm if c == 0 else [],
                         inc="pe")
                mmd = p.c("pe")
                st = self.p3_st[tt % 2]
                wev = [("pe", mmd), (q3r, rld),
                       (q3s, hist.get(cnt - ntt, {q3s: 0}).get(q3s, 0))]
                p.v("tensor_add", st[:], psb, rb[:], wait=wev)
                p.dma("sync", self.r1d[oc * 128:(oc + 1) * 128,
                                       tt * TT:(tt + 1) * TT], st[:],
                      wait=[("dve", p.c("dve"))], inc=q3s)
                hist[cnt] = p.snap()
                cnt += 1
            whist[oc] = p.c("pe")
        self.r1_done = [("q3s0", p.c("q3s0")), ("q3s1", p.c("q3s1"))]


    def mamba(self, ready, upto=None):
        p, Tc, TT, NJ, NC = self.p, self.Tc, self.TT, self.NJ, self.NC
        nth = Tc // TT
        Tc3 = Tc + 3
        s1p = [self.ps[0][0:1, 0:TT], self.ps[0][0:1, 512:512 + TT]]
        s2p = [self.ps[1][0:1, 0:TT], self.ps[1][0:1, 512:512 + TT]]
        s3p = [self.ps[0][0:1, 0:TT], self.ps[0][0:1, 512:512 + TT]]
        R = self.f_rows
        lnact = {}
        x2pe = {}
        for c in range(NCH):
            xt = self.m_cscr[c % 2]
            xb = self.m_zs[c % 2]
            x2a = self.m_y[:, (c % 2) * Tc:(c % 2 + 1) * Tc]
            ql = f"ml{c % 2}"
            wl = list(ready)
            if c >= 2:
                wl += [("act", lnact.get(c % 2, 0)), ("pe", x2pe.get(c % 2, 0)),
                       ("dve", self._m_cpd.get(c % 2, 0))]
            p.dma("sync", xt[:], self.r2d[c * 128:(c + 1) * 128, :],
                  wait=wl, inc=ql)
            ld = p.c(ql)
            p.v("tensor_copy", xb[:], xt[:],
                wait=[(ql, ld), ("dve", p.c("dve"))] + (ready if c == 0 else []))
            self._m_cpd[c % 2] = p.c("dve")
            p.act(x2a, xb[:], AF.Square,
                  wait=[("dve", p.c("dve")), ("act", lnact.get(c % 2, 0))] +
                  (ready if c == 0 else []))
            aq = p.c("act")
            lnact[c % 2] = aq
            for th in range(nth):
                p.mm(s1p[th], self.c_onescol[:, 0:1],
                     xb[:, th * TT:(th + 1) * TT], c == 0, c == NCH - 1,
                     wait=[("dve", self._m_cpd[c % 2]), ("act", aq)]
                     if th == 0 else [], inc="pe")
                p.mm(s2p[th], self.c_onescol[:, 0:1],
                     x2a[:, th * TT:(th + 1) * TT],
                     c == 0, c == NCH - 1, inc="pe")
            x2pe[c % 2] = p.c("pe")
        smm = p.c("pe")
        if upto == "m0a":
            return []
        for th in range(nth):
            mb = self.ps[2][:, 0:TT]
            vb = self.ps[2][:, 512:512 + TT]
            p.v("tensor_copy", R[0:1, 0:TT], s1p[th],
                wait=[("pe", max(smm, self._f_bcd)), ("dve", p.c("dve"))])
            p.v("tensor_copy", R[0:1, TT:2 * TT], s2p[th],
                wait=[("dve", p.c("dve"))])
            p.v("tensor_scalar_mul", R[0:1, 2 * TT:3 * TT], R[0:1, 0:TT],
                1.0 / D_MODEL, wait=[("dve", p.c("dve"))])
            p.v("tensor_mul", R[0:1, 3 * TT:4 * TT], R[0:1, 2 * TT:3 * TT],
                R[0:1, 2 * TT:3 * TT], wait=[("dve", p.c("dve"))])
            p.v("scalar_tensor_tensor", R[0:1, 4 * TT:5 * TT],
                R[0:1, TT:2 * TT], 1.0 / D_MODEL, R[0:1, 3 * TT:4 * TT],
                ALU.mult, ALU.subtract, wait=[("dve", p.c("dve"))])
            rowd = p.c("dve")
            p.mm(mb, self.c_onesrow[:], R[0:1, 2 * TT:3 * TT], True, True,
                 wait=[("dve", rowd), ("act", self._f_bcact)], inc="pe")
            p.mm(vb, self.c_onesrow[:], R[0:1, 4 * TT:5 * TT], True, True,
                 inc="pe")
            bcd = p.c("pe")
            self._f_bcd = bcd
            p.act(self.f_scr[0][:], vb, AF.Ln, bias=self.c_lneps[:, 0:1],
                  wait=[("pe", bcd), ("dve", self._f_aplydve)])
            p.act(self.f_rb[th][:], self.f_scr[0][:], AF.Exp, scale=-0.5,
                  wait=[("act", p.c("act"))])
            p.act(self.f_mub[th][:], mb, AF.Copy, wait=[("act", p.c("act"))])
            self._f_bcact = p.c("act")
        nrm = p.c("act")
        rows_dve = p.c("dve")
        if upto == "m0b":
            return []
        zh = {}
        for c in range(NCH):
            xt = self.m_cscr[c % 2]
            ql = f"ml{c % 2}"
            wl = [("act", nrm), ("dve", p.c("dve")), ("pe", smm)]
            if c >= 2:
                wl.append(("dve", zh[c - 2]["dve"]))
                wl.append(("pe", zh[c - 2]["pe"]))
            p.dma("sync", xt[:], self.r2d[c * 128:(c + 1) * 128, :],
                  wait=wl, inc=ql)
            ld = p.c(ql)
            for th in range(nth):
                st = self.f_st[th % 2]
                qz = f"mz{th % 2}"
                p.v("tensor_sub", self.f_scr[th % 2][:],
                    xt[:, th * TT:(th + 1) * TT], self.f_mub[th][:],
                    wait=[(ql, ld), ("act", nrm), ("dve", p.c("dve"))])
                p.v("tensor_mul", st[:], self.f_scr[th % 2][:],
                    self.f_rb[th][:],
                    wait=[("dve", p.c("dve")), (qz, p.c(qz))])
                zst = p.c("dve")
                p.dma("sync", self.z1d[c * 128:(c + 1) * 128,
                                       th * TT:(th + 1) * TT], st[:],
                      wait=[("dve", zst)], inc=qz)
                p.v("tensor_copy",
                    self.f_xb[:, c * Tc + th * TT:c * Tc + (th + 1) * TT],
                    st[:], wait=[("dve", zst)])
            cpd = p.c("dve")
            x2a = self.m_y[:, (c % 2) * Tc:(c % 2 + 1) * Tc]
            p.act(x2a, self.f_xb[:, c * Tc:(c + 1) * Tc], AF.Square,
                  wait=[("dve", cpd), ("act", lnact.get(c % 2, 0))])
            aq = p.c("act")
            lnact[c % 2] = aq
            for th in range(nth):
                p.mm(s3p[th], self.c_onescol[:, 0:1],
                     x2a[:, th * TT:(th + 1) * TT], c == 0, c == NCH - 1,
                     wait=[("act", aq), ("dve", rows_dve)] if th == 0 else [],
                     inc="pe")
            zh[c] = p.snap()
        smm2 = p.c("pe")
        if upto == "m0c":
            return []
        for th in range(nth):
            vb = self.ps[2][:, 512:512 + TT]
            p.v("tensor_copy", R[0:1, TT:2 * TT], s3p[th],
                wait=[("pe", max(smm2, self._f_bcd)), ("dve", p.c("dve"))])
            p.mm(vb, self.c_onesrow[:], R[0:1, TT:2 * TT], True, True,
                 wait=[("dve", p.c("dve")), ("act", self._f_bcact)], inc="pe")
            self._f_bcd = p.c("pe")
            p.act(self.f_scr[0][:], vb, AF.Ln, bias=self.c_lneps[:, 0:1],
                  scale=1.0 / D_MODEL,
                  wait=[("pe", p.c("pe")), ("dve", p.c("dve"))])
            p.act(self.f_rb[th][:], self.f_scr[0][:], AF.Exp, scale=-0.5,
                  wait=[("act", p.c("act"))])
            self._f_bcact = p.c("act")
        nrm2 = p.c("act")
        if upto == "m0d":
            return []
        for c in range(NCH):
            for th in range(nth):
                p.v("tensor_mul", self.f_scr[th % 2][:],
                    self.f_xb[:, c * Tc + th * TT:c * Tc + (th + 1) * TT],
                    self.f_rb[th][:],
                    wait=[("act", nrm2), ("dve", p.c("dve"))])
                p.v("tensor_scalar_mul",
                    self.f_xb[:, c * Tc + th * TT:c * Tc + (th + 1) * TT],
                    self.f_scr[th % 2][:], self.c_inlnw[:, c:c + 1],
                    wait=[("dve", p.c("dve"))])
        rms3_done = p.c("dve")
        if upto == "m0":
            return []
        if self.dbg:
            d6 = self._dram("dbg_rms3", [NCH * 128, Tc], F32, "ExternalOutput")
            p.dma("gpsimd", d6.ap().rearrange("(c p) t -> p c t", p=128),
                  self.f_xb.ap().rearrange("p (c t) -> p c t", t=Tc),
                  wait=[("dve", rms3_done)], inc="mhi")
            self._rms3dump = p.c("mhi")
        ghist = {}
        for oc in range(41):
            ncol = 128 if oc < 40 else 32
            fwq = f"fw{oc % 2}"
            wb = self.f_w[oc % 2]
            wwt = [("dve", rms3_done), ("mz0", p.c("mz0")),
                   ("mz1", p.c("mz1"))]
            if self.dbg and oc == 0:
                wwt.append(("mhi", self._rms3dump))
            if oc >= 2:
                wwt.append(("pe", ghist[oc - 2]))
            for grp in range(4):
                p.dma("gpsimd", wb[:, grp * 4 * ncol:(grp + 1) * 4 * ncol],
                      AP(self.inprojT, oc * 128 + grp * 4 * 128 * D_PROJ,
                         [[D_PROJ, 128], [128 * D_PROJ, 4], [1, ncol]]),
                      wait=wwt if grp == 0 else [], inc=fwq)
            wld = p.c(fwq)
            psA = self.ps[oc % 2]
            wA = [("dve", rms3_done), (fwq, wld),
                  ("act", self._m_psact.get(oc % 2, 0)),
                  ("dve", self._m_psdve.get(oc % 2, 0))]
            M = ncol
            for th in range(nth):
                for c in range(NCH):
                    p.mm(psA[0:M, th * TT:th * TT + TT],
                         wb[:, c * ncol:(c + 1) * ncol],
                         self.f_xb[:, c * Tc + th * TT:c * Tc + (th + 1) * TT],
                         c == 0, c == NCH - 1,
                         wait=wA if (th == 0 and c == 0) else [], inc="pe")
            mmd = p.c("pe")
            ghist[oc] = mmd
            if oc < 16:
                sg = self.f_sg[oc % 2]
                qza = f"mza{oc % 2}"
                p.act(sg[:, 0:Tc], psA[:, 0:Tc], AF.Silu,
                      wait=[("pe", mmd),
                            (qza, self._m_zast.get(oc % 2, 0))])
                self._m_psact[oc % 2] = p.c("act")
                p.dma("sync", self.zsd[oc * 128:(oc + 1) * 128, :],
                      sg[:, 0:Tc], wait=[("act", p.c("act"))], inc=qza)
                self._m_zast[oc % 2] = p.c(qza)
            elif oc < 40:
                ch = oc - 16
                p.v("tensor_copy",
                    self.m_xbc[:, ch * Tc3 + 3:ch * Tc3 + 3 + Tc],
                    psA[:, 0:Tc], wait=[("pe", mmd)])
                self._m_psdve[oc % 2] = p.c("dve")
            else:
                p.v("tensor_copy", self.m_dtraw[:, 0:Tc], psA[0:32, 0:Tc],
                    wait=[("pe", mmd)])
                self._m_psdve[oc % 2] = p.c("dve")
        inproj_done = [("dve", p.c("dve")), ("act", p.c("act")),
                       ("pe", p.c("pe"))]
        if upto == "m1":
            return []
        for cc in range(24):
            p.dma("sync", self.cch_in[cc * 128:(cc + 1) * 128, 0:3],
                  self.m_xbc[:, cc * Tc3 + Tc:cc * Tc3 + Tc + 3],
                  wait=inproj_done if cc == 0 else [], inc="mh")
        p.cchandler("AllGather", [self.cch_in.ap().opt()],
                    [self.cch_out.ap().opt()], [list(range(NC))],
                    wait=[("mh", p.c("mh"))])
        cch = p.c("cc")
        xbc3 = self.m_xbc.ap().rearrange("p (c t) -> p c t", t=Tc3)
        for grp in range(4):
            p.emit("gpsimd",
                   lambda e, grp=grp: e.reg_load(
                       self.greg, self.c_halooff[0:1, 6 * grp:6 * grp + 1]),
                   wait=[("cc", cch), ("qc", self.const_w[0][1])]
                   if grp == 0 else [])
            p.dma("gpsimd", xbc3[:, 6 * grp:6 * (grp + 1), 0:3],
                  AP(self.cch_out, self.greg,
                     [[3, 128], [128 * 3, 6], [1, 3]]), inc="mhi")
        hld = p.c("mhi")
        p.v("tensor_scalar_mul", xbc3[:, 0:24, 0:3], xbc3[:, 0:24, 0:3],
            self.c_hz128[:, 0:1], wait=[("mhi", hld), ("dve", p.c("dve"))])
        halo_done = p.c("dve")
        convh = {}
        for cc in range(24):
            scr = self.m_cscr[cc % 2]
            w0 = [("dve", halo_done)]
            if cc >= 2:
                w0.append(("act", convh[cc - 2]))
            p.v("tensor_scalar", scr[:],
                self.m_xbc[:, cc * Tc3 + 0:cc * Tc3 + Tc],
                self.c_convw[:, cc * DCONV:cc * DCONV + 1],
                self.c_convb[:, cc:cc + 1], ALU.mult, ALU.add,
                wait=w0 + [("dve", p.c("dve"))])
            for k in range(1, DCONV):
                p.v("scalar_tensor_tensor", scr[:],
                    self.m_xbc[:, cc * Tc3 + k:cc * Tc3 + k + Tc],
                    self.c_convw[:, cc * DCONV + k:cc * DCONV + k + 1],
                    scr[:], ALU.mult, ALU.add, wait=[("dve", p.c("dve"))])
            p.act(self.m_xc[:, cc * Tc:(cc + 1) * Tc], scr[:], AF.Silu,
                  wait=[("dve", p.c("dve"))])
            convh[cc] = p.c("act")
        conv_done = p.c("act")
        if upto == "m3":
            return []
        dtb = self.c_dtb[0:NH_M, 0:1]
        sA = self.m_cscr[0][0:32, 0:Tc]
        sB = self.m_cscr[1][0:32, 0:Tc]
        p.act(sA, self.m_dtraw[:, 0:Tc], AF.Abs, bias=dtb,
              wait=[("act", conv_done), ("dve", p.c("dve"))])
        p.act(sB, sA, AF.Exp, scale=-1.0, wait=[("act", p.c("act"))])
        p.act(sA, sB, AF.Ln, bias=self.c_one[0:32, 0:1],
              wait=[("act", p.c("act"))])
        p.act(sB, self.m_dtraw[:, 0:Tc], AF.Relu, bias=dtb,
              wait=[("act", p.c("act"))])
        p.v("tensor_add", self.m_ld[32:64, 0:Tc], sA, sB,
            wait=[("dve", p.c("dve")), ("act", p.c("act"))])
        dtp_done = p.c("dve")
        if self.dbg:
            d5 = self._dram("dbg_dtraw", [32, Tc], F32, "ExternalOutput")
            p.dma("gpsimd", d5[:, :], self.m_dtraw[:, 0:Tc],
                  wait=[("dve", dtp_done)], inc="mhi")
            self._dtrawdump = p.c("mhi")
        p.v("tensor_scalar_mul", sA, self.m_ld[32:64, 0:Tc],
            self.c_nege[0:NH_M, 0:1], wait=[("dve", dtp_done)])
        p.v("memset", self.m_lacc[:], 0.0,
            wait=[("dve", p.c("dve")), ("mza0", p.c("mza0")),
                  ("mza1", p.c("mza1"))])
        for j in range(NJ):
            p.v("tensor_tensor_scan", self.m_ld[0:32, j * 128:(j + 1) * 128],
                self.c_ones32[:, 0:128],
                self.m_cscr[0][0:32, j * 128:(j + 1) * 128],
                0.0, ALU.mult, ALU.add, wait=[("dve", p.c("dve"))])
            p.v("tensor_add", self.m_lacc[:], self.m_lacc[:],
                self.m_ld[0:32, j * 128 + 127:j * 128 + 128],
                wait=[("dve", p.c("dve"))])
        p.act(self.m_pcore[:], self.m_lacc[:], AF.Exp,
              wait=[("dve", p.c("dve"))])
        pcore_done = p.c("act")
        L_done = p.c("dve")
        tev = 0
        for j in range(NJ):
            wtr = [("dve", L_done), ("act", conv_done)]
            if self.dbg and j == 0:
                wtr.append(("mhi", self._dtrawdump))
            if j > 0:
                wtr.append(("dve", tev))
            p.trans(self.ps[2][:, 0:64], self.m_ld[:, j * 128:(j + 1) * 128],
                    self.c_identf[0:64, 0:64], wait=wtr, inc="pe")
            for g2 in range(4):
                p.trans(self.pbf[:, g2 * 128:(g2 + 1) * 128],
                        self.m_xc[:, g2 * Tc + j * 128:g2 * Tc + (j + 1) * 128],
                        self.c_identbf[:], inc="pe")
                p.trans(self.pbf[:, (4 + g2) * 128:(5 + g2) * 128],
                        self.m_xc[:, (4 + g2) * Tc + j * 128:
                                  (4 + g2) * Tc + (j + 1) * 128],
                        self.c_identbf[:], inc="pe")
            tp = p.c("pe")
            p.v("tensor_copy", self.m_ldt[:, j * 64:(j + 1) * 64],
                self.ps[2][:, 0:64], wait=[("pe", tp)])
            p.v("tensor_scalar_mul", self.m_nldt[:, j * 64:(j + 1) * 64],
                self.m_ldt[:, j * 64:(j + 1) * 64], -1.0,
                wait=[("dve", p.c("dve"))])
            for g2 in range(4):
                p.v("tensor_copy",
                    self.m_xt[:, j * 512 + g2 * 128:j * 512 + (g2 + 1) * 128],
                    self.pbf[:, g2 * 128:(g2 + 1) * 128],
                    wait=[("pe", tp)] if g2 == 0 else [])
                p.v("tensor_copy",
                    self.m_bt[:, j * 512 + g2 * 128:j * 512 + (g2 + 1) * 128],
                    self.pbf[:, (4 + g2) * 128:(5 + g2) * 128])
            tev = p.c("dve")
            p.trans(self.ps[1][0:1, 512:544],
                    self.m_ld[0:32, j * 128 + 127:j * 128 + 128],
                    self.c_identf[0:32, 0:32],
                    wait=[("dve", tev)], inc="pe")
            tld = p.c("pe")
            p.v("tensor_copy", self.f_rows[0:1, 1536 + j * 32:1536 + (j + 1) * 32],
                self.ps[1][0:1, 512:544], wait=[("pe", tld)])
            rowd = p.c("dve")
            p.mm(self.ps[1][:, 0:32], self.c_onesrow[:],
                 self.f_rows[0:1, 1536 + j * 32:1536 + (j + 1) * 32], True, True,
                 wait=[("dve", rowd), ("act", self._m_pre_act)], inc="pe")
            bcd = p.c("pe")
            p.v("tensor_sub", self.m_dt2e[:, j * 32:(j + 1) * 32],
                self.ps[1][:, 0:32], self.m_ldt[:, j * 64:j * 64 + 32],
                wait=[("pe", bcd), ("dve", p.c("dve"))])
            sbd = p.c("dve")
            p.act(self.m_dt2e[:, j * 32:(j + 1) * 32],
                  self.m_dt2e[:, j * 32:(j + 1) * 32], AF.Exp,
                  wait=[("dve", sbd), ("act", p.c("act"))])
            p.act(self.m_cde[:, j * 32:(j + 1) * 32], self.ps[1][:, 0:32],
                  AF.Exp, wait=[("act", p.c("act"))])
            self._m_pre_act = p.c("act")
            tev = p.c("dve")
        p.v("memset", self.m_hst[:], 0.0, wait=[("dve", tev)])
        prep_done = p.c("dve")
        if upto == "m4":
            return []
        LK = 4
        bL = [self.ps[2][:, 0:128], self.ps[2][:, 512:640]]
        Gp = [self.ps[0][:, 0:128], self.ps[0][:, 512:640]]
        Yp = [self.ps[1][0:64, 0:128], self.ps[1][0:64, 512:640]]
        Sps = [self.ps[0][0:64, 0:64], self.ps[0][0:64, 512:576],
               self.ps[1][0:64, 0:64], self.ps[1][0:64, 512:576]]
        units = [(j, h) for j in range(NJ) for h in range(NH_M)]
        NU = len(units)

        def usec(u):
            j, h = units[u]
            g = h // 4
            b0 = j * 512 + (g // 2) * 128 + (g % 2) * 64
            return (j, h, g, self.m_bt[:, b0:b0 + 64],
                    self.m_xt[:, b0:b0 + 64])

        pre_w = [("act", self._m_pre_act), ("dve", prep_done),
                 ("qc", self.const_w[0][1]), ("mz0", p.c("mz0")),
                 ("mz1", p.c("mz1"))]
        bw_dve = {}
        sp_pe = {}

        def m5_bw(u):
            j, h, g, bts, xts = usec(u)
            w = list(pre_w) if u < LK else [("pe", sp_pe[u - LK])]
            p.v("tensor_scalar", self.m_bw[u % LK][:], bts,
                self.m_ldt[:, j * 64 + 32 + h:j * 64 + 32 + h + 1],
                self.m_dt2e[:, j * 32 + h:j * 32 + h + 1],
                ALU.mult, ALU.mult, wait=w + [("dve", p.c("dve"))])
            bw_dve[u] = p.c("dve")

        def m5_sp(u):
            j, h, g, bts, xts = usec(u)
            w = [("dve", bw_dve[u]), ("dve", self._m_sprd.get(u % LK, 0))]
            if u == 0:
                w += pre_w
            p.mm(Sps[u % LK], self.m_bw[u % LK][:], xts, True, True,
                 wait=w, inc="pe")
            sp_pe[u] = p.c("pe")

        def m5_hst(u):
            j, h, g, bts, xts = usec(u)
            p.v("scalar_tensor_tensor", self.m_hst[:, h * HD:(h + 1) * HD],
                self.m_hst[:, h * HD:(h + 1) * HD],
                self.m_cde[0:64, j * 32 + h:j * 32 + h + 1],
                Sps[u % LK], ALU.mult, ALU.add,
                wait=[("pe", sp_pe[u]), ("dve", p.c("dve"))])
            self._m_sprd[u % LK] = p.c("dve")

        for s in range(NU + 2):
            if s < NU:
                m5_bw(s)
            if 1 <= s < NU + 1:
                m5_sp(s - 1)
            if s >= 2:
                m5_hst(s - 2)
        pass1_done = p.c("dve")
        if upto is not None and upto.startswith("m5"):
            return []
        p.dma("sync", self.ccs_in[0:64, :], self.m_hst[:],
              wait=[("dve", pass1_done)], inc="msum")
        p.dma("sync", AP(self.ccs_in, 64 * NH_M * HD, [[1, NH_M], [1, 1]]),
              self.m_pcore[:], wait=[("act", pcore_done)], inc="msum",
              allow_slow_non_contiguous=True)
        p.v("memset", self.f_scr[0][0:1, 0:TT], 0.0,
            wait=[("dve", p.c("dve")), ("act", nrm2)])
        zmd = p.c("dve")
        nz = NH_M * HD - NH_M
        for zi in range((nz + TT - 1) // TT):
            z0, z1 = zi * TT, min((zi + 1) * TT, nz)
            p.dma("sync", AP(self.ccs_in, 64 * NH_M * HD + NH_M + z0,
                             [[1, 1], [1, z1 - z0]]),
                  self.f_scr[0][0:1, 0:z1 - z0],
                  wait=[("dve", zmd)], inc="msum")
        p.cchandler("AllGather", [self.ccs_in.ap().opt()],
                    [self.ccs_out.ap().opt()], [list(range(NC))],
                    wait=[("msum", p.c("msum"))])
        ccs = p.c("cc")
        p.dma("sync", self.m_grow.ap().rearrange("o (r h) -> o r h", h=32),
              AP(self.ccs_out, 64 * NH_M * HD,
                 [[1, 1], [(DS + 1) * NH_M * HD, NC], [1, 32]]),
              wait=[("cc", ccs)], inc="msum")
        gld = p.c("msum")
        p.v("tensor_mul", R[0:1, 0:NC * 32], self.m_grow[0:1, :],
            self.c_pmrow[0:1, :], wait=[("msum", gld), ("dve", p.c("dve"))])
        p.v("tensor_scalar", R[0:1, NC * 32:2 * NC * 32], self.c_pmrow[0:1, :],
            -1.0, 1.0, ALU.mult, ALU.add, wait=[("dve", p.c("dve"))])
        p.v("tensor_add", R[0:1, 0:NC * 32], R[0:1, 0:NC * 32],
            R[0:1, NC * 32:2 * NC * 32], wait=[("dve", p.c("dve"))])
        p.v("memset", self.m_grow[0:1, (NC - 1) * 32:NC * 32], 1.0,
            wait=[("dve", p.c("dve"))])
        for r in range(NC - 2, -1, -1):
            p.v("tensor_mul", self.m_grow[0:1, r * 32:(r + 1) * 32],
                self.m_grow[0:1, (r + 1) * 32:(r + 2) * 32],
                R[0:1, (r + 1) * 32:(r + 2) * 32], wait=[("dve", p.c("dve"))])
        p.v("tensor_mul", self.m_wrow[0:1, :], self.m_grow[0:1, :],
            self.c_pmrow[0:1, :], wait=[("dve", p.c("dve"))])
        wrd = p.c("dve")
        p.mm(self.ps[2][0:64, 0:NC * 32], self.c_onesrow[0:1, 0:64],
             self.m_wrow[0:1, :], True, True, wait=[("dve", wrd)], inc="pe")
        p.v("tensor_copy", self.m_pb[:, 0:NC * 32],
            self.ps[2][0:64, 0:NC * 32], wait=[("pe", p.c("pe"))])
        p.v("memset", self.m_hst[:], 0.0,
            wait=[("dve", p.c("dve")), ("msum", p.c("msum"))])
        pbd = p.c("dve")
        for r in range(NC):
            sr = self.m_sr[r % 2]
            qs = f"msr{r % 2}"
            wl = [("cc", ccs), ("dve", pbd)]
            if r >= 2:
                wl.append(("dve", self._m_srd.get(r % 2, 0)))
            p.dma("sync", sr[:],
                  self.ccs_out[r * (DS + 1):r * (DS + 1) + 64, :],
                  wait=wl, inc=qs)
            srl = p.c(qs)
            for h in range(NH_M):
                p.v("scalar_tensor_tensor", self.m_hst[:, h * HD:(h + 1) * HD],
                    sr[:, h * HD:(h + 1) * HD],
                    self.m_pb[:, r * 32 + h:r * 32 + h + 1],
                    self.m_hst[:, h * HD:(h + 1) * HD], ALU.mult, ALU.add,
                    wait=[(qs, srl), ("dve", p.c("dve"))] if h == 0 else
                    [("dve", p.c("dve"))])
            self._m_srd[r % 2] = p.c("dve")
        hstart_done = p.c("dve")
        if upto == "m6":
            return []
        blm2, gm2, ea2, arg_dve = {}, {}, {}, {}
        dta2, wt_dve, cs_dve, bw2_dve = {}, {}, {}, {}
        ym2, sm2, hb_dve = {}, {}, {}

        def csec_of(u):
            j, h = units[u]
            cc, cb = c_head_slot(h)
            return cb, self.m_xc[cb:cb + 64,
                                 (8 + cc) * Tc + j * 128:(8 + cc) * Tc +
                                 (j + 1) * 128]

        def hb_copy(j):
            w = [("dve", p.c("dve"))]
            if j == 0:
                w.append(("dve", hstart_done))
            else:
                w.append(("pe", ym2[j * NH_M - 1]))
            p.v("tensor_copy", self.m_hb[:], self.m_hst[:], wait=w)
            hb_dve[j] = p.c("dve")

        import os as _os
        _st0lim = int(_os.environ.get("M7ST0", "4"))

        def st0(u):
            j, h, g, bts, xts = usec(u)
            pb = bL[u % 2]
            w = [("dve", hstart_done)]
            if u >= 2:
                w += [("act", ea2[u - 2]), ("dve", arg_dve[u - 2])]
            p.mm(pb, AP(self.c_i32sel, h, [[32, 32], [0, 128]]),
                 self.m_ld[0:32, j * 128:(j + 1) * 128], True, True,
                 wait=w, inc="pe")
            blm2[u] = p.c("pe")
            bsec = self.m_xc[(g % 2) * 64:(g % 2) * 64 + 64,
                             (4 + g // 2) * Tc + j * 128:
                             (4 + g // 2) * Tc + (j + 1) * 128]
            cb, csec = csec_of(u)
            if u >= 1 and _st0lim < 2:
                return
            p.mm(Gp[u % 2], bsec, csec, True, True,
                 wait=[("act", conv_done),
                       ("dve", self._m_wtd.get(u % 2, 0))], inc="pe")
            gm2[u] = p.c("pe")
            if u >= 1 and _st0lim < 3:
                return
            we = [("pe", blm2[u])]
            if u >= LK:
                we.append(("dve", cs_dve[u - LK]))
            p.act(self.m_esb[u % LK][:], pb, AF.Exp, wait=we)
            ea2[u] = p.c("act")
            if u >= 1 and _st0lim < 4:
                return
            wa = [("pe", blm2[u]), ("act", ea2[u]), ("dve", p.c("dve"))]
            if u >= LK:
                wa.append(("act", dta2[u - LK]))
            p.v("scalar_tensor_tensor", self.m_arg[u % LK][:], pb,
                self.m_nldt[:, j * 64 + h:j * 64 + h + 1],
                self.c_trineg[:], ALU.add, ALU.add, wait=wa)
            arg_dve[u] = p.c("dve")

        def st1(u):
            j, h, g, bts, xts = usec(u)
            wd = [("dve", arg_dve[u])]
            if u >= LK:
                wd.append(("dve", wt_dve[u - LK]))
            p.act(self.m_dtile[u % LK][:], self.m_arg[u % LK][:], AF.Exp,
                  wait=wd)
            dta2[u] = p.c("act")
            ww = [("pe", gm2[u]), ("act", dta2[u]), ("dve", p.c("dve"))]
            if u >= LK:
                ww.append(("pe", ym2[u - LK]))
            p.v("scalar_tensor_tensor", self.m_wt[u % LK][:], Gp[u % 2],
                self.m_ldt[:, j * 64 + 32 + h:j * 64 + 32 + h + 1],
                self.m_dtile[u % LK][:], ALU.mult, ALU.mult, wait=ww)
            wt_dve[u] = p.c("dve")
            self._m_wtd[u % 2] = wt_dve[u]
            cb, csec = csec_of(u)
            p.v("tensor_mul", self.m_cs[u % LK][:], csec,
                self.m_esb[u % LK][cb:cb + 64, :],
                wait=[("act", ea2[u]), ("dve", p.c("dve"))])
            cs_dve[u] = p.c("dve")
            if j < NJ - 1:
                wb = [("dve", p.c("dve"))]
                if u >= LK:
                    wb.append(("pe", sm2.get(u - LK, 0)))
                p.v("tensor_scalar", self.m_bw[u % LK][:], bts,
                    self.m_ldt[:, j * 64 + 32 + h:j * 64 + 32 + h + 1],
                    self.m_dt2e[:, j * 32 + h:j * 32 + h + 1],
                    ALU.mult, ALU.mult, wait=wb)
                bw2_dve[u] = p.c("dve")

        def st2(u):
            j, h, g, bts, xts = usec(u)
            Yb = Yp[u % 2]
            p.mm(Yb, xts, self.m_wt[u % LK][:], True, False,
                 wait=[("dve", wt_dve[u]),
                       ("dve", self._m_yev.get(u % 2, 0))], inc="pe")
            p.mm(Yb, self.m_hb[:, h * HD:(h + 1) * HD], self.m_cs[u % LK][:],
                 False, True,
                 wait=[("dve", cs_dve[u]), ("dve", hb_dve[j])], inc="pe")
            ym2[u] = p.c("pe")
            if j < NJ - 1:
                p.mm(self.ps3b[0:64, 0:64], self.m_bw[u % LK][:], xts,
                     True, True,
                     wait=[("dve", bw2_dve[u]), ("dve", self._m_sprd7)],
                     inc="pe")
                sm2[u] = p.c("pe")

        def st3(u):
            j, h, g, bts, xts = usec(u)
            xsec = self.m_xc[(g % 2) * 64:(g % 2) * 64 + 64,
                             (g // 2) * Tc + j * 128:(g // 2) * Tc +
                             (j + 1) * 128]
            p.v("scalar_tensor_tensor",
                self.m_y[(h % 2) * 64:(h % 2) * 64 + 64,
                         (h // 2) * Tc + j * 128:(h // 2) * Tc + (j + 1) * 128],
                xsec, self.c_dm64[(g % 2) * 64:(g % 2) * 64 + 64, h:h + 1],
                Yp[u % 2], ALU.mult, ALU.add,
                wait=[("pe", ym2[u]), ("dve", p.c("dve"))])
            self._m_yev[u % 2] = p.c("dve")
            if j < NJ - 1:
                p.v("scalar_tensor_tensor", self.m_hst[:, h * HD:(h + 1) * HD],
                    self.m_hst[:, h * HD:(h + 1) * HD],
                    self.m_cde[0:64, j * 32 + h:j * 32 + h + 1],
                    self.ps3b[0:64, 0:64], ALU.mult, ALU.add,
                    wait=[("pe", sm2[u]), ("dve", p.c("dve"))])
                self._m_sprd7 = p.c("dve")

        import os
        NU7 = int(os.environ.get("M7N", str(NU)))
        if os.environ.get("M7STYLE", "staged") == "inline":
            skip = os.environ.get("M7SKIP", "")
            hb_copy(0)
            for u in range(NU7):
                if u % NH_M == 0 and u > 0:
                    hb_copy(u // NH_M)
                st0(u)
                if u == 0 or "st1" not in skip:
                    st1(u)
                    if u == 0 or "st2" not in skip:
                        st2(u)
                        if u == 0 or "st3" not in skip:
                            st3(u)
        else:
            hb_copy(0)
            for s in range(NU7 + 3):
                if s < NU7:
                    st0(s)
                if 0 <= s - 1 < NU7:
                    st1(s - 1)
                if 0 <= s - 3 < NU7:
                    st3(s - 3)
                    if (s - 3) % NH_M == NH_M - 1 and (s - 3) // NH_M < NJ - 1:
                        hb_copy((s - 3) // NH_M + 1)
                if 0 <= s - 2 < NU7:
                    st2(s - 2)
        y_done = p.c("dve")
        if upto == "m7":
            return []
        if self.dbg:
            d1 = self._dram("dbg_xc", [24 * 128, Tc], F32, "ExternalOutput")
            p.dma("gpsimd", d1.ap().rearrange("(c p) t -> p c t", p=128),
                  self.m_xc.ap().rearrange("p (c t) -> p c t", t=Tc),
                  wait=[("act", conv_done)], inc="mhi")
            d2 = self._dram("dbg_ld", [64, Tc], F32, "ExternalOutput")
            p.dma("gpsimd", d2[:, :], self.m_ld[:], wait=[("dve", y_done)],
                  inc="mhi")
            d3 = self._dram("dbg_y", [NCH * 128, Tc], F32, "ExternalOutput")
            p.dma("gpsimd", d3.ap().rearrange("(c p) t -> p c t", p=128),
                  self.m_y.ap().rearrange("p (c t) -> p c t", t=Tc),
                  wait=[("dve", y_done)], inc="mhi")
            d4 = self._dram("dbg_hst", [64, NH_M * HD], F32, "ExternalOutput")
            p.dma("gpsimd", d4[:, :], self.m_hst[:],
                  wait=[("dve", hstart_done)], inc="mhi")

        gh = {}
        for c in range(NCH):
            zs = self.m_zs[c % 2]
            ql = f"ml{c % 2}"
            wl = [("dve", y_done), ("mza0", p.c("mza0")),
                  ("mza1", p.c("mza1")), ("mhi", p.c("mhi"))]
            if c >= 2:
                wl.append(("dve", gh[c - 2]["dve"]))
            p.dma("sync", zs[:], self.zsd[c * 128:(c + 1) * 128, :],
                  wait=wl, inc=ql)
            ld = p.c(ql)
            p.v("tensor_mul", self.m_y[:, c * Tc:(c + 1) * Tc],
                self.m_y[:, c * Tc:(c + 1) * Tc], zs[:],
                wait=[(ql, ld), ("dve", p.c("dve"))])
            yzd = p.c("dve")
            x2b = self.m_x2b[c % 2]
            for th in range(nth):
                p.act(x2b[:], self.m_y[:, c * Tc + th * TT:
                                       c * Tc + (th + 1) * TT], AF.Square,
                      wait=[("dve", yzd), ("act", p.c("act")),
                            ("pe", self._m_g2pe.get(c % 2, 0))])
                p.mm(s1p[th], self.c_onescol[:, 0:1], x2b[:],
                     c == 0, c == NCH - 1,
                     wait=[("act", p.c("act"))], inc="pe")
            self._m_g2pe[c % 2] = p.c("pe")
            gh[c] = p.snap()
        smm3 = p.c("pe")
        for th in range(nth):
            vb = self.ps[2][:, 512:512 + TT]
            p.v("tensor_copy", R[0:1, TT:2 * TT], s1p[th],
                wait=[("pe", max(smm3, self._f_bcd)), ("dve", p.c("dve"))])
            p.mm(vb, self.c_onesrow[:], R[0:1, TT:2 * TT], True, True,
                 wait=[("dve", p.c("dve")), ("act", self._f_bcact)], inc="pe")
            self._f_bcd = p.c("pe")
            p.act(self.f_scr[0][:], vb, AF.Ln, bias=self.c_lneps[:, 0:1],
                  scale=1.0 / D_MODEL,
                  wait=[("pe", p.c("pe")), ("dve", p.c("dve"))])
            p.act(self.f_rb[th][:], self.f_scr[0][:], AF.Exp, scale=-0.5,
                  wait=[("act", p.c("act"))])
            self._f_bcact = p.c("act")
        nrm3 = p.c("act")
        for c in range(NCH):
            for th in range(nth):
                p.v("tensor_mul", self.f_scr[th % 2][:],
                    self.m_y[:, c * Tc + th * TT:c * Tc + (th + 1) * TT],
                    self.f_rb[th][:], wait=[("act", nrm3), ("dve", p.c("dve"))])
                p.v("tensor_scalar_mul",
                    self.m_y[:, c * Tc + th * TT:c * Tc + (th + 1) * TT],
                    self.f_scr[th % 2][:], self.c_ssmw[:, c:c + 1],
                    wait=[("dve", p.c("dve"))])
        ynorm_done = p.c("dve")
        cnt = 0
        hist = {}
        wh = {}
        for oc in range(NCH):
            fwq = f"fw{oc % 2}"
            wb = self.f_wd[oc % 2]
            wwt = [("dve", ynorm_done)]
            if oc >= 2:
                wwt.append(("pe", wh[oc - 2]))
            for grp in range(4):
                p.dma("gpsimd", wb[:, grp * 512:(grp + 1) * 512],
                      AP(self.outprojT, oc * 128 + grp * 4 * 128 * D_MODEL,
                         [[D_MODEL, 128], [128 * D_MODEL, 4], [1, 128]]),
                      wait=wwt if grp == 0 else [], inc=fwq)
            wld = p.c(fwq)
            for th in range(nth):
                psb = self.ps[0][:, 0:TT] if cnt % 2 == 0 else self.ps[1][:, 0:TT]
                wmm = [("dve", ynorm_done), (fwq, wld),
                       ("dve", hist.get(cnt - 2, {"dve": 0})["dve"])]
                for c in range(NCH):
                    p.mm(psb, wb[:, c * 128:(c + 1) * 128],
                         self.m_y[:, c * Tc + th * TT:c * Tc + (th + 1) * TT],
                         c == 0, c == NCH - 1, wait=wmm if c == 0 else [],
                         inc="pe")
                mmd = p.c("pe")
                rz = self.f_res[th % 2]
                qra = f"fra{th % 2}"
                p.dma("sync", rz[:], self.z1d[oc * 128:(oc + 1) * 128,
                                              th * TT:(th + 1) * TT],
                      wait=[("dve", hist.get(cnt - nth, {"dve": 0})["dve"]),
                            ("dve", ynorm_done),
                            ("mz0", p.c("mz0")), ("mz1", p.c("mz1"))],
                      inc=qra)
                rld = p.c(qra)
                st = self.f_st[th % 2]
                qs = f"mo{th % 2}"
                p.v("tensor_add", st[:], psb, rz[:],
                    wait=[("pe", mmd), (qra, rld),
                          (qs, hist.get(cnt - nth, {qs: 0}).get(qs, 0))])
                p.dma("sync", self.inner2d[oc * 128:(oc + 1) * 128,
                                           th * TT:(th + 1) * TT], st[:],
                      wait=[("dve", p.c("dve"))], inc=qs)
                hist[cnt] = p.snap()
                cnt += 1
            wh[oc] = p.c("pe")
        return [("mo0", p.c("mo0")), ("mo1", p.c("mo1"))]

    def ffn_block(self, in_dram, wg, wu, wd, res_drams, out_dram, ready,
                  norm="ln", normw=None, out_tm=False):
        p, Tc, TT = self.p, self.Tc, self.TT
        nth = Tc // TT
        for c in range(NCH):
            p.dma("gpsimd", self.f_xb[:, c * Tc:(c + 1) * Tc],
                  in_dram[c * 128:(c + 1) * 128, :],
                  wait=ready if c == 0 else [], inc="flx")
        lx = p.c("flx")
        s1p = [self.ps[0][0:1, 0:TT], self.ps[0][0:1, 512:512 + TT]]
        s2p = [self.ps[1][0:1, 0:TT], self.ps[1][0:1, 512:512 + TT]]
        for c in range(NCH):
            for th in range(nth):
                up = (c * nth + th) % 2
                x2 = self.f_x2[up]
                wsq = [("flx", lx)] + (ready if (c == 0 and th == 0) else [])
                wsq.append(("pe", self._f_x2pe.get(up, 0)))
                p.act(x2[:], self.f_xb[:, c * Tc + th * TT:
                                       c * Tc + (th + 1) * TT], AF.Square,
                      wait=wsq)
                aq = p.c("act")
                if norm == "ln":
                    p.mm(s1p[th], self.c_onescol[:, 0:1],
                         self.f_xb[:, c * Tc + th * TT:c * Tc + (th + 1) * TT],
                         c == 0, c == NCH - 1,
                         wait=[("flx", lx), ("act", aq)], inc="pe")
                p.mm(s2p[th], self.c_onescol[:, 0:1],
                     x2[:], c == 0, c == NCH - 1,
                     wait=[("act", aq)], inc="pe")
                self._f_x2pe[up] = p.c("pe")
        smm = p.c("pe")
        R = self.f_rows
        appw = []
        for th in range(nth):
            mb = self.ps[2][:, 0:TT]
            vb = self.ps[2][:, 512:512 + TT]
            if norm == "ln":
                p.v("tensor_copy", R[0:1, 0:TT], s1p[th],
                    wait=[("pe", max(smm, self._f_bcd)),
                          ("dve", self._f_rowdve)])
                p.v("tensor_copy", R[0:1, TT:2 * TT], s2p[th],
                    wait=[("dve", p.c("dve"))])
                p.v("tensor_scalar_mul", R[0:1, 2 * TT:3 * TT], R[0:1, 0:TT],
                    1.0 / D_MODEL, wait=[("dve", p.c("dve"))])
                p.v("tensor_mul", R[0:1, 3 * TT:4 * TT],
                    R[0:1, 2 * TT:3 * TT], R[0:1, 2 * TT:3 * TT],
                    wait=[("dve", p.c("dve"))])
                p.v("scalar_tensor_tensor", R[0:1, 4 * TT:5 * TT],
                    R[0:1, TT:2 * TT], 1.0 / D_MODEL, R[0:1, 3 * TT:4 * TT],
                    ALU.mult, ALU.subtract, wait=[("dve", p.c("dve"))])
                rowd = p.c("dve")
                self._f_rowdve = rowd
                wbc = [("dve", rowd), ("act", self._f_bcact)]
                p.mm(mb, self.c_onesrow[:], R[0:1, 2 * TT:3 * TT], True, True,
                     wait=wbc, inc="pe")
                p.mm(vb, self.c_onesrow[:], R[0:1, 4 * TT:5 * TT], True, True,
                     inc="pe")
                bcd = p.c("pe")
                self._f_bcd = bcd
                p.act(self.f_scr[0][:], vb, AF.Ln, bias=self.c_lneps[:, 0:1],
                      wait=[("pe", bcd), ("dve", self._f_aplydve)])
                p.act(self.f_rb[th][:], self.f_scr[0][:], AF.Exp, scale=-0.5,
                      wait=[("act", p.c("act"))])
                p.act(self.f_mub[th][:], mb, AF.Copy,
                      wait=[("act", p.c("act"))])
                self._f_bcact = p.c("act")
            else:
                p.v("tensor_copy", R[0:1, TT:2 * TT], s2p[th],
                    wait=[("pe", max(smm, self._f_bcd)),
                          ("dve", self._f_rowdve)])
                rowd = p.c("dve")
                self._f_rowdve = rowd
                wbc = [("dve", rowd), ("act", self._f_bcact)]
                p.mm(vb, self.c_onesrow[:], R[0:1, TT:2 * TT], True, True,
                     wait=wbc, inc="pe")
                bcd = p.c("pe")
                self._f_bcd = bcd
                p.act(self.f_scr[0][:], vb, AF.Ln, bias=self.c_lneps[:, 0:1],
                      scale=1.0 / D_MODEL,
                      wait=[("pe", bcd), ("dve", self._f_aplydve)])
                p.act(self.f_rb[th][:], self.f_scr[0][:], AF.Exp, scale=-0.5,
                      wait=[("act", p.c("act"))])
                self._f_bcact = p.c("act")
            nrm = p.c("act")
            for c in range(NCH):
                scr = self.f_scr[c % 2]
                wap = [("act", nrm), ("dve", p.c("dve"))]
                if norm == "ln":
                    p.v("tensor_sub",
                        scr[:], self.f_xb[:, c * Tc + th * TT:
                                          c * Tc + (th + 1) * TT],
                        self.f_mub[th][:], wait=wap)
                    p.v("tensor_mul",
                        self.f_xb[:, c * Tc + th * TT:c * Tc + (th + 1) * TT],
                        scr[:], self.f_rb[th][:], wait=[("dve", p.c("dve"))])
                else:
                    p.v("tensor_mul", scr[:],
                        self.f_xb[:, c * Tc + th * TT:c * Tc + (th + 1) * TT],
                        self.f_rb[th][:], wait=wap)
                    p.v("tensor_scalar_mul",
                        self.f_xb[:, c * Tc + th * TT:c * Tc + (th + 1) * TT],
                        scr[:], normw[:, c:c + 1], wait=[("dve", p.c("dve"))])
            self._f_aplydve = p.c("dve")
        xb_ready = p.c("dve")
        ghist = {}
        for oc in range(NFH):
            fwq = f"fw{oc % 2}"
            wgb = self.f_w[oc % 2]
            wub = self.f_w[2 + oc % 2]
            wwt = [("dve", xb_ready)]
            if oc >= 2:
                wwt.append(("pe", ghist[oc - 2]))
            for grp in range(4):
                p.dma("gpsimd", wgb[:, grp * 512:(grp + 1) * 512],
                      AP(wg, oc * 128 + grp * 4 * 128 * D_FF,
                         [[D_FF, 128], [128 * D_FF, 4], [1, 128]]),
                      wait=wwt if grp == 0 else [], inc=fwq)
            for grp in range(4):
                p.dma("gpsimd", wub[:, grp * 512:(grp + 1) * 512],
                      AP(wu, oc * 128 + grp * 4 * 128 * D_FF,
                         [[D_FF, 128], [128 * D_FF, 4], [1, 128]]),
                      inc=fwq)
            wld = p.c(fwq)
            psA = self.ps[(2 * oc) % 3]
            psB = self.ps[(2 * oc + 1) % 3]
            wA = [("dve", xb_ready), (fwq, wld),
                  ("act", self._f_psact.get((2 * oc) % 3, 0)),
                  ("dve", self._f_psdve.get((2 * oc) % 3, 0))]
            for th in range(nth):
                for c in range(NCH):
                    p.mm(psA[:, th * TT:th * TT + TT],
                         wgb[:, c * 128:(c + 1) * 128],
                         self.f_xb[:, c * Tc + th * TT:c * Tc + (th + 1) * TT],
                         c == 0, c == NCH - 1,
                         wait=wA if (th == 0 and c == 0) else [], inc="pe")
            gmm = p.c("pe")
            wB = [(fwq, wld), ("act", self._f_psact.get((2 * oc + 1) % 3, 0)),
                  ("dve", self._f_psdve.get((2 * oc + 1) % 3, 0))]
            for th in range(nth):
                for c in range(NCH):
                    p.mm(psB[:, th * TT:th * TT + TT],
                         wub[:, c * 128:(c + 1) * 128],
                         self.f_xb[:, c * Tc + th * TT:c * Tc + (th + 1) * TT],
                         c == 0, c == NCH - 1,
                         wait=wB if (th == 0 and c == 0) else [], inc="pe")
            umm = p.c("pe")
            sg = self.f_sg[oc % 2]
            p.act(sg[:, 0:Tc], psA[:, 0:Tc], AF.Silu,
                  wait=[("pe", gmm), ("dve", self._f_sgdve.get(oc % 2, 0))])
            self._f_psact[(2 * oc) % 3] = p.c("act")
            p.v("tensor_mul", self.f_hid[:, oc * Tc:(oc + 1) * Tc],
                psB[:, 0:Tc], sg[:, 0:Tc],
                wait=[("pe", umm), ("act", p.c("act"))])
            self._f_psdve[(2 * oc + 1) % 3] = p.c("dve")
            self._f_psdve[(2 * oc) % 3] = p.c("dve")
            self._f_sgdve[oc % 2] = p.c("dve")
            ghist[oc] = p.c("pe")
        hid_done = p.c("dve")
        gu_pe = p.c("pe")
        cnt = 0
        hist = {}
        dwh = {}
        for oc2 in range(NCH):
            fdq = f"fd{oc2 % 2}"
            wdb = self.f_wd[oc2 % 2]
            wwt = [("pe", gu_pe)]
            if oc2 >= 2:
                wwt.append(("pe", dwh[oc2 - 2]))
            for grp in range(16):
                p.dma("gpsimd", wdb[:, grp * 512:(grp + 1) * 512],
                      AP(wd, oc2 * 128 + grp * 4 * 128 * D_MODEL,
                         [[D_MODEL, 128], [128 * D_MODEL, 4], [1, 128]]),
                      wait=wwt if grp == 0 else [], inc=fdq)
            wld = p.c(fdq)
            for th in range(nth):
                psb = self.ps[0][:, 0:TT] if cnt % 2 == 0 else self.ps[1][:, 0:TT]
                wmm = [("dve", hid_done), (fdq, wld),
                       ("dve", hist.get(cnt - 2, {"dve": 0})["dve"])]
                for hc in range(NFH):
                    p.mm(psb, wdb[:, hc * 128:(hc + 1) * 128],
                         self.f_hid[:, hc * Tc + th * TT:
                                    hc * Tc + (th + 1) * TT],
                         hc == 0, hc == NFH - 1,
                         wait=wmm if hc == 0 else [], inc="pe")
                mmd = p.c("pe")
                ra = self.f_res[th % 2]
                qra = f"fra{th % 2}"
                wr = [("dve", hist.get(cnt - nth, {"dve": 0})["dve"]),
                      ("pe", gu_pe), ("dve", hid_done)]
                p.dma("sync", ra[:], res_drams[0][oc2 * 128:(oc2 + 1) * 128,
                                                  th * TT:(th + 1) * TT],
                      wait=wr, inc=qra)
                rld = [(qra, p.c(qra))]
                if len(res_drams) > 1:
                    rb2 = self.f_res2[th % 2]
                    qrb = f"frb{th % 2}"
                    p.dma("sync", rb2[:],
                          res_drams[1][oc2 * 128:(oc2 + 1) * 128,
                                       th * TT:(th + 1) * TT],
                          wait=wr, inc=qrb)
                    rld.append((qrb, p.c(qrb)))
                st = self.f_st[th % 2]
                qs = f"fs{th % 2}"
                wev = [("pe", mmd),
                       (qs, hist.get(cnt - nth, {qs: 0}).get(qs, 0))] + rld
                if len(res_drams) > 1:
                    p.v("tensor_add", st[:], psb, self.f_res2[th % 2][:],
                        wait=wev)
                    p.v("tensor_add", st[:], st[:], ra[:],
                        wait=[("dve", p.c("dve"))])
                else:
                    p.v("tensor_add", st[:], psb, ra[:], wait=wev)
                evd = p.c("dve")
                if not out_tm:
                    p.dma("sync", out_dram[oc2 * 128:(oc2 + 1) * 128,
                                           th * TT:(th + 1) * TT], st[:],
                          wait=[("dve", evd)], inc=qs)
                else:
                    for c4 in range(TT // 128):
                        wtr = [("dve", evd)]
                        if cnt > 0 or c4 > 0:
                            wtr.append(("dve", self._f_tmev))
                        p.trans(self.ps[2][:, c4 * 128:(c4 + 1) * 128],
                                st[:, c4 * 128:(c4 + 1) * 128],
                                self.c_identf[:],
                                wait=wtr if c4 == 0 else [], inc="pe")
                    tp = p.c("pe")
                    ob = self.f_ost[th % 2]
                    for c4 in range(TT // 128):
                        p.v("tensor_copy", ob[:, c4 * 128:(c4 + 1) * 128],
                            self.ps[2][:, c4 * 128:(c4 + 1) * 128],
                            wait=[("pe", tp),
                                  (qs, hist.get(cnt - nth, {qs: 0}).get(qs, 0))]
                            if c4 == 0 else [])
                    self._f_tmev = p.c("dve")
                    for c4 in range(TT // 128):
                        p.dma("sync",
                              out_dram[th * TT + c4 * 128:
                                       th * TT + (c4 + 1) * 128,
                                       oc2 * 128:(oc2 + 1) * 128],
                              ob[:, c4 * 128:(c4 + 1) * 128],
                              wait=[("dve", self._f_tmev)] if c4 == 0 else [],
                              inc=qs)
                hist[cnt] = p.snap()
                cnt += 1
            dwh[oc2] = p.c("pe")
        return [("fs0", p.c("fs0")), ("fs1", p.c("fs1"))]

    def build(self, upto=None):
        self._p0ev = 0
        self._bev = 0
        self._bst2 = {}
        self._v_hist = []
        self._p1_prev_rstd = 0
        self._f_rowdve = 0
        self._f_bcact = 0
        self._f_aplydve = 0
        self._f_x2pe = {}
        self._f_psact = {}
        self._f_psdve = {}
        self._f_sgdve = {}
        self._f_tmev = 0
        self._f_bcd = 0
        self._m_cpd = {}
        self._m_psact = {}
        self._m_psdve = {}
        self._m_zast = {}
        self._m_sprd = {}
        self._m_sprd7 = 0
        self._m_pre_act = 0
        self._m_srd = {}
        self._m_wtd = {}
        self._m_yev = {}
        self._m_g2pe = {}
        self.const_load()
        self.phase0()
        if upto == "p0":
            self.dump_end()
            return self._fin()
        self.phase1()
        if upto == "p1":
            self.dump_qkv()
            self.dump_end()
            return self._fin()
        self.phase2()
        if upto == "p2":
            self.dump_o()
            self.dump_end()
            return self._fin()
        self.phase3()
        if upto == "p3":
            self.finish(self.r1d)
            return self._fin()
        ready = list(self.r1_done) + [("dve", self.p.c("dve")),
                                      ("pe", self.p.c("pe"))]
        self.r2_done = self.ffn_block(self.r1d, self.gateT, self.upT,
                                      self.downT, [self.r1d], self.r2d,
                                      ready, norm="ln")
        if upto == "ffn1":
            self.finish(self.r2d)
            return self._fin()
        ready = list(self.r2_done) + [("dve", self.p.c("dve")),
                                      ("pe", self.p.c("pe")),
                                      ("act", self.p.c("act"))]
        self.i2_done = self.mamba(ready, upto=upto)
        if upto is not None and upto.startswith("m") and upto != "mamba":
            self.dump_end()
            return self._fin()
        if upto == "mamba":
            self.finish(self.inner2d)
            return self._fin()
        ready = list(self.i2_done) + [("dve", self.p.c("dve")),
                                      ("pe", self.p.c("pe")),
                                      ("act", self.p.c("act"))]
        self.r4_done = self.ffn_block(self.inner2d, self.bgateT, self.bupT,
                                      self.bdownT, [self.inner2d, self.r2d],
                                      self.r4d, ready, norm="rms",
                                      normw=self.c_postlnw)
        if upto == "ffnb":
            self.finish(self.r4d)
            return self._fin()
        ready = list(self.r4_done) + [("dve", self.p.c("dve")),
                                      ("pe", self.p.c("pe")),
                                      ("act", self.p.c("act"))]
        self.y_done = self.ffn_block(self.r4d, self.gateT, self.upT,
                                     self.downT, [self.r4d], self.y_sh,
                                     ready, norm="ln", out_tm=True)
        self.p.emit("sync", lambda e: e.nop(),
                    wait=[("fs0", self.p.c("fs0")), ("fs1", self.p.c("fs1"))])
        return self._fin()

    def _fin(self):
        self.p.replay()
        self.nc.finalize()
        return self.nc

    def dump_qkv(self):
        p, T = self.p, self.T
        d = self._dram("dbg_qkv", [3 * 128, self.Ha * T], F32,
                       "ExternalOutput")
        w = [("dve", self.p1_done["dve"]), ("pe", self.p1_done["pe"])]
        p.dma("gpsimd", d[0:128, :], self.q_reg[:], wait=w, inc="gA")
        p.dma("gpsimd", d[128:256, :], self.k_reg[:], inc="gA")
        p.dma("gpsimd", d[256:384, :], self.v_reg[:], inc="gA")

    def dump_o(self):
        p, T = self.p, self.T
        d = self._dram("dbg_o", [self.NC * self.Ha * DH, T], F32,
                       "ExternalOutput")
        p.dma("gpsimd", d[:, :], self.cco_out.ap(),
              wait=[("cc", self.cc_o)], inc="gA")

    def dump_end(self):
        p = self.p
        p.emit("sync", lambda e: e.nop(),
               wait=[(s, self.p.c(s)) for s in self.p.cnt if self.p.c(s) > 0])

    def finish(self, src_dram):
        p, Tc = self.p, self.Tc
        nt = Tc // 128
        hist = {}
        fst = {}
        fev = 0
        for i in range(nt):
            xb = self.b_x[i % 2]
            qf = f"qf{i % 2}"
            wt = [(s, self.p.c(s)) for s in
                  ("q3s0", "q3s1", "fs0", "fs1", "mo0", "mo1", "mz0", "mz1")]
            wt += [("dve", self.p.c("dve")), ("pe", self.p.c("pe"))] \
                if i == 0 else []
            if i >= 2:
                wt.append(("pe", hist[i - 2]["pe"]))
            xbr = xb.ap().rearrange("p (c j) -> p c j", j=128)
            for grp in range(4):
                p.dma("sync", xbr[:, grp * 4:(grp + 1) * 4, :],
                      AP(src_dram, i * 128 + grp * 4 * 128 * Tc,
                         [[Tc, 128], [128 * Tc, 4], [1, 128]]),
                      wait=wt if grp == 0 else [], inc=qf)
            ld = p.c(qf)
            for g in range(NCH // 4):
                for c4 in range(4):
                    c = g * 4 + c4
                    wtr = [(qf, ld)]
                    if i > 0 or g > 0:
                        wtr.append(("dve", fev))
                    p.trans(self.ps[1][:, c4 * 128:(c4 + 1) * 128],
                            xb[:, c * 128:(c + 1) * 128], self.c_identf[:],
                            wait=wtr if c4 == 0 else [], inc="pe")
                tp = p.c("pe")
                sbuf = self.b_st[g % 2]
                qs = f"q0s{g % 2}"
                wev = [("pe", tp), (qs, fst.get(g % 2, self._bst2.get(g % 2, 0)))]
                for c4 in range(4):
                    p.v("tensor_copy", sbuf[:, c4 * 128:(c4 + 1) * 128],
                        self.ps[1][:, c4 * 128:(c4 + 1) * 128],
                        wait=wev if c4 == 0 else [])
                ev = p.c("dve")
                fev = ev
                for c4 in range(4):
                    c = g * 4 + c4
                    p.dma("sync", self.y_sh[i * 128:(i + 1) * 128,
                                            c * 128:(c + 1) * 128],
                          sbuf[:, c4 * 128:(c4 + 1) * 128],
                          wait=[("dve", ev)] if c4 == 0 else [], inc=qs)
                fst[g % 2] = p.c(qs)
            hist[i] = p.snap()
        p.emit("sync", lambda e: e.nop(),
               wait=[("q0s0", p.c("q0s0")), ("q0s1", p.c("q0s1"))])


def _rope_tables(T, D):
    inv_freq = 1.0 / 10000.0 ** (np.arange(0, D, 2, dtype=np.float32) / D)
    t = np.arange(T, dtype=np.float32)
    freqs = t[:, None] * inv_freq[None, :]
    sin = np.repeat(np.sin(freqs), 2, axis=-1).astype(np.float32)
    cos = np.repeat(np.cos(freqs), 2, axis=-1).astype(np.float32)
    return sin, cos


def _host_constants(NC, T):
    QT = min(512, T)
    sin, cos = _rope_tables(T, DH)
    sincosT = np.ascontiguousarray(
        np.concatenate([sin.T, cos.T], axis=0).astype(np.float32))
    nmr = QT // 128
    r = np.arange(128)[:, None]
    j = np.arange(QT)[None, :]
    mask = np.concatenate(
        [np.where(128 * rr + r > j, np.float32(MASK_NEG), np.float32(0.0))
         for rr in range(nmr)], axis=1).astype(np.float32)
    ident = np.eye(128)
    ptm = np.zeros((128, 128), dtype=np.float32)
    for i in range(64):
        ptm[2 * i, 2 * i + 1] = -1.0
        ptm[2 * i + 1, 2 * i] = 1.0
    s = np.arange(128)[:, None]
    trineg = np.where(s > j[:, 0:128], np.float32(MASK_NEG),
                      np.float32(0.0)).astype(np.float32)
    return {
        "sincosT": sincosT,
        "maskadd": np.ascontiguousarray(mask),
        "ident_bf": ident.astype(ml_dtypes.bfloat16),
        "ident_f32": ident.astype(np.float32),
        "pt_bf": np.ascontiguousarray(ptm.T).astype(ml_dtypes.bfloat16),
        "trineg": trineg,
        "i32sel": np.eye(32, dtype=np.float32),
        "onesrow": np.ones((1, 128), dtype=np.float32),
        "onescol_bf": np.ones((128, 1), dtype=ml_dtypes.bfloat16),
        "ones128": np.ones((128, 128), dtype=np.float32),
    }


def make_in_maps(inputs, NC, T):
    Tc = T // NC
    Ha = N_HEAD_A // NC
    g = {k: np.ascontiguousarray(np.asarray(v)) for k, v in inputs.items()}
    x = g["L0_attn_a_x"].reshape(T, D_MODEL)
    consts = _host_constants(NC, T)
    wqkv = g["L0_attn_a_W_qkv"]
    perm = c_perm_rows()
    inproj = g["L0_attn_b_in_proj_w"]
    inproj_p = inproj.copy()
    inproj_p[3072:5120] = inproj[3072:5120][perm]
    convw = g["L0_attn_b_conv_w"].copy()
    convb = g["L0_attn_b_conv_b"].copy()
    convw[1024:3072] = convw[1024:3072][perm]
    convb[1024:3072] = convb[1024:3072][perm]
    base = {
        "woT": np.ascontiguousarray(g["L0_attn_a_W_o"].T),
        "gateT": np.ascontiguousarray(g["L0_ffn_w_gate"].T),
        "upT": np.ascontiguousarray(g["L0_ffn_w_up"].T),
        "downT": np.ascontiguousarray(g["L0_ffn_w_down"].T),
        "bgateT": np.ascontiguousarray(g["L0_attn_b_gate_w"].T),
        "bupT": np.ascontiguousarray(g["L0_attn_b_up_w"].T),
        "bdownT": np.ascontiguousarray(g["L0_attn_b_down_w"].T),
        "inprojT": np.ascontiguousarray(inproj_p.T),
        "outprojT": np.ascontiguousarray(g["L0_attn_b_out_proj_w"].T),
        "convw": np.ascontiguousarray(convw),
        "convb": np.ascontiguousarray(convb.reshape(CONV_CH, 1)),
        "dtbias": np.ascontiguousarray(
            g["L0_attn_b_dt_bias"].reshape(NH_M, 1)),
        "alog": np.ascontiguousarray(g["L0_attn_b_A_log"].reshape(NH_M, 1)),
        "dmrow": np.ascontiguousarray(g["L0_attn_b_D"].reshape(1, NH_M)),
        "ssmw": np.ascontiguousarray(
            g["L0_attn_b_ssm_norm_w"].reshape(NCH, 128)),
        "inlnw": np.ascontiguousarray(
            g["L0_attn_b_input_ln_w"].reshape(NCH, 128)),
        "postlnw": np.ascontiguousarray(
            g["L0_attn_b_post_ln_w"].reshape(NCH, 128)),
    }
    base.update(consts)
    in_maps = []
    for c in range(NC):
        m = dict(base)
        m["x_sh"] = np.ascontiguousarray(x[c * Tc:(c + 1) * Tc])
        rows = []
        for s in range(3):
            for hh in range(Ha):
                h = c * Ha + hh
                rows.append(wqkv[s * D_MODEL + h * DH:s * D_MODEL + (h + 1) * DH])
        m["wqkvT"] = np.ascontiguousarray(np.concatenate(rows, 0).T)
        pm = np.zeros(NC * 32, dtype=np.float32)
        pm[:c * 32] = 1.0
        m["pmrow"] = np.ascontiguousarray(pm.reshape(1, NC * 32))
        m["haloz"] = np.array([[0.0 if c == 0 else 1.0]], dtype=np.float32)
        m["aoff"] = np.asarray([[cc * 128 * T + c * Tc for cc in range(NCH)]],
                               dtype=np.int32)
        m["halooff"] = np.asarray(
            [[((c - 1) % NC) * CONV_CH * 3 + cc * 128 * 3
              for cc in range(24)]], dtype=np.int32)
        in_maps.append(m)
    return in_maps


def kernel(**inputs):
    NC, T = 8, 8192
    mk = MK(NC=NC, T=T)
    nc = mk.build()
    in_maps = make_in_maps(inputs, NC, T)
    res = run_bass_kernel_spmd(nc, in_maps, core_ids=list(range(NC)))
    out = np.concatenate([np.asarray(r["y_sh"]) for r in res.results], axis=0)
    return out.reshape(1, T, D_MODEL)
